# version 06
# baseline (speedup 1.0000x reference)
import math
from contextlib import ExitStack

import numpy as np

import concourse.bass as bass
import concourse.mybir as mybir
from concourse.expressions import RuntimeValue

F32 = mybir.dt.float32
BF16 = mybir.dt.bfloat16
I32 = mybir.dt.int32
AF = mybir.ActivationFunctionType
ALU = mybir.AluOpType
ds = bass.ds
AXX = mybir.AxisListType.X

P = 128
D = 2048
KD = 16
NTOK = 1024
STRIP = 1152
NT = STRIP // P
NCH = 8
HK, HV = 16, 32
DK = DV = 128
SENT = 1923491837
LNDK = float(np.log(np.float64(128.0)))
EPS_L2 = 1e-6
EPS_LN = 1e-5
EPS_RMS = 1e-6

VH = [2 * (r % 16) + (r // 16) for r in range(32)]
USE_EEU = False
QKV_SLABS = [(0, 512), (512, 512), (1024, 128)]
OWN_SLABS = [(128, 512), (640, 512)]


class Sem:
    def __init__(self, ctx, nc, name):
        self.h = ctx.enter_context(nc.semaphore(name))
        self.n = 0
        self.name = name


class Prog:
    ENGINES = ("sync", "gpsimd", "tensor", "vector", "scalar")

    def __init__(self, serial_sem=None):
        self.ops = {e: [] for e in self.ENGINES}
        self.last_wait = {}
        self.serial = serial_sem
        self.serial_on = serial_sem is not None
        self.pending_tails = {}
        self.rec = []

    def emit(self, eng, fn, waits=(), inc=None, inc_amt=1, dma=False):
        toks = [w for w in list(waits) + self.pending_tails.pop(eng, []) if w is not None]
        ser = self.serial if self.serial_on else None
        ser_wait = ser.n if ser is not None else None

        if ser is not None and dma:
            assert inc is not None, "serial mode requires an inc sem on DMA ops"
        inc_val = (inc.n + inc_amt) if inc is not None else None
        self.rec.append((eng, [(id(s), v) for (s, v) in toks],
                         id(inc.h) if inc is not None else None, inc_amt,
                         (ser.n if ser is not None else None)))

        def op(e):
            if ser is not None:
                e.wait_ge(ser.h, ser_wait)
            for (s, v) in toks:
                key = (eng, id(s))
                if self.last_wait.get(key, -1) < v:
                    e.wait_ge(s, v)
                    self.last_wait[key] = v
            last = fn(e)
            if inc is not None:
                last = last.then_inc(inc.h, inc_amt)
            if ser is not None:
                if dma:
                    e.wait_ge(inc.h, inc_val)
                e.nop().then_inc(ser.h, 1)

        self.ops[eng].append(op)
        if ser is not None:
            ser.n += 1
        if inc is not None:
            inc.n += inc_amt
            return (inc.h, inc.n)
        return None


class Builder:
    def __init__(self, probes=(), serial=False, serial_phases=None):
        self.nc = bass.Bass("TRN2", target_bir_lowering=False)
        self.ctx = ExitStack()
        self.probes = probes
        self.probe_aps = {}
        self.sems = {}
        use_ser = serial or serial_phases
        self._ser = Sem(self.ctx, self.nc, "s_serial") if use_ser else None
        self.prog = Prog(self._ser)
        self.serial_phases = set(serial_phases or ([
            "hs", "p1", "p2", "p3", "p4", "p4l", "p4l2", "p4l3", "p4p",
            "p4p1", "p4p2", "p4p3",
            "p4s", "p4h", "p6", "p7a", "p7b", "p8"] if serial else []))

    def ph(self, name):
        self.prog.serial_on = (self._ser is not None) and (name in self.serial_phases)

    def phase_barrier(self):
        tails = [(s.h, s.n) for s in self.sems.values() if s.n > 0]
        self.prog.pending_tails = {e: list(tails) for e in Prog.ENGINES}

    def dt(self, name, shape, dtype=F32, kind="ExternalInput", **kw):
        return self.nc.dram_tensor(name, shape, dtype, kind=kind, **kw).ap()

    def sb(self, name, shape, dtype=F32):
        return self.ctx.enter_context(self.nc.sbuf_tensor(name, shape, dtype))

    def psum(self, name, shape, dtype=F32, scope=None):
        st = scope if scope is not None else self.ctx
        return st.enter_context(self.nc.psum_tensor(name, shape, dtype))

    def sem(self, name):
        s = Sem(self.ctx, self.nc, name)
        self.sems[name] = s
        return s

    def probe(self, name, shape, dtype=F32):
        if name in self.probes:
            ap = self.dt("probe_" + name, shape, dtype, kind="ExternalOutput")
            self.probe_aps[name] = ap
            return ap
        return None

    def E(self, eng, fn, waits=(), inc=None, inc_amt=1, dma=False):
        return self.prog.emit(eng, fn, waits, inc, inc_amt, dma)

    def ccol(self, value, rows=P):
        if value not in self._const_cols:
            j = len(self._const_cols)
            assert j < 8
            if "s_cc" not in self.sems:
                self.sem("s_cc")
            tok = self.E("vector", lambda e, j=j, value=value: e.memset(
                self.const_sb[:, j:j + 1], float(value)), inc=self.sems["s_cc"])
            self._const_cols[value] = (j, tok)
        j, tok = self._const_cols[value]
        return self.const_sb[0:rows, j:j + 1], tok

    def dump(self, name, src_hbm):
        ap = self.probe_aps.get(name)
        if ap is None:
            return None
        if "s_probe" not in self.sems:
            self.sems["s_probe"] = Sem(self.ctx, self.nc, "s_probe")
        s_pr = self.sems["s_probe"]
        return self.E("sync", lambda e: e.dma_start(out=ap, in_=src_hbm),
                      inc=s_pr, inc_amt=16, dma=True)

    def declare(self):
        dt = self.dt
        self.x_strip = dt("x_strip", [STRIP, D])
        self.wqkvzT = dt("wqkvzT", [D, 12288])
        self.wbaT = dt("wbaT", [D, 64])
        self.conv_wi = dt("conv_wi", [8192, 4])
        self.dt_col_i = dt("dt_col", [32, 1])
        self.alog_col_i = dt("alog_col", [32, 1])
        self.eye_i = dt("eye_i", [P, P])
        self.mtri_attnT_i = dt("mtri_attnT", [P, P])
        self.mtri_A_i = dt("mtri_A", [P, P])
        self.mtri_AT_i = dt("mtri_AT", [P, P])
        self.nw_col_i = dt("nw_col", [P, 1])
        self.woutT_i = dt("woutT", [4096, D])
        self.par_ii = dt("par_i", [1, 1], I32)
        self.sent_ii = dt("sent_i", [1, 1], I32)
        self.gwgT = dt("gwgT", [D, 64])
        self.inv_map = dt("inv_map", [1, 64], I32)
        self.muW_i = dt("muW", [8, D, 8])
        self.gwT_i = dt("gwT", [8, D, 1408])
        self.uwT_i = dt("uwT", [8, D, 1408])
        self.dwT_i = dt("dwT", [8, 1408, D])
        self.gqaT_i = dt("gqaT", [8, D, 64])
        self.uqaT_i = dt("uqaT", [8, D, 64])
        self.dqaT_i = dt("dqaT", [8, 1408, 64])
        self.gqbF_i = dt("gqbF", [8, 64, 1408])
        self.uqbF_i = dt("uqbF", [8, 64, 1408])
        self.dqbF_i = dt("dqbF", [8, 64, D])
        self.gsc_i = dt("gsc", [8, 1, 1408])
        self.usc_i = dt("usc", [8, 1, 1408])
        self.dsc_i = dt("dsc", [8, 1, D])
        self.shgT_i = dt("shgT", [D, 2816])
        self.shuT_i = dt("shuT", [D, 2816])
        self.shdT_i = dt("shdT", [2816, D])
        self.E8_i = dt("E8_i", [8, 64])
        self.out_hbm = dt("out", [NTOK, D], kind="ExternalOutput")
        self.mixed_hbm = dt("mixed_hbm", [64, P, NTOK], BF16, kind="Internal")
        self.z_hbm = dt("z_hbm", [32, P, NTOK], BF16, kind="Internal")
        self.hg_hbm = dt("hg_hbm", [NCH, 32, P, P], BF16, kind="Internal")
        self.x2_hbm = dt("x2_hbm", [NTOK, D], F32, kind="Internal")
        self.sheets_hbm = dt("sheets_hbm", [8, 32, NTOK], F32, kind="Internal")
        self.st_sh = dt("st_sh", [2, 32, P, P], F32, kind="Internal", addr_space="Shared")
        self.fl_sh = dt("fl_sh", [2, 1, 1], I32, kind="Internal", addr_space="Shared")

    def sbf(self, name, shape, dtype=F32):
        return self.ctx_front.enter_context(self.nc.sbuf_tensor(name, shape, dtype, side="right"))

    def sbm(self, name, shape, dtype=F32):
        return self.ctx_mid.enter_context(self.nc.sbuf_tensor(name, shape, dtype, side="right"))

    def sb4(self, name, shape, dtype=F32):
        return self.ctx_p4sb.enter_context(self.nc.sbuf_tensor(name, shape, dtype, side="right"))

    def alloc_front(self):
        self.ctx_mid = ExitStack()
        self.ctx_front = ExitStack()
        sb, psum = self.sb, self.psum
        self.ones_col = sb("ones_col", [P, 1], BF16)
        self.const_sb = sb("const_sb", [P, 8])
        self._const_cols = {}
        self.sh_gg = self.sbm("sh_gg", [32, NTOK])
        self.sh_gc = self.sbm("sh_gc", [32, NTOK])
        self.sh_sps = self.sbm("sh_sps", [32, NTOK])
        self.sh_lnq = self.sbm("sh_lnq", [32, NTOK])
        self.sh_lnk = self.sbm("sh_lnk", [32, NTOK])
        self.sh_u = self.sbm("sh_u", [32, NTOK])
        self.sh_w = self.sbm("sh_w", [32, NTOK])
        self.sh_p = self.sbm("sh_p", [32, NTOK])
        self.sh_be = self.sbm("sh_be", [32, NTOK])
        self.sh_eg = self.sbm("sh_eg", [32, NTOK])
        self.sh_bet = self.sbm("sh_bet", [32, NTOK])
        self.sh_eu = self.sbm("sh_eu", [32, NTOK])
        self.gam_sb = self.sbm("gam_sb", [32, NCH])
        self.gamrow = self.sbm("gamrow", [1, 32 * NCH])
        self.state_f = self.sbm("state_f", [P, 32, P])
        self.state_b = self.sbm("state_b", [P, 32, P], BF16)
        self.par_sb = self.sbm("par_sb", [1, 1], I32)
        self.ptgt_sb = self.sbm("ptgt_sb", [1, 1], I32)
        self.parf_col = self.sbm("parf_col", [P, 1])
        self.flag_sb = self.sbm("flag_sb", [1, 1], I32)
        self.sent_sb = self.sbm("sent_sb", [1, 1], I32)
        self.zero_i32 = self.sbm("zero_i32", [1, 1], I32)
        self.x_sb = self.sbf("x_sb", [P, 2, D])
        self.stat_sb = sb("stat_sb", [P, 10])
        self.ln_tok = self.sbf("ln_tok", [P, 2, D], BF16)
        self.ln1T = self.sbf("ln1T", [P, KD, STRIP], BF16)
        self.wq_sb = self.sbf("wq_sb", [P, 2, KD, P], BF16)
        self.strip_sb = self.sbf("strip_sb", [P, 2, STRIP], BF16)
        self.conv_acc = self.sbf("conv_acc", [P, 2, NTOK], BF16)
        self.mixed_sb = self.sbf("mixed_sb", [P, 2, NTOK], BF16)
        self.sq_sb = self.sbf("sq_sb", [P, 2, NTOK], BF16)
        self.scr32 = self.sbf("scr32", [32, 2, NTOK])
        self.convw_sb = self.sbf("convw_sb", [P, 64, 4])
        self.wba_sb = self.sbf("wba_sb", [P, KD, 64], BF16)
        self.ba_b = self.sbf("ba_b", [32, NTOK])
        self.ba_a = self.sbf("ba_a", [32, NTOK])
        self.rowbuf = self.sbf("rowbuf", [1, 2, NTOK])
        self.eye_sb = sb("eye_sb", [P, P], BF16)
        self.mtA_sb = sb("mtA_sb", [P, P])
        self.mtAT_sb = sb("mtAT_sb", [P, P])
        self.mtQ_sb = sb("mtQ_sb", [P, P])
        self.dt_col = sb("dt_col_sb", [32, 1])
        self.alog_col = sb("alog_col_sb", [32, 1])
        self.negea_col = sb("negea_col", [32, 1])

    def phase01_ln1(self):
        self.ph("p1")
        E = self.E
        self._kkguard = None
        self._qkguard = None
        self._ppg = None
        self._pbg = None
        self._pgg = None
        self._vng = None
        self._kbeg = None
        self._vbg = None
        self._mskg = None
        self._ktguard = [None, None]
        self.eyef = self.sb("eyef", [P, P])
        self.t_eyef = self.E("sync", lambda e: e.dma_start(out=self.eyef[:, :], in_=self.eye_i),
                             inc=self.sem("s_eyf"), inc_amt=16, dma=True)
        self._rsqg = None
        self._rsqfg = None
        self._x6g = {0: None, 1: None}
        self._t6g = [None, None]
        self._pAg = [None, None]
        self._pBg = [None, None]
        self._pCg = [None, None]
        self._pDtg = [None, None]
        self._pDg = None
        self._pD1g = None
        self._wexg = None
        self._awgg = None
        self._awug = None
        self._awdg = None
        self._silg = None
        self._da0g = None
        self._gwg = {}
        self._uwg = {}
        self._dwg = {}
        self._ldg = {}
        self._gpsg = [None, None]
        self._upsg = [None, None]
        self._dpsg = [None, None]
        self._actg = {}
        self.onesrow_f = self.sb("onesrow_f", [1, P])
        self.onesrow_b = self.sb("onesrow_b", [1, P], BF16)
        self.nw_sb = self.sb("nw_sb", [P, 1])
        self.t_onesrow = self.E("vector", lambda e: (
            e.memset(self.onesrow_f[:, :], 1.0), e.memset(self.onesrow_b[:, :], 1.0))[-1],
            inc=self.sem("s_or"))
        self.t_nw = self.E("sync", lambda e: e.dma_start(out=self.nw_sb[:, :], in_=self.nw_col_i),
                           inc=self.sem("s_nw"), inc_amt=16, dma=True)
        self.ctx_p1 = ExitStack()
        self.ps_t = [self.psum("ps_t0", [P, P], BF16, self.ctx_p1),
                     self.psum("ps_t1", [P, P], BF16, self.ctx_p1)]
        s_x = self.sem("s_x")
        s_c = self.sem("s_c")
        s_st = self.sem("s_st")
        s_v = self.sem("s_v")
        s_tp = self.sem("s_tp")
        s_te = self.sem("s_te")
        tc = []
        tc.append(E("sync", lambda e: e.dma_start(
            out=self.convw_sb[:, :, :], in_=self.conv_wi.rearrange("(c p) t -> p c t", p=P)),
            inc=s_c, inc_amt=16, dma=True))
        tc.append(E("gpsimd", lambda e: e.dma_start(
            out=self.wba_sb[:, :, :],
            in_=self.wbaT.rearrange("(kc kp) c -> kp kc c", kp=P)),
            inc=s_c, inc_amt=16, dma=True))
        tc.append(E("gpsimd", lambda e: e.dma_start(out=self.eye_sb[:, :], in_=self.eye_i),
                    inc=s_c, inc_amt=16, dma=True))
        tc.append(E("sync", lambda e: e.dma_start(out=self.mtA_sb[:, :], in_=self.mtri_A_i),
                    inc=s_c, inc_amt=16, dma=True))
        tc.append(E("sync", lambda e: e.dma_start(out=self.mtAT_sb[:, :], in_=self.mtri_AT_i),
                    inc=s_c, inc_amt=16, dma=True))
        tc.append(E("sync", lambda e: e.dma_start(out=self.mtQ_sb[:, :], in_=self.mtri_attnT_i),
                    inc=s_c, inc_amt=16, dma=True))
        tc.append(E("sync", lambda e: e.dma_start(out=self.dt_col[:, :], in_=self.dt_col_i),
                    inc=s_c, inc_amt=16, dma=True))
        tc.append(E("sync", lambda e: e.dma_start(out=self.alog_col[:, :], in_=self.alog_col_i),
                    inc=s_c, inc_amt=16, dma=True))
        self.t_consts = tc
        self.t_ones = E("vector", lambda e: e.memset(self.ones_col[:, :], 1.0), inc=self.sem("s_ones"))
        t_eye = tc[2]

        t_evs = []
        self._lnguard = {}
        for t in range(NT):
            r = t % 2
            lng0 = self._lnguard.get(r)
            lng = lng0
            t_x = E("sync", lambda e, t=t, r=r: e.dma_start(
                out=self.x_sb[:, r, :], in_=self.x_strip[t * P:(t + 1) * P, :]),
                waits=[self._xguard[r] if hasattr(self, "_xguard") and r in self._xguard else None],
                inc=s_x, inc_amt=16, dma=True)
            c0 = r * 5
            st_ = self.stat_sb
            t_s1 = E("scalar", lambda e, r=r, c0=c0: e.activation(
                self.ln_tok[:, r, :], self.x_sb[:, r, :], AF.Identity,
                accum_out=st_[:, c0:c0 + 1]), waits=[t_x, lng0], inc=s_st)
            t_s2 = E("scalar", lambda e, r=r, c0=c0: e.activation(
                self.ln_tok[:, r, :], self.x_sb[:, r, :], AF.Square,
                accum_out=st_[:, c0 + 1:c0 + 2]), waits=[t_s1], inc=s_st)
            t_mu = E("vector", lambda e, c0=c0: e.tensor_scalar(
                st_[:, c0 + 2:c0 + 3], st_[:, c0:c0 + 1], 1.0 / D, None, ALU.mult),
                waits=[t_s1], inc=s_v)
            t_msq = E("vector", lambda e, c0=c0: e.tensor_tensor(
                st_[:, c0 + 4:c0 + 5], st_[:, c0 + 2:c0 + 3], st_[:, c0 + 2:c0 + 3], ALU.mult),
                waits=[t_mu], inc=s_v)
            t_var = E("vector", lambda e, c0=c0: e.scalar_tensor_tensor(
                st_[:, c0 + 3:c0 + 4], st_[:, c0 + 1:c0 + 2], 1.0 / D,
                st_[:, c0 + 4:c0 + 5], ALU.mult, ALU.subtract),
                waits=[t_s2, t_msq], inc=s_v)
            bb_eps, tok_eps = self.ccol(EPS_LN)
            t_lnv = E("scalar", lambda e, c0=c0, bb=bb_eps: e.activation(
                st_[:, c0 + 4:c0 + 5], st_[:, c0 + 3:c0 + 4], AF.Ln, bias=bb),
                waits=[t_var, tok_eps], inc=s_st)
            t_rstd = E("scalar", lambda e, c0=c0: e.activation(
                st_[:, c0 + 3:c0 + 4], st_[:, c0 + 4:c0 + 5], AF.Exp, scale=-0.5),
                waits=[t_lnv], inc=s_st)
            t_nrm = E("vector", lambda e, r=r, c0=c0: e.tensor_scalar(
                self.ln_tok[:, r, :], self.x_sb[:, r, :],
                st_[:, c0 + 2:c0 + 3], st_[:, c0 + 3:c0 + 4],
                ALU.subtract, ALU.mult), waits=[t_rstd], inc=s_v)
            if not hasattr(self, "_xguard"):
                self._xguard = {}
            self._xguard[r] = t_nrm
            for kc in range(KD):
                i = t * KD + kc
                pb = i % 2
                guard = t_evs[i - 2] if i >= 2 else None
                t_tp = E("tensor", lambda e, kc=kc, r=r, pb=pb: e.transpose(
                    self.ps_t[pb][:, :], self.ln_tok[:, r, kc * P:(kc + 1) * P],
                    self.eye_sb[:, :]),
                    waits=[t_nrm, t_eye, guard], inc=s_tp)
                eng = "vector" if (i % 2 == 0) else "scalar"
                if eng == "vector":
                    t_te = E("vector", lambda e, kc=kc, t=t, pb=pb: e.tensor_copy(
                        self.ln1T[:, kc, t * P:(t + 1) * P], self.ps_t[pb][:, :]),
                        waits=[t_tp], inc=s_te)
                else:
                    t_te = E("scalar", lambda e, kc=kc, t=t, pb=pb: e.copy(
                        self.ln1T[:, kc, t * P:(t + 1) * P], self.ps_t[pb][:, :]),
                        waits=[t_tp], inc=s_te)
                t_evs.append(t_te)
                last_tp = t_tp
            self._lnguard[r] = last_tp
        self.t_ln1T_done = (s_te.h, s_te.n)

    def phase2_qkvz(self):
        self.ph("p2")
        E = self.E
        self.ctx_p1.close()
        self.ctx_p2 = ExitStack()
        self.ps_q = [self.psum("ps_qa", [P, 2, 512], F32, self.ctx_p2),
                     self.psum("ps_qb", [P, 2, 512], F32, self.ctx_p2)]
        self.ps_q3 = [self.psum("ps_q3a", [P, P], F32, self.ctx_p2),
                      self.psum("ps_q3b", [P, P], F32, self.ctx_p2)]
        self.ps_nrm = self.psum("ps_nrm", [1, NTOK], F32, self.ctx_p2)
        s_w = self.sem("s_w")
        s_qm = self.sem("s_qm")
        s_qe = self.sem("s_qe")
        s_cv = self.sem("s_cv")
        s_mx = self.sem("s_mx")
        s_sq = self.sem("s_sq")
        s_nm = self.sem("s_nm")
        s_nr = self.sem("s_nr")
        s_nd = self.sem("s_nd")
        s_mh = self.sem("s_mh")
        s_ba = self.sem("s_ba")
        wq = self.wq_sb
        ln1T_done = self.t_ln1T_done
        t_ones_tok = self.t_ones
        prev_evac = [None, None]
        prev_w_consumer = [None, None]
        strip_guard = [None, None]
        conv_guard = [None, None]
        sq_guard = [None, None]
        mix_guard = [None, None]
        row_guard = [None, None]

        def w_ap(ct):
            return self.wqkvzT.rearrange("(kc kp) c -> kp kc c", kp=P)[:, :, ct * P:(ct + 1) * P]

        for ct in range(97):
            r = ct % 2
            is_ba = (ct == 96)
            is_z = 64 <= ct < 96
            is_qk = ct < 32
            slabs = QKV_SLABS if ct < 64 else OWN_SLABS
            if not is_ba:
                t_w = E("gpsimd", lambda e, ct=ct, r=r: e.dma_start(out=wq[:, r, :, :], in_=w_ap(ct)),
                        waits=[prev_w_consumer[r]], inc=s_w, inc_amt=16, dma=True)
            else:
                t_w = self.t_consts[1]

            def mm_tile(e, ct=ct, r=r, slabs=slabs, is_ba=is_ba):
                last = None
                for kc in range(KD):
                    for si, (s0, sw) in enumerate(slabs):
                        if is_ba:
                            lhs = self.wba_sb[:, kc, :]
                        else:
                            lhs = wq[:, r, kc, :]
                        if sw == 512:
                            out_ap = self.ps_q[r][0:64, si, :] if is_ba else self.ps_q[r][:, si, :]
                        else:
                            out_ap = self.ps_q3[r][0:64, :] if is_ba else self.ps_q3[r][:, :]
                        last = e.matmul(out_ap, lhs, self.ln1T[:, kc, s0:s0 + sw],
                                        start=(kc == 0), stop=(kc == KD - 1))
                return last
            t_mm = E("tensor", mm_tile, waits=[t_w, ln1T_done, prev_evac[r]], inc=s_qm)
            prev_w_consumer[r] = t_mm

            if is_ba:
                t_es = []
                for si in range(2):
                    t_es.append(E("vector", lambda e, r=r, si=si: e.tensor_copy(
                        self.ba_b[:, si * 512:si * 512 + 512], self.ps_q[r][0:32, si, :]),
                        waits=[t_mm], inc=s_ba))
                    t_es.append(E("vector", lambda e, r=r, si=si: e.tensor_copy(
                        self.ba_a[:, si * 512:si * 512 + 512], self.ps_q[r][32:64, si, :]),
                        waits=[t_mm], inc=s_ba))
                prev_evac[r] = t_es[-1]
                continue
            if is_z:
                t_e1 = E("scalar", lambda e, r=r: e.activation(self.mixed_sb[:, r, 0:512], self.ps_q[r][:, 0, :], AF.Silu),
                         waits=[t_mm, (s_mh.h, max(0, s_mh.n - 16)), mix_guard[r]], inc=s_qe)
                t_e2 = E("scalar", lambda e, r=r: e.activation(self.mixed_sb[:, r, 512:1024], self.ps_q[r][:, 1, :], AF.Silu),
                         waits=[t_mm], inc=s_qe)
                prev_evac[r] = t_e2
                E("sync", lambda e, ct=ct, r=r: e.dma_start(out=self.z_hbm[ct - 64, :, :], in_=self.mixed_sb[:, r, :]),
                  waits=[t_e1, t_e2], inc=s_mh, inc_amt=16, dma=True)
                continue
            t_e = []
            for si, (s0, sw) in enumerate(slabs):
                if sw == 512:
                    t_e.append(E("vector", lambda e, r=r, si=si, s0=s0: e.tensor_copy(
                        self.strip_sb[:, r, s0:s0 + 512], self.ps_q[r][:, si, :]),
                        waits=[t_mm, strip_guard[r]], inc=s_qe))
                else:
                    t_e.append(E("vector", lambda e, r=r, s0=s0: e.tensor_copy(
                        self.strip_sb[:, r, s0:s0 + P], self.ps_q3[r][:, :]),
                        waits=[t_mm, strip_guard[r]], inc=s_qe))
            prev_evac[r] = t_e[-1]
            t_c = E("vector", lambda e, r=r, ct=ct: e.tensor_scalar(
                self.conv_acc[:, r, :], self.strip_sb[:, r, 128:1152],
                self.convw_sb[:, ct, 3:4], None, ALU.mult),
                waits=t_e + [self.t_consts[0], conv_guard[r]], inc=s_cv)
            for j in range(3):
                t_c = E("vector", lambda e, r=r, ct=ct, j=j: e.scalar_tensor_tensor(
                    self.conv_acc[:, r, :], self.strip_sb[:, r, 125 + j:125 + j + NTOK],
                    self.convw_sb[:, ct, j:j + 1], self.conv_acc[:, r, :],
                    ALU.mult, ALU.add), waits=[t_c], inc=s_cv)
            t_mxd = E("scalar", lambda e, r=r: e.activation(
                self.mixed_sb[:, r, :], self.conv_acc[:, r, :], AF.Silu),
                waits=[t_c, (s_mh.h, max(0, s_mh.n - 16)), mix_guard[r]], inc=s_mx)
            conv_guard[r] = t_mxd
            strip_guard[r] = t_c
            E("sync", lambda e, ct=ct, r=r: e.dma_start(out=self.mixed_hbm[ct, :, :], in_=self.mixed_sb[:, r, :]),
              waits=[t_mxd], inc=s_mh, inc_amt=16, dma=True)
            if is_qk:
                t_sq = E("vector", lambda e, r=r: e.tensor_mul(
                    self.sq_sb[:, r, :], self.mixed_sb[:, r, :], self.mixed_sb[:, r, :]),
                    waits=[t_mxd, sq_guard[r]], inc=s_sq)
                mix_guard[r] = t_sq
                t_n = E("tensor", lambda e, r=r: (
                    e.matmul(self.ps_nrm[0:1, 0:512], self.ones_col[:, :], self.sq_sb[:, r, 0:512], start=True, stop=True),
                    e.matmul(self.ps_nrm[0:1, 512:1024], self.ones_col[:, :], self.sq_sb[:, r, 512:1024], start=True, stop=True))[-1],
                    waits=[t_sq, (s_nr.h, s_nr.n), t_ones_tok],
                    inc=s_nm, inc_amt=2)
                sq_guard[r] = t_n
                rb = ct % 2
                t_r1 = E("scalar", lambda e, rb=rb: e.activation(self.rowbuf[:, rb, :], self.ps_nrm[0:1, :], AF.Sqrt),
                         waits=[t_n, row_guard[rb]], inc=s_nr)
                bb_l2, tok_l2 = self.ccol(EPS_L2, rows=1)
                t_r2 = E("scalar", lambda e, rb=rb, bb=bb_l2: e.activation(self.rowbuf[:, rb, :], self.rowbuf[:, rb, :], AF.Ln, bias=bb),
                         waits=[t_r1, tok_l2], inc=s_nr)
                tgt = self.sh_lnq if ct < 16 else self.sh_lnk
                kh = ct % 16
                E("sync", lambda e, kh=kh, tgt=tgt, rb=rb: e.dma_start(
                    out=tgt[kh:kh + 1, :], in_=self.rowbuf[:, rb, :]),
                    waits=[t_r2], inc=s_nd, inc_amt=16, dma=True)
                row_guard[rb] = E("sync", lambda e, kh=kh, tgt=tgt, rb=rb: e.dma_start(
                    out=tgt[16 + kh:16 + kh + 1, :], in_=self.rowbuf[:, rb, :]),
                    waits=[t_r2], inc=s_nd, inc_amt=16, dma=True)
        self.t_qkvz_done = [(self.sems["s_mh"].h, self.sems["s_mh"].n),
                            (self.sems["s_nd"].h, self.sems["s_nd"].n),
                            (self.sems["s_ba"].h, self.sems["s_ba"].n)]

    def phase3_sheets(self):
        self.ph("p3")
        E = self.E
        s_p3 = self.sem("s_p3")
        s_p3s = self.sem("s_p3s")
        s_gmr = self.sem("s_gmr")
        ba_done = self.t_qkvz_done[2]
        n_done = self.t_qkvz_done[1]
        GG, GC, SPS = self.sh_gg, self.sh_gc, self.sh_sps
        LNQ, LNK = self.sh_lnq, self.sh_lnk
        U, W, PP = self.sh_u, self.sh_w, self.sh_p
        BE, EG = self.sh_be, self.sh_eg
        b_rows = self.ba_b[:, :]
        a_rows = self.ba_a[:, :]
        bb_one, tok_one = self.ccol(1.0, rows=32)

        t1 = E("scalar", lambda e: e.activation(SPS[:, :], b_rows, AF.Abs), waits=[ba_done], inc=s_p3s)
        t2 = E("scalar", lambda e: e.activation(SPS[:, :], SPS[:, :], AF.Exp, scale=-1.0), waits=[t1], inc=s_p3s)
        t3 = E("scalar", lambda e, bb=bb_one: e.activation(SPS[:, :], SPS[:, :], AF.Ln, bias=bb), waits=[t2, tok_one], inc=s_p3s)
        t4 = E("scalar", lambda e: e.activation(self.scr32[:, 0, :], b_rows, AF.Relu, scale=-1.0),
               waits=[ba_done], inc=s_p3s)
        t5 = E("vector", lambda e: e.tensor_add(SPS[:, :], SPS[:, :], self.scr32[:, 0, :]), waits=[t3, t4], inc=s_p3)
        u1 = E("scalar", lambda e: e.activation(GG[:, :], a_rows, AF.Abs, bias=self.dt_col[:, :]),
               waits=[ba_done, self.t_consts[6]], inc=s_p3s)
        u2 = E("scalar", lambda e: e.activation(GG[:, :], GG[:, :], AF.Exp, scale=-1.0), waits=[u1], inc=s_p3s)
        u3 = E("scalar", lambda e, bb=bb_one: e.activation(GG[:, :], GG[:, :], AF.Ln, bias=bb), waits=[u2, tok_one], inc=s_p3s)
        u4 = E("scalar", lambda e: e.activation(self.scr32[:, 1, :], a_rows, AF.Relu, bias=self.dt_col[:, :]),
               waits=[ba_done, self.t_consts[6], t4], inc=s_p3s)
        u5 = E("vector", lambda e: e.tensor_add(GG[:, :], GG[:, :], self.scr32[:, 1, :]), waits=[u3, u4], inc=s_p3)
        e1 = E("scalar", lambda e: e.activation(self.negea_col[:, :], self.alog_col[:, :], AF.Exp),
               waits=[self.t_consts[7]], inc=s_p3s)
        e2 = E("vector", lambda e: e.tensor_scalar(self.negea_col[:, :], self.negea_col[:, :], -1.0, None, ALU.mult),
               waits=[e1], inc=s_p3)
        g1 = E("vector", lambda e: e.tensor_scalar(GG[:, :], GG[:, :], self.negea_col[:, :], None, ALU.mult),
               waits=[u5, e2], inc=s_p3)
        gs = []
        for ci in range(NCH):
            gs.append(E("vector", lambda e, ci=ci: e.tensor_tensor_scan(
                GC[:, ci * P:(ci + 1) * P], GG[:, ci * P:(ci + 1) * P],
                GG[:, ci * P:(ci + 1) * P], 0.0, ALU.add, ALU.bypass),
                waits=[g1], inc=s_p3))
        w1 = E("vector", lambda e: e.tensor_scalar(W[:, :], GC[:, :], -1.0, None, ALU.mult),
               waits=gs, inc=s_p3)
        w2 = E("vector", lambda e: e.tensor_sub(W[:, :], W[:, :], LNK[:, :]),
               waits=[w1, n_done], inc=s_p3)
        p1 = E("vector", lambda e: e.tensor_sub(PP[:, :], GC[:, :], SPS[:, :]),
               waits=gs + [t5], inc=s_p3)
        p2 = E("vector", lambda e: e.tensor_sub(PP[:, :], PP[:, :], LNK[:, :]),
               waits=[p1, n_done], inc=s_p3)
        b1 = E("scalar", lambda e: e.activation(BE[:, :], PP[:, :], AF.Exp), waits=[p2], inc=s_p3s)
        q1 = E("vector", lambda e: e.tensor_sub(U[:, :], GC[:, :], LNQ[:, :]),
               waits=gs + [n_done], inc=s_p3)
        q2 = E("vector", lambda e: e.tensor_scalar(U[:, :], U[:, :], -0.5 * LNDK, None, ALU.add),
               waits=[q1], inc=s_p3)
        egs = []
        for ci in range(NCH):
            egs.append(E("vector", lambda e, ci=ci: e.tensor_scalar(
                EG[:, ci * P:(ci + 1) * P], GC[:, ci * P:(ci + 1) * P],
                GC[:, ci * P + P - 1:ci * P + P], None, ALU.subtract),
                waits=gs, inc=s_p3))
        eg2 = E("vector", lambda e: e.tensor_add(EG[:, :], EG[:, :], LNK[:, :]),
                waits=egs + [n_done], inc=s_p3)
        eg3 = E("scalar", lambda e: e.activation(EG[:, :], EG[:, :], AF.Exp, scale=-1.0),
                waits=[eg2], inc=s_p3s)
        gm1 = []
        for ci in range(NCH):
            gm1.append(E("vector", lambda e, ci=ci: e.tensor_copy(
                self.gam_sb[:, ci:ci + 1], GC[:, ci * P + P - 1:ci * P + P]),
                waits=gs, inc=s_p3))
        gm2 = E("scalar", lambda e: e.activation(self.gam_sb[:, :], self.gam_sb[:, :], AF.Exp),
                waits=gm1, inc=s_p3s)
        gm3 = E("sync", lambda e: e.dma_start(out=self.gamrow[:, :], in_=self.gam_sb[:, :]),
                waits=[gm2], inc=s_gmr, inc_amt=16, dma=True)
        bt1 = E("scalar", lambda e: e.activation(self.sh_bet[:, :], SPS[:, :], AF.Exp, scale=-1.0),
                waits=[t5], inc=s_p3s)
        eu1 = E("scalar", lambda e: e.activation(self.sh_eu[:, :], U[:, :], AF.Exp),
                waits=[q2], inc=s_p3s)
        self.t_sheets = [w2, p2, b1, q2, eg3, gm3, t5, u5, bt1, eu1]
        self.t_gamrow = gm3
        if "sheets" in self.probes:
            s_shd = self.sem("s_shd")
            rows = [GG, GC, SPS, LNQ, LNK, U, W, PP]
            for i, tn in enumerate(rows):
                E("sync", lambda e, tn=tn, i=i: e.dma_start(
                    out=self.sheets_hbm[i, :, :], in_=tn[:, :]),
                    waits=self.t_sheets, inc=s_shd, inc_amt=16, dma=True)
            self.dump("sheets", self.sheets_hbm)

    def phase35_handshake_init(self):
        self.ph("hs")
        """Emitted FIRST: zero states, publish zeroed state slot, zero flag."""
        E = self.E
        s_hs = self.sem("s_hs")
        s_hv = self.sem("s_hv")
        self.parf_i = self.dt("parf_col_i", [P, 1])
        self.ptgt_i = self.dt("ptgt_i", [1, 1], I32)

        tz = E("vector", lambda e: e.memset(self.zero_i32[:, :], 0), inc=s_hv)
        t0 = E("vector", lambda e: e.memset(self.state_f[:, :, :], 0.0), inc=s_hv)
        t1 = E("vector", lambda e: e.memset(self.state_b[:, :, :], 0.0), inc=s_hv)
        tp = E("sync", lambda e: e.dma_start(out=self.par_sb[:, :], in_=self.par_ii),
               inc=s_hs, inc_amt=16, dma=True)
        tt = E("sync", lambda e: e.dma_start(out=self.ptgt_sb[:, :], in_=self.ptgt_i),
               inc=s_hs, inc_amt=16, dma=True)
        tf = E("sync", lambda e: e.dma_start(out=self.parf_col[:, :], in_=self.parf_i),
               inc=s_hs, inc_amt=16, dma=True)
        ts = E("sync", lambda e: e.dma_start(out=self.sent_sb[:, :], in_=self.sent_ii),
               inc=s_hs, inc_amt=16, dma=True)
        def pub(e):
            me_reg = e.alloc_register("me_hs")
            e.reg_load(me_reg, self.par_sb[0:1, 0:1])
            me = RuntimeValue(me_reg, min_val=0, max_val=1)
            self._me_rv_gp = me
            return e.dma_start(out=self.st_sh[ds(me, 1), :, :, :].rearrange("o h a b -> (o a) h b"),
                               in_=self.state_f[:, :, :])
        t2 = E("gpsimd", pub, waits=[t0, tp], inc=s_hs, inc_amt=16, dma=True)
        t3 = E("gpsimd", lambda e: e.dma_start(
            out=self.fl_sh[ds(self._me_rv_gp, 1), :, :], in_=self.zero_i32[0:1, 0:1]),
            waits=[t2, tz], inc=s_hs, inc_amt=16, dma=True)
        self.t_hs_done = t3
        self.s_hs = s_hs

    def phase4_core(self):
        self.ph("p4")
        """Per-chunk: reload mixed/z, transposes, masks, T-solve, scan."""
        E = self.E
        nc = self.nc
        self.phase_barrier()
        self.ctx_front.close()
        self.ctx_p2.close()
        self.ctx_p4 = ExitStack()
        self.ctx_p4sb = ExitStack()
        psum = lambda n, s, d=F32: self.psum(n, s, d, self.ctx_p4)
        self.pb_rep = [psum("pb_rep0", [P, 128], F32)]
        self.pb_kk = psum("pb_kk", [P, 128], F32)
        self.pb_qk = psum("pb_qk", [P, 128], F32)
        self.pb_p = [psum("pb_p0", [P, 128], F32), psum("pb_p1", [P, 128], F32)]
        self.pb_b = psum("pb_b", [P, 128], F32)
        self.pb_g = [psum("pb_g0", [P, 128], F32), psum("pb_g1", [P, 128], F32)]
        sb = self.sb4
        self.q_ch = sb("q_ch", [P, 16, P], BF16)
        self.k_ch = sb("k_ch", [P, 16, P], BF16)
        self.v_ch = sb("v_ch", [P, 32, P], BF16)
        self.z_ch = sb("z_ch", [P, 32, P], BF16)
        self.kr_ch = sb("kr_ch", [P, 16, P], BF16)
        self.vr_ch = sb("vr_ch", [P, 32, P], BF16)
        self.tcol = sb("tcol", [P, 8, 32])
        self.repbuf = sb("repbuf", [P, P])
        self.msk = [sb("msk0", [P, P]), sb("msk1", [P, P]), sb("msk2", [P, P])]
        self.tsp = [[sb("tsp%d_%d" % (rp, i), [P, P], BF16) for i in range(2)] for rp in range(2)]
        self.tsb = [[sb("tsb%d_%d" % (rp, i), [P, P], BF16) for i in range(2)] for rp in range(2)]
        self.tsg = [[sb("tsg%d_%d" % (rp, i), [P, P], BF16) for i in range(2)] for rp in range(2)]
        self.kbe_sb = sb("kbe_sb", [P, P], BF16)
        self.ktil_s = sb("ktil_s", [P, 32, P], BF16)
        self.vb_sb = sb("vb_sb", [P, P], BF16)
        self.val_s = sb("val_s", [P, 32, P], BF16)
        self.kcdT_s = sb("kcdT_s", [P, 32, P], BF16)
        self.qt_s = sb("qt_s", [P, 32, P], BF16)
        self.attnT_s = sb("attnT_s", [P, 32, P], BF16)
        self.vnew_sb = sb("vnew_sb", [P, P], BF16)
        self.hT_sb = sb("hT_sb", [P, P], BF16)
        self.eu_f = sb("eu_f", [P, P])
        self.sq128 = sb("sq128", [P, P], BF16)
        self.rsq_row = sb("rsq_row", [1, P], BF16)
        self.rsq_rowf = sb("rsq_rowf", [1, P])
        self.rsqbc_sb = sb("rsqbc_sb", [P, P])
        self.hg_sb = sb("hg_sb", [P, 32, P], BF16)
        self.gambc = sb("gambc", [P, 32 * NCH])

        s_ld4 = self.sem("s_ld4")
        s_t4 = self.sem("s_t4")
        s_e4 = self.sem("s_e4")
        s_s4 = self.sem("s_s4")
        s_m4 = self.sem("s_m4")
        s_sc4 = self.sem("s_sc4")
        s_hg = self.sem("s_hg")
        s_pol = self.sem("s_pol")
        s_pol2 = self.sem("s_pol2")
        s_sto = self.sem("s_sto")

        t_eyef = self.t_eyef
        self._ppg = [None, None]
        self._pgg = [None, None]
        t_gb = E("tensor", lambda e: e.matmul(self.pb_g[0][:, :], self.onesrow_f[0:1, :],
                                              self.gamrow[0:1, 0:128], start=True, stop=True),
                 waits=[self.t_gamrow, self.t_onesrow], inc=s_m4)
        t_gbe = E("vector", lambda e: e.tensor_copy(self.gambc[:, 0:128], self.pb_g[0][:, :]),
                  waits=[t_gb], inc=s_e4)
        t_gb2 = E("tensor", lambda e: e.matmul(self.pb_g[0][:, :], self.onesrow_f[0:1, :],
                                               self.gamrow[0:1, 128:256], start=True, stop=True),
                  waits=[self.t_gamrow, t_gbe], inc=s_m4)
        t_gbe2 = E("vector", lambda e: e.tensor_copy(self.gambc[:, 128:256], self.pb_g[0][:, :]),
                   waits=[t_gb2], inc=s_e4)
        t_gambc = t_gbe2
        self._pgg[0] = t_gbe2

        mixed_done = self.t_qkvz_done[0]
        sheets_done = list(self.t_sheets)
        SHEETS = [self.sh_u, self.sh_w, self.sh_p, self.sh_be, self.sh_eg,
                  self.sh_bet, self.sh_eu]
        ch_guard = None
        scan_started = False

        def rep_T(col_ap, wait_toks):
            tok_v = E("vector", lambda e, col_ap=col_ap: e.tensor_copy(
                self.repbuf[:, :], col_ap.broadcast_to((P, P))),
                waits=wait_toks, inc=s_e4)
            tok_t = E("tensor", lambda e: e.transpose(
                self.pb_rep[0][:, :], self.repbuf[:, :], self.eyef[:, :]),
                waits=[tok_v, t_eyef] + self._repguard, inc=s_t4)
            return self.pb_rep[0], tok_t, 0

        self._repguard = []
        self._eug = None
        self._hTg = None
        self._tcol_Sg = None

        for ci in range(NCH):
            c0 = ci * P
            self.ph("p4l")
            t_q = E("gpsimd", lambda e, ci=ci: e.dma_start(
                out=self.q_ch[:, :, :], in_=self.mixed_hbm[0:16, :, ci * P:(ci + 1) * P].rearrange("t p j -> p t j")),
                waits=[mixed_done, ch_guard], inc=s_ld4, inc_amt=16, dma=True)
            t_k = E("gpsimd", lambda e, ci=ci: e.dma_start(
                out=self.k_ch[:, :, :], in_=self.mixed_hbm[16:32, :, ci * P:(ci + 1) * P].rearrange("t p j -> p t j")),
                waits=[mixed_done, ch_guard], inc=s_ld4, inc_amt=16, dma=True)
            t_v = E("gpsimd", lambda e, ci=ci: e.dma_start(
                out=self.v_ch[:, :, :], in_=self.mixed_hbm[32:64, :, ci * P:(ci + 1) * P].rearrange("t p j -> p t j")),
                waits=[mixed_done, ch_guard], inc=s_ld4, inc_amt=16, dma=True)
            t_z = E("gpsimd", lambda e, ci=ci: e.dma_start(
                out=self.z_ch[:, :, :], in_=self.z_hbm[:, :, ci * P:(ci + 1) * P].rearrange("t p j -> p t j")),
                waits=[mixed_done, ch_guard], inc=s_ld4, inc_amt=16, dma=True)
            self.ph("p4l2")
            t_tc = []
            tc_guard = ch_guard
            for j, sh in enumerate(SHEETS):
                tok_t = E("tensor", lambda e, sh=sh, c0=c0: e.transpose(
                    self.pb_rep[0][0:P, 0:32], sh[:, c0:c0 + P], self.eyef[0:32, 0:32]),
                    waits=sheets_done + [t_eyef, tc_guard, self._tcol_Sg] + self._repguard, inc=s_t4)
                tok_e = E("vector", lambda e, j=j: e.tensor_copy(
                    self.tcol[:, j, :], self.pb_rep[0][0:P, 0:32]),
                    waits=[tok_t], inc=s_e4)
                self._repguard = [tok_e]
                t_tc.append(tok_e)
            self.ph("p4l3")
            slots = [self.pb_kk[:, :].bitcast(BF16)[:, 0:P],
                     self.pb_qk[:, :].bitcast(BF16)[:, 0:P]]
            t_kr, t_vr = [], []
            self._ktguard = [self._kkguard, self._qkguard]
            tr_i = 0
            for grp, (src_t, dst_t, n, t_src) in enumerate(
                    ((self.k_ch, self.kr_ch, 16, t_k),
                     (self.v_ch, self.vr_ch, 32, t_v))):
                for i in range(n):
                    sl = tr_i % 2
                    tr_i += 1
                    tok_t = E("tensor", lambda e, src_t=src_t, i=i, sl=sl: e.transpose(
                        slots[sl], src_t[:, i, :], self.eye_sb[:, :]),
                        waits=[t_src, self._ktguard[sl]], inc=s_t4)
                    if sl == 0:
                        f = lambda e, dst_t=dst_t, i=i, sl=sl: e.tensor_copy(dst_t[:, i, :], slots[sl])
                        tok_e = E("vector", f, waits=[tok_t], inc=s_e4)
                    else:
                        f = lambda e, dst_t=dst_t, i=i, sl=sl: e.copy(dst_t[:, i, :], slots[sl])
                        tok_e = E("scalar", f, waits=[tok_t], inc=s_sc4)
                    self._ktguard[sl] = tok_e
                    (t_kr if grp == 0 else t_vr).append(tok_e)
            self._kkguard = self._ktguard[0]
            self._qkguard = self._ktguard[1]
            self.ph("p4p")
            prep_done = {}

            def emit_masks_kkqk(r):
                kh = r % 16
                rp = r % 2
                self.ph("p4p1")
                ps_u, tk_u, _ = rep_T(self.tcol[:, 0, r:r + 1], t_tc)
                m0 = E("vector", lambda e, ps_u=ps_u: e.tensor_add(self.msk[0][:, :], ps_u[:, :], self.mtQ_sb[:, :]),
                       waits=[tk_u, self.t_consts[5]], inc=s_e4)
                self._repguard = [m0]
                e0 = E("scalar", lambda e, r=r: e.activation(self.msk[0][:, :], self.msk[0][:, :], AF.Exp,
                                                             bias=self.tcol[:, 1, r:r + 1]),
                       waits=[m0], inc=s_sc4)
                ps_w, tk_w, _ = rep_T(self.tcol[:, 1, r:r + 1], t_tc)
                m1 = E("vector", lambda e, ps_w=ps_w: e.tensor_add(self.msk[1][:, :], ps_w[:, :], self.mtA_sb[:, :]),
                       waits=[tk_w, self.t_consts[3]], inc=s_e4)
                self._repguard = [m1]
                e1 = E("scalar", lambda e, r=r: e.activation(self.msk[1][:, :], self.msk[1][:, :], AF.Exp,
                                                             bias=self.tcol[:, 2, r:r + 1]),
                       waits=[m1], inc=s_sc4)
                ps_p, tk_p, _ = rep_T(self.tcol[:, 2, r:r + 1], t_tc)
                m2 = E("vector", lambda e, ps_p=ps_p: e.tensor_add(self.msk[2][:, :], ps_p[:, :], self.mtAT_sb[:, :]),
                       waits=[tk_p, self.t_consts[4]], inc=s_e4)
                self._repguard = [m2]
                e2 = E("scalar", lambda e, r=r: e.activation(self.msk[2][:, :], self.msk[2][:, :], AF.Exp,
                                                             bias=self.tcol[:, 1, r:r + 1]),
                       waits=[m2], inc=s_sc4)
                tsp, tsb, tsg = self.tsp[rp], self.tsb[rp], self.tsg[rp]
                t_kk = E("tensor", lambda e, kh=kh: e.matmul(self.pb_kk[:, :], self.k_ch[:, kh, :], self.k_ch[:, kh, :],
                                                             start=True, stop=True),
                         waits=[t_k, self._kkguard], inc=s_m4)
                t_qk = E("tensor", lambda e, kh=kh: e.matmul(self.pb_qk[:, :], self.k_ch[:, kh, :], self.q_ch[:, kh, :],
                                                             start=True, stop=True),
                         waits=[t_q, t_k, self._qkguard], inc=s_m4)
                aA = E("vector", lambda e, tsp=tsp: e.scalar_tensor_tensor(tsp[0][:, :], self.pb_kk[:, :], -1.0,
                                                                           self.msk[1][:, :], ALU.mult, ALU.mult),
                       waits=[t_kk, e1], inc=s_e4)
                aAT = E("vector", lambda e, tsb=tsb: e.scalar_tensor_tensor(tsb[0][:, :], self.pb_kk[:, :], -1.0,
                                                                            self.msk[2][:, :], ALU.mult, ALU.mult),
                        waits=[t_kk, e2], inc=s_e4)
                self._kkguard = aAT
                aQ = E("vector", lambda e, r=r: e.tensor_mul(self.attnT_s[:, r, :], self.pb_qk[:, :], self.msk[0][:, :]),
                       waits=[t_qk, e0], inc=s_e4)
                self._qkguard = aQ
                g0 = E("vector", lambda e, tsg=tsg, tsb=tsb: e.tensor_add(tsg[0][:, :], tsb[0][:, :], self.eye_sb[:, :]),
                       waits=[aAT], inc=s_e4)
                return dict(r=r, kh=kh, rp=rp, cp=0, cb=0, cg=0,
                            tok_p=aA, tok_b=aAT, tok_g=g0, aA=aA, aAT=aAT, aQ=aQ)

            def emit_level(sts, j):
                caps = []
                for st in sts:
                    rp = st["rp"]
                    caps.append((st, rp, self.tsp[rp], self.tsb[rp], self.tsg[rp],
                                 st["cp"], st["cb"], st["cg"]))
                w_ps = [t for st in sts for t in (st["tok_p"], st["tok_b"], self._ppg[st["rp"]])]
                t_ps = E("tensor", lambda e, caps=caps: [
                    e.matmul(self.pb_p[rp][:, :], tsb[cb][:, :], tsp[cp][:, :], start=True, stop=True)
                    for (st, rp, tsp, tsb, tsg, cp, cb, cg) in caps][-1],
                    waits=w_ps, inc=s_m4)
                if j < 6:
                    for (st, rp, tsp, tsb, tsg, cp, cb, cg) in caps:
                        t_bs = E("tensor", lambda e, tsp=tsp, tsb=tsb, cp=cp, cb=cb: e.matmul(
                            self.pb_b[:, :], tsp[cp][:, :], tsb[cb][:, :], start=True, stop=True),
                            waits=[st["tok_p"], st["tok_b"], self._pbg], inc=s_m4)
                        ev_b = E("scalar", lambda e, tsb=tsb, cb=cb: e.copy(tsb[1 - cb][:, :], self.pb_b[:, :]),
                                 waits=[t_bs], inc=s_sc4)
                        self._pbg = ev_b
                        st["tok_b"] = ev_b
                ev_p = E("vector", lambda e, caps=caps: [
                    e.tensor_copy(tsp[1 - cp][:, :], self.pb_p[rp][:, :])
                    for (st, rp, tsp, tsb, tsg, cp, cb, cg) in caps][-1],
                    waits=[t_ps], inc=s_e4)
                w_gm = [ev_p, t_gambc] + [t for st in sts for t in (st["tok_g"], self._pgg[st["rp"]])]
                t_gm = E("tensor", lambda e, caps=caps: [
                    (e.matmul(self.pb_g[rp][:, :], tsp[1 - cp][:, :], tsg[cg][:, :], start=True, stop=False),
                     e.matmul(self.pb_g[rp][:, :], self.eye_sb[:, :], tsg[cg][:, :], start=False, stop=True))[-1]
                    for (st, rp, tsp, tsb, tsg, cp, cb, cg) in caps][-1],
                    waits=w_gm, inc=s_m4)
                ev_g = E("vector", lambda e, caps=caps: [
                    e.tensor_copy(tsg[1 - cg][:, :], self.pb_g[rp][:, :])
                    for (st, rp, tsp, tsb, tsg, cp, cb, cg) in caps][-1],
                    waits=[t_gm], inc=s_e4)
                for st in sts:
                    rp = st["rp"]
                    self._ppg[rp] = ev_p
                    self._pgg[rp] = ev_g
                    st["tok_g"] = ev_g
                    st["tok_p"] = ev_p
                    st["cp"], st["cb"], st["cg"] = 1 - st["cp"], 1 - st["cb"], 1 - st["cg"]

            def emit_apps(st):
                r, kh, rp = st["r"], st["kh"], st["rp"]
                tsg = self.tsg[rp]
                G_slot = st["cg"]
                tok_g = st["tok_g"]
                self.ph("p4p3")
                kbe1 = E("scalar", lambda e, kh=kh, r=r: e.activation(
                    self.kbe_sb[:, :], self.kr_ch[:, kh, :], AF.Identity,
                    scale=self.tcol[:, 3, r:r + 1]),
                    waits=t_kr + t_tc + [self._kbeg], inc=s_sc4)
                kt1 = E("scalar", lambda e, kh=kh, r=r: e.activation(
                    self.ktil_s[:, r, :], self.kr_ch[:, kh, :], AF.Identity,
                    scale=self.tcol[:, 4, r:r + 1]),
                    waits=t_kr + t_tc, inc=s_sc4)
                vb1 = E("scalar", lambda e, r=r: e.activation(
                    self.vb_sb[:, :], self.vr_ch[:, VH[r], :], AF.Identity,
                    scale=self.tcol[:, 5, r:r + 1]),
                    waits=t_vr + t_tc + [self._vbg], inc=s_sc4)
                t_val = E("tensor", lambda e, cg=G_slot, tsg=tsg: e.matmul(self.pb_kk[:, :], tsg[cg][:, :],
                                                                           self.vb_sb[:, :], start=True, stop=True),
                          waits=[tok_g, vb1, st["aA"], st["aAT"], self._kkguard], inc=s_m4)
                ev_val = E("vector", lambda e, r=r: e.tensor_copy(self.val_s[:, r, :], self.pb_kk[:, :]),
                           waits=[t_val], inc=s_e4)
                self._kkguard = ev_val
                self._vbg = t_val
                t_kcd = E("tensor", lambda e, cg=G_slot, tsg=tsg: e.matmul(self.pb_qk[:, :], self.kbe_sb[:, :],
                                                                           tsg[cg][:, :], start=True, stop=True),
                          waits=[tok_g, kbe1, st["aQ"], self._qkguard], inc=s_m4)
                ev_kcd = E("scalar", lambda e, r=r: e.copy(self.kcdT_s[:, r, :], self.pb_qk[:, :]),
                           waits=[t_kcd], inc=s_sc4)
                self._qkguard = ev_kcd
                self._kbeg = t_kcd
                ps_e, tk_e, _ = rep_T(self.tcol[:, 6, r:r + 1], t_tc)
                qt1 = E("vector", lambda e, kh=kh, r=r, ps_e=ps_e: e.scalar_tensor_tensor(
                    self.qt_s[:, r, :], ps_e[:, :], 1.0, self.q_ch[:, kh, :], ALU.mult, ALU.mult),
                    waits=[tk_e, t_q], inc=s_e4)
                self._repguard = [qt1]
                prep_done[r] = [ev_val, ev_kcd, qt1, st["aQ"], kt1]

            for pr in range(16):
                sts = [emit_masks_kkqk(2 * pr), emit_masks_kkqk(2 * pr + 1)]
                self.ph("p4p2")
                for j in range(1, 7):
                    emit_level(sts, j)
                for st in sts:
                    emit_apps(st)
            for r in ():
                if "prep" in self.probes and ci == 0 and r == 0:
                    s_pp = self.sem("s_pp")
                    pa = self.probe_aps["prep"]
                    for i, (tn, ap_) in enumerate([
                            (self.attnT_s, self.attnT_s[:, 0, :]),
                            (None, self.tsg[G_slot][:, :]),
                            (None, self.val_s[:, 0, :]),
                            (None, self.kcdT_s[:, 0, :]),
                            (None, self.qt_s[:, 0, :]),
                            (None, self.tsp[0][:, :]),
                            (None, self.tsb[0][:, :])]):
                        E("sync", lambda e, i=i, ap_=ap_: e.dma_start(out=pa[i, :, :], in_=ap_),
                          waits=prep_done[r] + [aA, aAT], inc=s_pp, inc_amt=16, dma=True)
                    self._t_prep_probe = (s_pp.h, s_pp.n)
            self.ph("p4s")
            if ci == 0:
                def poll(e):
                    flag_reg = e.alloc_register("flag_pl")
                    cnt_reg = e.alloc_register("cnt_pl")
                    tgt_reg = e.alloc_register("tgt_pl")
                    e.reg_mov(cnt_reg, 0)
                    e.reg_load(tgt_reg, self.ptgt_sb[0:1, 0:1])
                    e.reg_mov(flag_reg, -12345)
                    me = self._me_rv_gp
                    with e.While(lambda: RuntimeValue(flag_reg) != RuntimeValue(tgt_reg)):
                        e.dma_start(out=self.flag_sb[:, :], in_=self.fl_sh[ds(1 - me, 1), :, :]).then_inc(s_pol.h, 16)
                        e.reg_add(cnt_reg, cnt_reg, 16)
                        e.wait_ge(s_pol.h, RuntimeValue(cnt_reg))
                        e.reg_load(flag_reg, self.flag_sb[0:1, 0:1])
                    return e.dma_start(out=self.state_f[:, :, :],
                                       in_=self.st_sh[ds(1 - me, 1), :, :, :].rearrange("o h a b -> (o a) h b"))
                t_poll = E("gpsimd", poll, waits=[self.t_hs_done], inc=s_pol2, inc_amt=16, dma=True)
                t_init = E("vector", lambda e: e.tensor_scalar(
                    self.state_f[:, :, :], self.state_f[:, :, :], self.parf_col[:, 0:1], None, ALU.mult),
                    waits=[t_poll, (self.s_hs.h, self.s_hs.n)], inc=s_e4)
                t_initb = E("scalar", lambda e: e.copy(self.state_b[:, :, :], self.state_f[:, :, :]),
                            waits=[t_init], inc=s_sc4)
                self._stok = [[t_initb, t_init]] * 32
            hg_toks = []
            for r in range(32):
                pd = prep_done[r]
                sready = self._stok[r]
                rp = r % 2
                t_vn = E("tensor", lambda e, r=r, rp=rp: e.matmul(self.pb_p[rp][:, :], self.kcdT_s[:, r, :],
                                                                  self.state_b[:, r, :], start=True, stop=True),
                         waits=pd + sready + [self._ppg[rp]], inc=s_m4)
                vn = E("vector", lambda e, r=r, rp=rp: e.tensor_sub(self.vnew_sb[:, :], self.val_s[:, r, :],
                                                                    self.pb_p[rp][:, :]),
                       waits=[t_vn, self._vng], inc=s_e4)
                self._ppg[rp] = vn
                t_h = E("tensor", lambda e, r=r: (
                    e.matmul(self.pb_b[:, :], self.state_b[:, r, :], self.qt_s[:, r, :], start=True, stop=False),
                    e.matmul(self.pb_b[:, :], self.vnew_sb[:, :], self.attnT_s[:, r, :], start=False, stop=True))[-1],
                    waits=[vn, self._pbg], inc=s_m4)
                ev_h = E("vector", lambda e: e.tensor_copy(self.hT_sb[:, :], self.pb_b[:, :]),
                         waits=[t_h, self._hTg], inc=s_e4)
                self._pbg = ev_h
                t_sq = E("scalar", lambda e: e.activation(self.sq128[:, :], self.hT_sb[:, :], AF.Square),
                         waits=[ev_h], inc=s_sc4)
                self._hTg = t_sq
                t_cs = E("tensor", lambda e: e.matmul(self.pb_kk[0:1, :], self.ones_col[:, :], self.sq128[:, :],
                                                      start=True, stop=True),
                         waits=[t_sq, self._kkguard], inc=s_m4)
                bb_eps, tok_eps2 = self.ccol(EPS_RMS, rows=1)
                t_l1 = E("scalar", lambda e, bb=bb_eps: e.activation(self.rsq_rowf[:, :], self.pb_kk[0:1, :], AF.Ln,
                                                                     bias=bb, scale=1.0 / 128.0),
                         waits=[t_cs, tok_eps2, self._rsqfg], inc=s_sc4)
                t_l2 = E("scalar", lambda e: e.activation(self.rsq_row[:, :], self.rsq_rowf[:, :], AF.Exp, scale=-0.5),
                         waits=[t_l1], inc=s_sc4)
                self._rsqfg = t_l2
                t_rb = E("tensor", lambda e: e.matmul(self.pb_qk[:, :], self.onesrow_b[0:1, :], self.rsq_row[0:1, :],
                                                      start=True, stop=True),
                         waits=[t_l2, self._qkguard, self.t_onesrow], inc=s_m4)
                ev_rb = E("vector", lambda e: e.tensor_copy(self.rsqbc_sb[:, :], self.pb_qk[:, :]),
                          waits=[t_rb, self._rsqg], inc=s_e4)
                self._qkguard = ev_rb
                g1 = E("vector", lambda e: e.tensor_mul(self.msk[0][:, :], self.hT_sb[:, :], self.rsqbc_sb[:, :]),
                       waits=[ev_rb, ev_h, self._mskg], inc=s_e4)
                self._rsqg = g1
                g2 = E("vector", lambda e, r=r: e.scalar_tensor_tensor(
                    self.hg_sb[:, VH[r], :], self.z_ch[:, VH[r], :], self.nw_sb[:, 0:1],
                    self.msk[0][:, :], ALU.mult, ALU.mult),
                    waits=[g1, t_z, self.t_nw], inc=s_e4)
                self._mskg = g2
                hg_toks.append(g2)
                self._kkguard = t_l1
                t_su = E("tensor", lambda e, r=r, rp=rp: e.matmul(self.pb_g[rp][:, :], self.ktil_s[:, r, :],
                                                                  self.vnew_sb[:, :], start=True, stop=True),
                         waits=[vn, self._pgg[rp]], inc=s_m4)
                self._vng = t_su
                if "prep" in self.probes and ci == 0 and r == 0:
                    s_pp = self.sems["s_pp"]
                    pa = self.probe_aps["prep"]
                    E("sync", lambda e: e.dma_start(out=pa[7, :, :], in_=self.vnew_sb[:, :]),
                      waits=[vn, g2], inc=s_pp, inc_amt=16, dma=True)
                    E("gpsimd", lambda e: e.dma_start(out=pa[8, :, :], in_=self.msk[0][:, :]),
                      waits=[g1, g2], inc=s_pp, inc_amt=16, dma=True)
                    E("sync", lambda e: e.dma_start(out=pa[9, :, :], in_=self.sq128[:, :]),
                      waits=[t_sq, g2, t_cs], inc=s_pp, inc_amt=16, dma=True)
                su = E("vector", lambda e, r=r, ci=ci, rp=rp: e.scalar_tensor_tensor(
                    self.state_f[:, r, :], self.state_f[:, r, :], self.gambc[:, r * NCH + ci:r * NCH + ci + 1],
                    self.pb_g[rp][:, :], ALU.mult, ALU.add),
                    waits=[t_su, t_gambc], inc=s_e4)
                self._pgg[rp] = su
                sb1 = E("scalar", lambda e, r=r: e.copy(self.state_b[:, r, :], self.state_f[:, r, :]),
                        waits=[su], inc=s_sc4)
                self._stok[r] = [sb1]
            self._tcol_Sg = self._stok[31][0]
            t_hgst = E("sync", lambda e, ci=ci: e.dma_start(
                out=self.hg_hbm[ci, :, :, :].rearrange("h a b -> a h b"),
                in_=self.hg_sb[:, :, :]),
                waits=hg_toks, inc=s_hg, inc_amt=16, dma=True)
            ch_guard = t_hgst
        self.ph("p4h")
        t_pub = E("gpsimd", lambda e: e.dma_start(
            out=self.st_sh[ds(self._me_rv_gp, 1), :, :, :].rearrange("o h a b -> (o a) h b"),
            in_=self.state_f[:, :, :]),
            waits=[self._stok[31][0], (self.sems["s_e4"].h, self.sems["s_e4"].n)],
            inc=s_sto, inc_amt=16, dma=True)
        E("gpsimd", lambda e: e.dma_start(out=self.fl_sh[ds(self._me_rv_gp, 1), :, :], in_=self.sent_sb[0:1, 0:1]),
          waits=[t_pub], inc=s_sto, inc_amt=16, dma=True)
        self.t_hg_all = (s_hg.h, s_hg.n)

    def phase6_wout(self):
        self.ph("p6")
        """W_out (two c-halves, ci-outer, vh accumulated in psum) + LN + residual.
        Half-0 raw rows bounce through x2_hbm; half-1 completes rows, does both
        layernorms, writes x2 to HBM and x2lnT to SBUF."""
        E = self.E
        self.ctx_p4.close()
        self.ctx_p4sb.close()
        self.ctx_mid.close()
        self.ctx_p6 = ExitStack()
        self.ctx_p6keep = ExitStack()
        self.ctx_p6sb = ExitStack()
        sb6k = lambda n, s, d=F32: self.ctx_p6keep.enter_context(self.nc.sbuf_tensor(n, s, d, side="right"))
        sb6 = lambda n, s, d=F32: self.ctx_p6sb.enter_context(self.nc.sbuf_tensor(n, s, d, side="right"))
        ps6 = lambda n, s, d=F32: self.psum(n, s, d, self.ctx_p6)
        self.x2lnT = sb6k("x2lnT", [P, KD, NTOK], BF16)
        self.wo_sb = sb6("wo_sb", [P, 32, NTOK], BF16)
        self.hg6 = sb6("hg6", [P, 2, 32, P], BF16)
        self.x2_sb = sb6("x2_sb", [P, 2, D])
        self.x6_sb = sb6("x6_sb", [P, 2, D])
        self.st6 = sb6("st6", [P, 12])
        self.x2ln_sb = sb6("x2ln_sb", [P, 2, D], BF16)
        self.ps_w6 = [ps6("ps_w6a", [P, 2, 512]), ps6("ps_w6b", [P, 2, 512])]
        self.ps_t6 = [ps6("ps_t6a", [P, P], BF16), ps6("ps_t6b", [P, P], BF16)]
        s_w6 = self.sem("s_w6")
        s_h6 = self.sem("s_h6")
        s_m6 = self.sem("s_m6")
        s_e6 = self.sem("s_e6")
        s_s6 = self.sem("s_s6")
        s_x6 = self.sem("s_x6")
        s_st6 = self.sem("s_st6")
        s_t6 = self.sem("s_t6")
        hg_all = self.t_hg_all
        self.phase_barrier()
        ev_guard = [None, None]
        hg_guard = [None, None]
        x2g = [None, None]
        half_toks = {}
        wo_guard = None
        self._lnTg = [None, None]
        bb_eps, tok_eps = self.ccol(EPS_LN)
        for half in range(2):
            cs = half * NTOK
            t_w = E("gpsimd", lambda e, cs=cs: e.dma_start(
                out=self.wo_sb[:, :, :],
                in_=self.woutT_i.rearrange("(vh p) c -> p vh c", p=P)[:, :, cs:cs + NTOK]),
                waits=[wo_guard], inc=s_w6, inc_amt=16, dma=True)
            for ci in range(NCH):
                rr = ci % 2
                t_hg = E("sync", lambda e, ci=ci, rr=rr: e.dma_start(
                    out=self.hg6[:, rr, :, :], in_=self.hg_hbm[ci, :, :, :].rearrange("h a b -> a h b")),
                    waits=[hg_all, hg_guard[rr]], inc=s_h6, inc_amt=16, dma=True)
                def mm6(e, rr=rr):
                    last = None
                    for vh in range(32):
                        for cc in range(2):
                            last = e.matmul(self.ps_w6[rr][:, cc, :],
                                            self.hg6[:, rr, vh, :],
                                            self.wo_sb[:, vh, cc * 512:cc * 512 + 512],
                                            start=(vh == 0), stop=(vh == 31))
                    return last
                t_mm = E("tensor", mm6, waits=[t_w, t_hg, ev_guard[rr]], inc=s_m6)
                hg_guard[rr] = t_mm
                wo_guard = t_mm
                if half == 0:
                    ev1 = E("vector", lambda e, rr=rr: e.tensor_copy(
                        self.x2_sb[:, rr, 0:512], self.ps_w6[rr][:, 0, :]),
                        waits=[t_mm, x2g[rr]], inc=s_e6)
                    ev2 = E("vector", lambda e, rr=rr: e.tensor_copy(
                        self.x2_sb[:, rr, 512:1024], self.ps_w6[rr][:, 1, :]),
                        waits=[t_mm], inc=s_e6)
                    ev_guard[rr] = ev2
                    t_st = E("sync", lambda e, ci=ci, rr=rr: e.dma_start(
                        out=self.x2_hbm[ci * P:(ci + 1) * P, 0:1024], in_=self.x2_sb[:, rr, 0:1024]),
                        waits=[ev1, ev2], inc=s_st6, inc_amt=16, dma=True)
                    x2g[rr] = t_st
                    half_toks[ci] = t_st
                    continue
                t_r0 = E("sync", lambda e, ci=ci, rr=rr: e.dma_start(
                    out=self.x2_sb[:, rr, 0:1024], in_=self.x2_hbm[ci * P:(ci + 1) * P, 0:1024]),
                    waits=[half_toks[ci], x2g[rr]], inc=s_x6, inc_amt=16, dma=True)
                ev1 = E("vector", lambda e, rr=rr: e.tensor_copy(
                    self.x2_sb[:, rr, 1024:1536], self.ps_w6[rr][:, 0, :]),
                    waits=[t_mm, t_r0], inc=s_e6)
                ev2 = E("vector", lambda e, rr=rr: e.tensor_copy(
                    self.x2_sb[:, rr, 1536:2048], self.ps_w6[rr][:, 1, :]),
                    waits=[t_mm], inc=s_e6)
                ev_guard[rr] = ev2
                t_x = E("sync", lambda e, ci=ci, rr=rr: e.dma_start(
                    out=self.x6_sb[:, rr, :], in_=self.x_strip[(1 + ci) * P:(2 + ci) * P, :]),
                    waits=[self._x6g[rr]], inc=s_x6, inc_amt=16, dma=True)
                c0 = rr * 6
                st_ = self.st6
                t_s1 = E("scalar", lambda e, rr=rr, c0=c0: e.activation(
                    self.x2ln_sb[:, rr, :], self.x2_sb[:, rr, :], AF.Identity,
                    accum_out=st_[:, c0:c0 + 1]),
                    waits=[t_r0, ev1, ev2, self._lnTg[rr]], inc=s_s6)
                t_s2 = E("scalar", lambda e, rr=rr, c0=c0: e.activation(
                    self.x2ln_sb[:, rr, :], self.x2_sb[:, rr, :], AF.Square,
                    accum_out=st_[:, c0 + 1:c0 + 2]), waits=[t_s1], inc=s_s6)
                t_mu = E("vector", lambda e, c0=c0: e.tensor_scalar(
                    st_[:, c0 + 2:c0 + 3], st_[:, c0:c0 + 1], 1.0 / D, None, ALU.mult),
                    waits=[t_s1], inc=s_e6)
                t_msq = E("vector", lambda e, c0=c0: e.tensor_tensor(
                    st_[:, c0 + 4:c0 + 5], st_[:, c0 + 2:c0 + 3], st_[:, c0 + 2:c0 + 3], ALU.mult),
                    waits=[t_mu], inc=s_e6)
                t_var = E("vector", lambda e, c0=c0: e.scalar_tensor_tensor(
                    st_[:, c0 + 3:c0 + 4], st_[:, c0 + 1:c0 + 2], 1.0 / D,
                    st_[:, c0 + 4:c0 + 5], ALU.mult, ALU.subtract),
                    waits=[t_s2, t_msq], inc=s_e6)
                t_lnv = E("scalar", lambda e, c0=c0, bb=bb_eps: e.activation(
                    st_[:, c0 + 4:c0 + 5], st_[:, c0 + 3:c0 + 4], AF.Ln, bias=bb),
                    waits=[t_var, tok_eps], inc=s_s6)
                t_rstd = E("scalar", lambda e, c0=c0: e.activation(
                    st_[:, c0 + 3:c0 + 4], st_[:, c0 + 4:c0 + 5], AF.Exp, scale=-0.5),
                    waits=[t_lnv], inc=s_s6)
                t_n1 = E("vector", lambda e, rr=rr, c0=c0: e.tensor_scalar(
                    self.x2_sb[:, rr, :], self.x2_sb[:, rr, :],
                    st_[:, c0 + 2:c0 + 3], st_[:, c0 + 3:c0 + 4],
                    ALU.subtract, ALU.mult), waits=[t_rstd], inc=s_e6)
                t_n2 = E("vector", lambda e, rr=rr: e.tensor_add(
                    self.x2_sb[:, rr, :], self.x2_sb[:, rr, :], self.x6_sb[:, rr, :]),
                    waits=[t_n1, t_x], inc=s_e6)
                self._x6g[rr] = t_n2
                t_st = E("sync", lambda e, ci=ci, rr=rr: e.dma_start(
                    out=self.x2_hbm[ci * P:(ci + 1) * P, :], in_=self.x2_sb[:, rr, :]),
                    waits=[t_n2], inc=s_st6, inc_amt=16, dma=True)
                x2g[rr] = t_st
                u_s1 = E("scalar", lambda e, rr=rr, c0=c0: e.activation(
                    self.x2ln_sb[:, rr, :], self.x2_sb[:, rr, :], AF.Identity,
                    accum_out=st_[:, c0:c0 + 1]), waits=[t_n2], inc=s_s6)
                u_s2 = E("scalar", lambda e, rr=rr, c0=c0: e.activation(
                    self.x2ln_sb[:, rr, :], self.x2_sb[:, rr, :], AF.Square,
                    accum_out=st_[:, c0 + 1:c0 + 2]), waits=[u_s1], inc=s_s6)
                u_mu = E("vector", lambda e, c0=c0: e.tensor_scalar(
                    st_[:, c0 + 2:c0 + 3], st_[:, c0:c0 + 1], 1.0 / D, None, ALU.mult),
                    waits=[u_s1], inc=s_e6)
                u_msq = E("vector", lambda e, c0=c0: e.tensor_tensor(
                    st_[:, c0 + 4:c0 + 5], st_[:, c0 + 2:c0 + 3], st_[:, c0 + 2:c0 + 3], ALU.mult),
                    waits=[u_mu], inc=s_e6)
                u_var = E("vector", lambda e, c0=c0: e.scalar_tensor_tensor(
                    st_[:, c0 + 3:c0 + 4], st_[:, c0 + 1:c0 + 2], 1.0 / D,
                    st_[:, c0 + 4:c0 + 5], ALU.mult, ALU.subtract),
                    waits=[u_s2, u_msq], inc=s_e6)
                u_lnv = E("scalar", lambda e, c0=c0, bb=bb_eps: e.activation(
                    st_[:, c0 + 4:c0 + 5], st_[:, c0 + 3:c0 + 4], AF.Ln, bias=bb),
                    waits=[u_var], inc=s_s6)
                u_rstd = E("scalar", lambda e, c0=c0: e.activation(
                    st_[:, c0 + 3:c0 + 4], st_[:, c0 + 4:c0 + 5], AF.Exp, scale=-0.5),
                    waits=[u_lnv], inc=s_s6)
                u_nrm = E("vector", lambda e, rr=rr, c0=c0: e.tensor_scalar(
                    self.x2ln_sb[:, rr, :], self.x2_sb[:, rr, :],
                    st_[:, c0 + 2:c0 + 3], st_[:, c0 + 3:c0 + 4],
                    ALU.subtract, ALU.mult), waits=[u_rstd], inc=s_e6)
                for kc in range(KD):
                    i = ci * KD + kc
                    pb = i % 2
                    t_tp = E("tensor", lambda e, kc=kc, rr=rr, pb=pb: e.transpose(
                        self.ps_t6[pb][:, :], self.x2ln_sb[:, rr, kc * P:(kc + 1) * P],
                        self.eye_sb[:, :]),
                        waits=[u_nrm, self._t6g[pb]], inc=s_t6)
                    eng = "vector" if i % 2 == 0 else "scalar"
                    f = (lambda e, kc=kc, ci=ci, pb=pb: e.tensor_copy(
                        self.x2lnT[:, kc, ci * P:(ci + 1) * P], self.ps_t6[pb][:, :])) \
                        if eng == "vector" else \
                        (lambda e, kc=kc, ci=ci, pb=pb: e.copy(
                            self.x2lnT[:, kc, ci * P:(ci + 1) * P], self.ps_t6[pb][:, :]))
                    tok = E(eng, f, waits=[t_tp], inc=s_e6 if eng == "vector" else s_s6)
                    self._t6g[pb] = tok
                    self._lnTg[rr] = t_tp
        self.t_x2lnT_done = [(self.sems["s_e6"].h, self.sems["s_e6"].n),
                             (self.sems["s_s6"].h, self.sems["s_s6"].n)]
        self.t_x2_hbm_done = (self.sems["s_st6"].h, self.sems["s_st6"].n)

    def phase7_ffn(self):
        self.ph("p7a")
        """FFN gating. Gating transients live in ctx_p7g (freed before experts);
        softT/scal/y survive in ctx_p7sb."""
        E = self.E
        self.phase_barrier()
        self.ctx_p6.close()
        self.ctx_p6sb.close()
        ps7s = ExitStack()
        self.ctx_p7 = ps7s
        self.ctx_p7sb = ExitStack()
        self.ctx_p7g = ExitStack()
        sb7 = lambda n, s, d=F32: self.ctx_p7sb.enter_context(self.nc.sbuf_tensor(n, s, d, side="right"))
        sb7g = lambda n, s, d=F32: self.ctx_p7g.enter_context(self.nc.sbuf_tensor(n, s, d, side="right"))
        ps7 = lambda n, s, d=F32: self.psum(n, s, d, ps7s)
        self.pA = ps7("pA", [P, 2, 512])
        self.pB = ps7("pB", [P, 2, 512])
        self.pC = ps7("pC", [P, 2, 512])
        self.pD = ps7("pD", [P, 2, 512])
        self.pDt = [self.pC[:, 0, 0:P], self.pC[:, 1, 0:P]]
        self.softT_sb = sb7("softT_sb", [8, 8, NTOK], BF16)
        self.scal_sb = sb7("scal_sb", [P, 8, 8])
        self.y_sb = sb7("y_sb", [P, 8, D], BF16)
        self.gwg_sb = sb7g("gwg_sb", [P, KD, 64], BF16)
        self.scores = sb7g("scores", [P, 8, 64])
        self.scr7 = sb7g("scr7", [P, 64])
        self.col7 = sb7g("col7", [P, 8])
        self.mr8 = sb7g("mr8", [P, 8])
        self.wd_sb = sb7g("wd_sb", [P, 64])
        self.wdT_sb = sb7g("wdT_sb", [64, NTOK], BF16)
        self.iot = sb7g("iot", [64, 64])
        self.invf = sb7g("invf", [1, 64])
        self.G_sb = sb7g("G_sb", [64, 64], BF16)
        self.flat_sb = sb7g("flat_sb", [P, 8, 64])
        self.ef_sb = sb7g("ef_sb", [P, 64])
        self.den7 = sb7g("den7", [P, 8, 1])
        self.soft_sb = sb7g("soft_sb", [P, 8, 64])
        s_l7 = self.sem("s_l7")
        s_m7 = self.sem("s_m7")
        s_e7 = self.sem("s_e7")
        s_s7 = self.sem("s_s7")
        s_g7 = self.sem("s_g7")
        xt = self.t_x2lnT_done

        t_gw = E("gpsimd", lambda e: e.dma_start(
            out=self.gwg_sb[:, :, :], in_=self.gwgT.rearrange("(kc kp) c -> kp kc c", kp=P)),
            inc=s_l7, inc_amt=16, dma=True)
        t_iv = E("gpsimd", lambda e: e.dma_start(out=self.invf[:, :], in_=self.inv_map),
                 inc=s_l7, inc_amt=16, dma=True)
        t_io = E("gpsimd", lambda e: e.iota(self.iot[:, :], [[0, 64]], base=0, channel_multiplier=1, allow_small_or_imprecise_dtypes=True),
                 inc=s_g7)
        t_ib = E("tensor", lambda e: e.matmul(self.pD[0:64, 0, 0:64], self.onesrow_f[0:1, 0:64],
                                              self.invf[0:1, :], start=True, stop=True),
                 waits=[t_iv, self.t_onesrow], inc=s_m7)
        t_G = E("vector", lambda e: e.tensor_tensor(self.G_sb[:, :], self.iot[:, :], self.pD[0:64, 0, 0:64], ALU.is_equal),
                waits=[t_ib, t_io], inc=s_e7)
        t_mr = E("vector", lambda e: e.memset(self.mr8[:, :], 3e38), inc=s_e7)
        sc_toks = []
        for tt in range(8):
            rr = tt % 2
            def mmsc(e, tt=tt, rr=rr):
                last = None
                for kc in range(KD):
                    last = e.matmul(self.pA[0:P, rr, 0:64], self.x2lnT[:, kc, tt * P:(tt + 1) * P],
                                    self.gwg_sb[:, kc, :], start=(kc == 0), stop=(kc == KD - 1))
                return last
            t_sc = E("tensor", mmsc, waits=[t_gw, self._pAg[rr]] + xt, inc=s_m7)
            t_mx = E("vector", lambda e, rr=rr: e.reduce_max(self.col7[:, 0:1], self.pA[0:P, rr, 0:64], axis=AXX),
                     waits=[t_sc], inc=s_e7)
            t_ng = E("vector", lambda e: e.tensor_scalar(self.col7[:, 1:2], self.col7[:, 0:1], -1.0, None, ALU.mult),
                     waits=[t_mx], inc=s_e7)
            t_ex = E("scalar", lambda e, tt=tt, rr=rr: e.activation(
                self.scores[:, tt, :], self.pA[0:P, rr, 0:64], AF.Exp, bias=self.col7[:, 1:2],
                accum_out=self.col7[:, 2:3]), waits=[t_ng], inc=s_s7)
            self._pAg[rr] = t_ex
            t_rc = E("vector", lambda e: e.reciprocal(self.col7[:, 3:4], self.col7[:, 2:3]),
                     waits=[t_ex], inc=s_e7)
            t_sm = E("vector", lambda e, tt=tt: e.tensor_scalar(
                self.scores[:, tt, :], self.scores[:, tt, :], self.col7[:, 3:4], None, ALU.mult),
                waits=[t_rc], inc=s_e7)
            t_cp = E("vector", lambda e, tt=tt: e.tensor_copy(self.scr7[:, :], self.scores[:, tt, :]),
                     waits=[t_sm], inc=s_e7)
            tk = t_cp
            for kk in range(6):
                t_m = E("vector", lambda e: e.reduce_max(self.mr8[:, 0:1], self.scr7[:, :], axis=AXX),
                        waits=[tk, t_mr], inc=s_e7)
                tk = E("vector", lambda e: e.match_replace(self.scr7[:, :], self.mr8[:, :], self.scr7[:, :], -1e30),
                       waits=[t_m], inc=s_e7)
            t_msk = E("vector", lambda e: e.tensor_scalar(self.scr7[:, :], self.scr7[:, :], -1e29, None, ALU.is_lt),
                      waits=[tk], inc=s_e7)
            t_wd = E("vector", lambda e, tt=tt: e.tensor_mul(self.wd_sb[:, :], self.scores[:, tt, :], self.scr7[:, :]),
                     waits=[t_msk], inc=s_e7)
            t_tp = E("tensor", lambda e, rr=rr: e.transpose(self.pDt[rr][0:64, :], self.wd_sb[:, 0:64], self.eyef[:, :]),
                     waits=[t_wd, self._pDtg[rr], self.t_eyef], inc=s_m7)
            t_we = E("vector", lambda e, tt=tt, rr=rr: e.tensor_copy(self.wdT_sb[:, tt * P:(tt + 1) * P], self.pDt[rr][0:64, :]),
                     waits=[t_tp], inc=s_e7)
            self._pDtg[rr] = t_we
            t_fl = E("tensor", lambda e, tt=tt, rr=rr: e.matmul(
                self.pB[0:P, rr, 0:64], self.wdT_sb[:, tt * P:(tt + 1) * P], self.G_sb[:, :],
                start=True, stop=True), waits=[t_we, t_G, self._pBg[rr]], inc=s_m7)
            t_fe = E("vector", lambda e, tt=tt, rr=rr: e.tensor_copy(self.flat_sb[:, tt, :], self.pB[0:P, rr, 0:64]),
                     waits=[t_fl], inc=s_e7)
            self._pBg[rr] = t_fe
            t_sl = E("vector", lambda e, tt=tt: e.reduce_sum(
                self.scal_sb[:, tt, :],
                self.flat_sb[:, tt, :].rearrange("p (f k) -> p f k", f=8), axis=AXX),
                waits=[t_fe], inc=s_e7)
            t_ef = E("scalar", lambda e, tt=tt: e.activation(self.ef_sb[:, :], self.flat_sb[:, tt, :], AF.Exp),
                     waits=[t_fe], inc=s_s7)
            t_nz = E("vector", lambda e, tt=tt: e.tensor_scalar(self.scr7[:, :], self.flat_sb[:, tt, :], 0.0, None, ALU.not_equal),
                     waits=[t_fe, t_wd], inc=s_e7)
            t_em = E("vector", lambda e: e.tensor_mul(self.ef_sb[:, :], self.ef_sb[:, :], self.scr7[:, :]),
                     waits=[t_ef, t_nz], inc=s_e7)
            t_dn = E("vector", lambda e: e.reduce_sum(self.den7[:, :, :], self.ef_sb[:, :].rearrange("p (f k) -> p f k", f=8), axis=AXX),
                     waits=[t_em], inc=s_e7)
            t_dm = E("vector", lambda e: e.tensor_scalar(self.den7[:, :, :], self.den7[:, :, :], 1e-30, None, ALU.max),
                     waits=[t_dn], inc=s_e7)
            t_dr = E("vector", lambda e: e.reciprocal(self.den7[:, :, :], self.den7[:, :, :]),
                     waits=[t_dm], inc=s_e7)
            t_sf = E("vector", lambda e, tt=tt: e.tensor_tensor(
                self.soft_sb[:, tt, :].rearrange("p (f k) -> p f k", f=8),
                self.ef_sb[:, :].rearrange("p (f k) -> p f k", f=8),
                self.den7[:, :, :].broadcast_to((P, 8, 8)),
                ALU.mult), waits=[t_dr], inc=s_e7)
            sc_toks.append(t_sf)
            for f in range(8):
                rr2 = f % 2
                t_t2 = E("tensor", lambda e, tt=tt, f=f, rr2=rr2: e.transpose(
                    self.pDt[rr2][0:8, :], self.soft_sb[:, tt, 8 * f:8 * f + 8], self.eyef[:, :]),
                    waits=[t_sf, self._pDtg[rr2]], inc=s_m7)
                t_e2 = E("vector", lambda e, tt=tt, f=f, rr2=rr2: e.tensor_copy(
                    self.softT_sb[:, f, tt * P:(tt + 1) * P], self.pDt[rr2][0:8, :]),
                    waits=[t_t2], inc=s_e7)
                self._pDtg[rr2] = t_e2
                sc_toks.append(t_e2)
        self.t_gating = sc_toks

    def phase7_experts(self):
        self.ph("p7b")
        E = self.E
        self.phase_barrier()
        self.ctx_p7g.close()
        sb7 = lambda n, s, d=F32: self.ctx_p7sb.enter_context(self.nc.sbuf_tensor(n, s, d, side="right"))
        self.E8_sb = sb7("E8_sb", [8, 64], BF16)
        self.qaTg = sb7("qaTg", [P, KD, 64], BF16)
        self.qaTu = sb7("qaTu", [P, KD, 64], BF16)
        self.muW_sb = sb7("muW_sb", [P, KD, 8], BF16)
        self.dqaT_sb = sb7("dqaT_sb", [P, 11, 64], BF16)
        self.qbg_sb = sb7("qbg_sb", [64, 1408], BF16)
        self.qbu_sb = sb7("qbu_sb", [64, 1408], BF16)
        self.qbd_sb = sb7("qbd_sb", [64, D], BF16)
        self.scrow = sb7("scrow", [1, 4864], BF16)
        self.qmt_sb = sb7("qmt_sb", [8, 64], BF16)
        self.umt_sb = sb7("umt_sb", [8, 64], BF16)
        self.gmt_sb = sb7("gmt_sb", [8, P], BF16)
        self.gmt2_sb = sb7("gmt2_sb", [8, P], BF16)
        self.wexp_sb = sb7("wexp_sb", [64, NTOK])
        self.awg_sb = sb7("awg_sb", [64, NTOK], BF16)
        self.awu_sb = sb7("awu_sb", [64, NTOK], BF16)
        self.awd_sb = sb7("awd_sb", [64, NTOK], BF16)
        self.w_ring = sb7("w_ring", [P, 2, KD, P], BF16)
        self.silu_sb = sb7("silu_sb", [P, NTOK], BF16)
        self.act_sb = sb7("act_sb", [P, 6, NTOK], BF16)
        self.dw_sb = sb7("dw_sb", [P, 6, D], BF16)
        self.da_f32 = sb7("da_f32", [64, NTOK])
        s_l7 = self.sems["s_l7"]
        s_m7 = self.sems["s_m7"]
        s_e7 = self.sems["s_e7"]
        s_s7 = self.sems["s_s7"]
        gat = self.t_gating
        I11 = 11
        self._yg = {}

        t_e8 = E("gpsimd", lambda e: e.dma_start(out=self.E8_sb[:, :], in_=self.E8_i),
                 inc=s_l7, inc_amt=16, dma=True)

        def casted_load(out_ap, in_ap, waits=()):
            return E("gpsimd", lambda e: e.dma_start(out=out_ap, in_=in_ap),
                     waits=list(waits), inc=s_l7, inc_amt=16, dma=True)

        for f in range(9):
            shared = (f == 8)
            softT_f = self.softT_sb[:, min(f, 7), :]
            if not shared:
                lds = {}
                lds["qag"] = casted_load(self.qaTg[:, :, :],
                                         self.gqaT_i[f].rearrange("(kc kp) c -> kp kc c", kp=P),
                                         [self._ldg.get("qag")])
                lds["qau"] = casted_load(self.qaTu[:, :, :],
                                         self.uqaT_i[f].rearrange("(kc kp) c -> kp kc c", kp=P),
                                         [self._ldg.get("qau")])
                lds["muw"] = casted_load(self.muW_sb[:, :, :],
                                         self.muW_i[f].rearrange("(kc kp) c -> kp kc c", kp=P),
                                         [self._ldg.get("muw")])
                lds["dqa"] = casted_load(self.dqaT_sb[:, :, :],
                                         self.dqaT_i[f].rearrange("(ic ip) c -> ip ic c", ip=P),
                                         [self._ldg.get("dqa")])
                lds["qbg"] = casted_load(self.qbg_sb[:, :], self.gqbF_i[f], [self._ldg.get("qbg")])
                lds["qbu"] = casted_load(self.qbu_sb[:, :], self.uqbF_i[f], [self._ldg.get("qbu")])
                lds["qbd"] = casted_load(self.qbd_sb[:, :], self.dqbF_i[f], [self._ldg.get("qbd")])
                lds["gsc"] = casted_load(self.scrow[:, 0:1408], self.gsc_i[f], [self._ldg.get("gsc")])
                lds["usc"] = casted_load(self.scrow[:, 1408:2816], self.usc_i[f], [self._ldg.get("gsc")])
                lds["dsc"] = casted_load(self.scrow[:, 2816:4864], self.dsc_i[f], [self._ldg.get("gsc")])
                pres = []
                scw = [lds["gsc"], lds["usc"], lds["dsc"], lds["qbg"], lds["qbu"], lds["qbd"]]
                for (qb, s0, width) in ((self.qbg_sb, 0, 1408), (self.qbu_sb, 1408, 1408), (self.qbd_sb, 2816, 2048)):
                    for c0 in range(0, width, 512):
                        cw = min(512, width - c0)
                        t_b = E("tensor", lambda e, s0=s0, c0=c0, cw=cw: e.matmul(
                            self.pD[0:64, 0, 0:cw], self.onesrow_b[0:1, 0:64],
                            self.scrow[0:1, s0 + c0:s0 + c0 + cw], start=True, stop=True),
                            waits=scw + [self._pDg, self._upsg[1], self.t_onesrow], inc=s_m7)
                        t_v = E("vector", lambda e, qb=qb, c0=c0, cw=cw: e.tensor_mul(
                            qb[:, c0:c0 + cw], qb[:, c0:c0 + cw], self.pD[0:64, 0, 0:cw]),
                            waits=[t_b], inc=s_e7)
                        self._pDg = t_v
                        pres.append(t_v)
                def mm_qmt(e, qa):
                    last = None
                    for kc in range(KD):
                        last = e.matmul(self.pD[0:8, 1, 0:64], self.muW_sb[:, kc, :], qa[:, kc, :],
                                        start=(kc == 0), stop=(kc == KD - 1))
                    return last
                t_qm = E("tensor", lambda e: mm_qmt(e, self.qaTg),
                         waits=[lds["qag"], lds["muw"], self._pD1g], inc=s_m7)
                e_qm = E("vector", lambda e: e.tensor_copy(self.qmt_sb[:, :], self.pD[0:8, 1, 0:64]),
                         waits=[t_qm], inc=s_e7)
                t_um = E("tensor", lambda e: mm_qmt(e, self.qaTu), waits=[lds["qau"], e_qm], inc=s_m7)
                e_um = E("vector", lambda e: e.tensor_copy(self.umt_sb[:, :], self.pD[0:8, 1, 0:64]),
                         waits=[t_um], inc=s_e7)
                self._pD1g = e_um
                t_wx = E("tensor", lambda e, softT_f=softT_f: (
                    e.matmul(self.pB[0:64, 0, :], self.E8_sb[:, :], softT_f[0:8, 0:512], start=True, stop=True),
                    e.matmul(self.pB[0:64, 1, :], self.E8_sb[:, :], softT_f[0:8, 512:1024], start=True, stop=True))[-1],
                    waits=gat + [t_e8, self._pBg[0], self._pBg[1],
                                 self._dpsg[0], self._dpsg[1]], inc=s_m7)
                e_wx = E("vector", lambda e: (
                    e.tensor_copy(self.wexp_sb[:, 0:512], self.pB[0:64, 0, :]),
                    e.tensor_copy(self.wexp_sb[:, 512:1024], self.pB[0:64, 1, :]))[-1],
                    waits=[t_wx, self._wexg], inc=s_e7)
                self._pBg = [e_wx, e_wx]
                def mm_a(e, qa, qmt, softT_f=softT_f):
                    last = None
                    for kc in range(KD):
                        for th in range(2):
                            last = e.matmul(self.pA[0:64, th, :], qa[:, kc, :],
                                            self.x2lnT[:, kc, th * 512:th * 512 + 512],
                                            start=(kc == 0), stop=False)
                    for th in range(2):
                        last = e.matmul(self.pA[0:64, th, :], qmt, softT_f[:, th * 512:th * 512 + 512],
                                        start=False, stop=True)
                    return last
                t_a = E("tensor", lambda e: mm_a(e, self.qaTg, self.qmt_sb[:, :]),
                        waits=[lds["qag"], e_qm, self._pAg[0], self._pAg[1]] + gat + self.t_x2lnT_done, inc=s_m7)
                e_aw = E("vector", lambda e: (
                    e.tensor_mul(self.awg_sb[:, 0:512], self.pA[0:64, 0, :], self.wexp_sb[:, 0:512]),
                    e.tensor_mul(self.awg_sb[:, 512:1024], self.pA[0:64, 1, :], self.wexp_sb[:, 512:1024]))[-1],
                    waits=[t_a, e_wx, self._awgg], inc=s_e7)
                t_a2 = E("tensor", lambda e: mm_a(e, self.qaTu, self.umt_sb[:, :]),
                         waits=[lds["qau"], e_um, e_aw], inc=s_m7)
                e_aw2 = E("vector", lambda e: (
                    e.tensor_mul(self.awu_sb[:, 0:512], self.pA[0:64, 0, :], self.wexp_sb[:, 0:512]),
                    e.tensor_mul(self.awu_sb[:, 512:1024], self.pA[0:64, 1, :], self.wexp_sb[:, 512:1024]))[-1],
                    waits=[t_a2, self._awug], inc=s_e7)
                self._pAg = [e_aw2, e_aw2]
                self._awgg = self._awug = None
                a_done = [e_aw, e_aw2]
            else:
                a_done = [self._pAg[0], self._pD1g, self._dpsg[0], self._dpsg[1]]
            nI = 22 if shared else 11
            blocks = [list(range(b, min(b + 6, nI))) for b in range(0, nI, 6)]
            nblk = len(blocks)
            for half, ics in enumerate(blocks):
                act_toks = []
                for j, ic in enumerate(ics):
                    rw = ic % 2
                    if shared:
                        gsl = self.shgT_i.rearrange("(kc kp) c -> kp kc c", kp=P)[:, :, ic * P:(ic + 1) * P]
                        usl = self.shuT_i.rearrange("(kc kp) c -> kp kc c", kp=P)[:, :, ic * P:(ic + 1) * P]
                    else:
                        gsl = self.gwT_i[f].rearrange("(kc kp) c -> kp kc c", kp=P)[:, :, ic * P:(ic + 1) * P]
                        usl = self.uwT_i[f].rearrange("(kc kp) c -> kp kc c", kp=P)[:, :, ic * P:(ic + 1) * P]
                    rg = ic % 2
                    gps = self.pA if rg == 0 else self.pC
                    ups = self.pB if rg == 0 else self.pD
                    t_lg = casted_load(self.w_ring[:, 0, :, :], gsl, [self._gwg.get(0)])
                    def mm_g(e, gps=gps, shared=shared):
                        last = None
                        for kc in range(KD):
                            for th in range(2):
                                last = e.matmul(gps[:, th, :], self.w_ring[:, 0, kc, :],
                                                self.x2lnT[:, kc, th * 512:th * 512 + 512],
                                                start=(kc == 0), stop=(shared and kc == KD - 1))
                        return last
                    t_g1 = E("tensor", mm_g,
                             waits=[t_lg, self._gpsg[rg]] + a_done + self.t_x2lnT_done, inc=s_m7)
                    toks_g = [t_g1]
                    if not shared:
                        gm_t = self.pD if rg == 0 else self.pB
                        def mm_gm(e, gm_t=gm_t):
                            last = None
                            for kc in range(KD):
                                last = e.matmul(gm_t[0:8, 1, 0:P], self.muW_sb[:, kc, :],
                                                self.w_ring[:, 0, kc, :],
                                                start=(kc == 0), stop=(kc == KD - 1))
                            return last
                        t_gm = E("tensor", mm_gm,
                                 waits=[t_lg, lds["muw"], self._pD1g, self._upsg[1 - rg]], inc=s_m7)
                        e_gm = E("vector", lambda e, gm_t=gm_t: e.tensor_copy(self.gmt_sb[:, :], gm_t[0:8, 1, 0:P]),
                                 waits=[t_gm], inc=s_e7)
                        self._pD1g = e_gm
                        self._upsg[1 - rg] = e_gm
                        def mm_g2(e, gps=gps, ic=ic, softT_f=softT_f):
                            last = None
                            for th in range(2):
                                last = e.matmul(gps[:, th, :], self.gmt_sb[:, :],
                                                softT_f[:, th * 512:th * 512 + 512],
                                                start=False, stop=False)
                            for th in range(2):
                                last = e.matmul(gps[:, th, :], self.qbg_sb[:, ic * P:(ic + 1) * P],
                                                self.awg_sb[:, th * 512:th * 512 + 512],
                                                start=False, stop=True)
                            return last
                        t_g2 = E("tensor", mm_g2, waits=[e_gm, a_done[0]] + pres, inc=s_m7)
                        toks_g.append(t_g2)
                        self._qbgg = t_g2
                    t_sil = E("scalar", lambda e, gps=gps: e.activation(
                        self.silu_sb[:, :], gps[:, :, :].rearrange("p a b -> p (a b)"),
                        AF.Silu), waits=toks_g + [self._silg], inc=s_s7)
                    self._gpsg[rg] = t_sil
                    self._gwg[0] = toks_g[-1]
                    t_lu = casted_load(self.w_ring[:, 1, :, :], usl, [self._uwg.get(0)])
                    def mm_u(e, ups=ups, shared=shared):
                        last = None
                        for kc in range(KD):
                            for th in range(2):
                                last = e.matmul(ups[:, th, :], self.w_ring[:, 1, kc, :],
                                                self.x2lnT[:, kc, th * 512:th * 512 + 512],
                                                start=(kc == 0), stop=(shared and kc == KD - 1))
                        return last
                    t_u1 = E("tensor", mm_u,
                             waits=[t_lu, self._upsg[rg], self._dpsg[0], self._dpsg[1],
                                    self._pD1g] + self.t_x2lnT_done, inc=s_m7)
                    toks_u = [t_u1]
                    if not shared:
                        def mm_um(e, gm_t=gm_t):
                            last = None
                            for kc in range(KD):
                                last = e.matmul(gm_t[0:8, 1, 0:P], self.muW_sb[:, kc, :],
                                                self.w_ring[:, 1, kc, :],
                                                start=(kc == 0), stop=(kc == KD - 1))
                            return last
                        t_um2 = E("tensor", mm_um, waits=[t_lu, lds["muw"], self._pD1g], inc=s_m7)
                        e_um2 = E("vector", lambda e, gm_t=gm_t: e.tensor_copy(self.gmt2_sb[:, :], gm_t[0:8, 1, 0:P]),
                                  waits=[t_um2], inc=s_e7)
                        self._pD1g = e_um2
                        def mm_u2(e, ups=ups, ic=ic, softT_f=softT_f):
                            last = None
                            for th in range(2):
                                last = e.matmul(ups[:, th, :], self.gmt2_sb[:, :],
                                                softT_f[:, th * 512:th * 512 + 512],
                                                start=False, stop=False)
                            for th in range(2):
                                last = e.matmul(ups[:, th, :], self.qbu_sb[:, ic * P:(ic + 1) * P],
                                                self.awu_sb[:, th * 512:th * 512 + 512],
                                                start=False, stop=True)
                            return last
                        t_u2 = E("tensor", mm_u2, waits=[a_done[1], t_u1, e_um2] + pres, inc=s_m7)
                        toks_u.append(t_u2)
                        self._qbuu = t_u2
                    t_act = E("vector", lambda e, ups=ups, j=j: e.tensor_mul(
                        self.act_sb[:, j, :], self.silu_sb[:, :],
                        ups[:, :, :].rearrange("p a b -> p (a b)")),
                        waits=toks_u + [t_sil, self._actg.get(j)], inc=s_e7)
                    self._upsg[rg] = t_act
                    self._silg = t_act
                    self._uwg[0] = toks_u[-1]
                    act_toks.append(t_act)
                    self._actg[j] = None
                nic = len(ics)
                if not shared:
                    def mm_da(e, nic=nic, ics=ics):
                        last = None
                        for j in range(nic):
                            for th in range(2):
                                last = e.matmul(self.pA[0:64, th, :], self.dqaT_sb[:, ics[j], :],
                                                self.act_sb[:, j, th * 512:th * 512 + 512],
                                                start=(j == 0), stop=(j == nic - 1))
                        return last
                    t_da = E("tensor", mm_da, waits=act_toks + [lds["dqa"], self._pAg[0], self._pAg[1]], inc=s_m7)
                    if half < nblk - 1:
                        e_da0 = E("vector", lambda e: e.tensor_copy(
                            self.da_f32[:, :], self.pA[0:64, :, :].rearrange("p a b -> p (a b)")),
                            waits=[t_da, self._da0g], inc=s_e7)
                        self._da0g = e_da0
                        self._gpsg[0] = e_da0
                        self._pAg = [e_da0, e_da0]
                        da_tok = e_da0
                    else:
                        e_daA = E("vector", lambda e: e.scalar_tensor_tensor(
                            self.da_f32[:, :], self.pA[0:64, :, :].rearrange("p a b -> p (a b)"),
                            1.0, self.da_f32[:, :], ALU.mult, ALU.add),
                            waits=[t_da, self._da0g], inc=s_e7)
                        e_daB = E("vector", lambda e: e.tensor_mul(
                            self.awd_sb[:, :], self.da_f32[:, :], self.wexp_sb[:, :]),
                            waits=[e_daA, self._awdg], inc=s_e7)
                        self._da0g = e_daB
                        self._gpsg[0] = e_daA
                        self._pAg = [e_daA, e_daA]
                        self._awdg = None
                        self._wexg = e_daB
                        da_tok = e_daB
                        self._daB = e_daB
                else:
                    da_tok = None
                t_dws = []
                for j, ic in enumerate(ics):
                    if shared:
                        dsl = self.shdT_i.rearrange("(ic ip) c -> ip ic c", ip=P)[:, ic, :]
                    else:
                        dsl = self.dwT_i[f].rearrange("(ic ip) c -> ip ic c", ip=P)[:, ic, :]
                    t_dws.append(casted_load(self.dw_sb[:, j, :], dsl,
                                             [self._dwg.get(j)] + act_toks))
                lora_now = (not shared) and (half == nblk - 1)
                first_write = (f == 0 and half == 0)
                for tt in range(8):
                    for hc in range(4):
                        rd = (tt * 4 + hc) % 2
                        def mm_d(e, tt=tt, hc=hc, rd=rd, nic=nic, lora_now=lora_now):
                            last = None
                            for j in range(nic):
                                last = e.matmul(self.pB[:, rd, :], self.act_sb[:, j, tt * P:(tt + 1) * P],
                                                self.dw_sb[:, j, hc * 512:hc * 512 + 512],
                                                start=(j == 0), stop=(not lora_now) and j == nic - 1)
                            if lora_now:
                                last = e.matmul(self.pB[:, rd, :], self.awd_sb[:, tt * P:(tt + 1) * P],
                                                self.qbd_sb[:, hc * 512:hc * 512 + 512],
                                                start=False, stop=True)
                            return last
                        wl = t_dws + act_toks + [self._dpsg[rd]]
                        if lora_now:
                            wl += [da_tok, lds["qbd"]] + pres
                        t_d = E("tensor", mm_d, waits=wl, inc=s_m7)
                        if first_write:
                            e_y = E("vector", lambda e, tt=tt, hc=hc, rd=rd, f=f: e.tensor_scalar(
                                self.y_sb[:, tt, hc * 512:hc * 512 + 512], self.pB[:, rd, :],
                                self.scal_sb[:, tt, f:f + 1], None, ALU.mult),
                                waits=[t_d, self._yg.get((tt, hc))], inc=s_e7)
                        elif not shared:
                            e_y = E("vector", lambda e, tt=tt, hc=hc, rd=rd, f=f: e.scalar_tensor_tensor(
                                self.y_sb[:, tt, hc * 512:hc * 512 + 512], self.pB[:, rd, :],
                                self.scal_sb[:, tt, f:f + 1],
                                self.y_sb[:, tt, hc * 512:hc * 512 + 512], ALU.mult, ALU.add),
                                waits=[t_d, self._yg.get((tt, hc))], inc=s_e7)
                        else:
                            e_y = E("vector", lambda e, tt=tt, hc=hc, rd=rd: e.tensor_add(
                                self.y_sb[:, tt, hc * 512:hc * 512 + 512],
                                self.y_sb[:, tt, hc * 512:hc * 512 + 512], self.pB[:, rd, :]),
                                waits=[t_d, self._yg.get((tt, hc))], inc=s_e7)
                        self._dpsg[rd] = e_y
                        self._yg[(tt, hc)] = e_y
                for j in range(len(ics)):
                    self._dwg[j] = self._dpsg[0]
            if not shared:
                self._ldg = {
                    "qag": t_a, "qau": t_a2, "muw": self._pD1g,
                    "dqa": self._daB, "qbg": self._qbgg, "qbu": self._qbuu,
                    "qbd": e_y, "gsc": pres[-1],
                }
        self.t_y_done = [self._yg[(tt, hc)] for tt in range(8) for hc in range(4)]

    def phase8_out(self):
        self.ph("p8")
        E = self.E
        self.phase_barrier()
        s_e8 = self.sem("s_e8")
        s_s8 = self.sem("s_s8")
        s_x8 = self.sem("s_x8")
        s_o8 = self.sem("s_o8")
        self.ctx_p8sb = ExitStack()
        sb8 = lambda n, s, d=F32: self.ctx_p8sb.enter_context(self.nc.sbuf_tensor(n, s, d, side="right"))
        self.x8_sb = sb8("x8_sb", [P, 2, D])
        self.o8_sb = sb8("o8_sb", [P, 2, D])
        self.st8 = sb8("st8", [P, 12])
        bb_eps, tok_eps = self.ccol(EPS_LN)
        xg = {0: None, 1: None}
        og = {0: None, 1: None}
        st_ = self.st8
        for tt in range(8):
            rr = tt % 2
            c0 = rr * 6
            yt = [self._yg[(tt, hc)] for hc in range(4)]
            t_x = E("sync", lambda e, tt=tt, rr=rr: e.dma_start(
                out=self.x8_sb[:, rr, :], in_=self.x2_hbm[tt * P:(tt + 1) * P, :]),
                waits=[self.t_x2_hbm_done, xg[rr]], inc=s_x8, inc_amt=16, dma=True)
            t_s1 = E("scalar", lambda e, tt=tt, c0=c0: e.activation(
                self.o8_sb[:, 0, :], self.y_sb[:, tt, :], AF.Identity,
                accum_out=st_[:, c0:c0 + 1]), waits=yt + [og[0]], inc=s_s8)
            t_s2 = E("scalar", lambda e, tt=tt, c0=c0: e.activation(
                self.o8_sb[:, 0, :], self.y_sb[:, tt, :], AF.Square,
                accum_out=st_[:, c0 + 1:c0 + 2]), waits=[t_s1], inc=s_s8)
            t_mu = E("vector", lambda e, c0=c0: e.tensor_scalar(
                st_[:, c0 + 2:c0 + 3], st_[:, c0:c0 + 1], 1.0 / D, None, ALU.mult),
                waits=[t_s1], inc=s_e8)
            t_msq = E("vector", lambda e, c0=c0: e.tensor_tensor(
                st_[:, c0 + 4:c0 + 5], st_[:, c0 + 2:c0 + 3], st_[:, c0 + 2:c0 + 3], ALU.mult),
                waits=[t_mu], inc=s_e8)
            t_var = E("vector", lambda e, c0=c0: e.scalar_tensor_tensor(
                st_[:, c0 + 3:c0 + 4], st_[:, c0 + 1:c0 + 2], 1.0 / D,
                st_[:, c0 + 4:c0 + 5], ALU.mult, ALU.subtract),
                waits=[t_s2, t_msq], inc=s_e8)
            t_lnv = E("scalar", lambda e, c0=c0, bb=bb_eps: e.activation(
                st_[:, c0 + 4:c0 + 5], st_[:, c0 + 3:c0 + 4], AF.Ln, bias=bb),
                waits=[t_var, tok_eps], inc=s_s8)
            t_rstd = E("scalar", lambda e, c0=c0: e.activation(
                st_[:, c0 + 3:c0 + 4], st_[:, c0 + 4:c0 + 5], AF.Exp, scale=-0.5),
                waits=[t_lnv], inc=s_s8)
            t_n1 = E("vector", lambda e, tt=tt, rr=rr, c0=c0: e.tensor_scalar(
                self.o8_sb[:, rr, :], self.y_sb[:, tt, :],
                st_[:, c0 + 2:c0 + 3], st_[:, c0 + 3:c0 + 4],
                ALU.subtract, ALU.mult), waits=[t_rstd, og[rr]], inc=s_e8)
            t_n2 = E("vector", lambda e, rr=rr: e.tensor_add(
                self.o8_sb[:, rr, :], self.o8_sb[:, rr, :], self.x8_sb[:, rr, :]),
                waits=[t_n1, t_x], inc=s_e8)
            xg[rr] = t_n2
            t_o = E("sync", lambda e, tt=tt, rr=rr: e.dma_start(
                out=self.out_hbm[tt * P:(tt + 1) * P, :], in_=self.o8_sb[:, rr, :]),
                waits=[t_n2], inc=s_o8, inc_amt=16, dma=True)
            og[rr] = t_o
        self.t_out_done = (s_o8.h, s_o8.n)

    def finalize(self, wait_all=True):
        if "mixed" in self.probes:
            self.dump("mixed", self.mixed_hbm)
        if "zout" in self.probes:
            self.dump("zout", self.z_hbm)
        sems = [s for s in self.sems.values() if s.n > 0 and s.name != "s_serial"]

        def fin(e):
            for s in sems:
                e.wait_ge(s.h, s.n)
            return e.nop()
        self.prog.emit("sync", fin)

    def compile(self):
        prog = self.prog
        with self.nc.Block() as block:
            @block.sync
            def _(e):
                for op in prog.ops["sync"]:
                    op(e)

            @block.gpsimd
            def _(e):
                for op in prog.ops["gpsimd"]:
                    op(e)

            @block.tensor
            def _(e):
                for op in prog.ops["tensor"]:
                    op(e)

            @block.vector
            def _(e):
                for op in prog.ops["vector"]:
                    op(e)

            @block.scalar
            def _(e):
                for op in prog.ops["scalar"]:
                    op(e)
        self.ctx.close()
        return self.nc


def build_all(probes=(), serial=False, serial_phases=None):
    b = Builder(probes, serial, serial_phases)
    b.declare()
    b.probe("mixed", [64, P, NTOK], BF16)
    b.probe("zout", [32, P, NTOK], BF16)
    b.probe("sheets", [8, 32, NTOK], F32)
    b.probe("hg", [NCH, 32, P, P], BF16)
    b.probe("prep", [10, P, P], BF16)
    b.probe("x2", [NTOK, D], F32)
    b.alloc_front()
    b.phase35_handshake_init()
    b.phase01_ln1()
    b.phase2_qkvz()
    b.phase3_sheets()
    b.phase4_core()
    b.phase6_wout()
    if "x2" in b.probes:
        b.E("sync", lambda e: e.dma_start(out=b.probe_aps["x2"], in_=b.x2_hbm),
            waits=[b.t_x2_hbm_done], inc=b.sem("s_prx"), inc_amt=16, dma=True)
    b.phase7_ffn()
    b.phase7_experts()
    b.phase8_out()
    b.finalize()
    nc = b.compile()
    return nc, b.probe_aps


from concourse.bass_utils import run_bass_kernel_spmd

VHP = VH


def _tri_masks():
    NEG = np.float32(-1e30)
    c = np.arange(P)
    m_attnT = np.where(c[None, :] >= c[:, None], np.float32(0), NEG).astype(np.float32)
    m_A = np.where(c[:, None] > c[None, :], np.float32(0), NEG).astype(np.float32)
    return m_attnT, m_A, np.ascontiguousarray(m_A.T)


def _prep_core_inputs(c, full, shared_cache):
    b, hh = c // 2, c % 2
    xb = np.asarray(full["L0_attn_hidden_states"][b], np.float32)
    if hh == 0:
        halo = np.zeros((P, D), np.float32)
        own = xb[0:NTOK]
    else:
        halo = xb[NTOK - P:NTOK]
        own = xb[NTOK:2 * NTOK]
    x_strip = np.ascontiguousarray(np.concatenate([halo, own], axis=0))
    in_map = dict(shared_cache)
    in_map["x_strip"] = x_strip
    in_map["par_i"] = np.array([[hh]], np.int32)
    in_map["parf_col_i"] = np.full((P, 1), float(hh), np.float32)
    in_map["ptgt_i"] = np.array([[0 if hh == 0 else SENT]], np.int32)
    return in_map


def _prep_shared(full):
    f32 = lambda k: np.asarray(full[k], np.float32)
    Wq = f32("L0_attn_W_qkvz")
    Wba = f32("L0_attn_W_ba")
    conv_w = f32("L0_attn_conv_w")
    perm = np.array(VHP, np.int64)
    wba_perm = np.concatenate([Wba[perm], Wba[32 + perm]], axis=0)
    m_attnT, m_A, m_AT = _tri_masks()
    gW, uW, dW = f32("L0_ffn_gate_W"), f32("L0_ffn_up_W"), f32("L0_ffn_down_W")
    gqa, uqa, dqa = f32("L0_ffn_gate_qa"), f32("L0_ffn_up_qa"), f32("L0_ffn_down_qa")
    gqb, uqb, dqb = f32("L0_ffn_gate_qb"), f32("L0_ffn_up_qb"), f32("L0_ffn_down_qb")
    ac = np.ascontiguousarray
    return {
        "wqkvzT": ac(Wq.T),
        "wbaT": ac(wba_perm.T),
        "conv_wi": ac(conv_w[:, 0, :]),
        "dt_col": ac(f32("L0_attn_dt_bias")[perm][:, None]),
        "alog_col": ac(f32("L0_attn_A_log")[perm][:, None]),
        "eye_i": np.eye(P, dtype=np.float32),
        "mtri_attnT": m_attnT, "mtri_A": m_A, "mtri_AT": m_AT,
        "nw_col": ac(f32("L0_attn_norm_weight")[:, None]),
        "woutT": ac(f32("L0_attn_W_out").T),
        "sent_i": np.array([[SENT]], np.int32),
        "gwgT": ac(f32("L0_ffn_gate_weight").T),
        "inv_map": ac(np.asarray(full["L0_ffn_inv_mapping"], np.int32).reshape(1, 64)),
        "muW": ac(f32("L0_ffn_mask_up_W")),
        "gwT": ac(np.transpose(gW, (0, 2, 1))),
        "uwT": ac(np.transpose(uW, (0, 2, 1))),
        "dwT": ac(np.transpose(dW, (0, 2, 1))),
        "gqaT": ac(np.transpose(gqa.reshape(8, 64, D), (0, 2, 1))),
        "uqaT": ac(np.transpose(uqa.reshape(8, 64, D), (0, 2, 1))),
        "dqaT": ac(np.transpose(dqa.reshape(8, 64, 1408), (0, 2, 1))),
        "gqbF": ac(np.transpose(gqb, (0, 1, 3, 2)).reshape(8, 64, 1408)),
        "uqbF": ac(np.transpose(uqb, (0, 1, 3, 2)).reshape(8, 64, 1408)),
        "dqbF": ac(np.transpose(dqb, (0, 1, 3, 2)).reshape(8, 64, D)),
        "gsc": ac(f32("L0_ffn_gate_scale")[:, None, :]),
        "usc": ac(f32("L0_ffn_up_scale")[:, None, :]),
        "dsc": ac(f32("L0_ffn_down_scale")[:, None, :]),
        "shgT": ac(f32("L0_ffn_sh_gate_W").T),
        "shuT": ac(f32("L0_ffn_sh_up_W").T),
        "shdT": ac(f32("L0_ffn_sh_down_W").T),
        "E8_i": np.repeat(np.eye(8, dtype=np.float32), 8, axis=1),
    }


SERIAL = False


def kernel(**inputs):
    nc, _ = build_all(probes=(), serial=SERIAL)
    shared = _prep_shared(inputs)
    in_maps = [_prep_core_inputs(c, inputs, shared) for c in range(8)]
    res = run_bass_kernel_spmd(nc, in_maps, core_ids=list(range(8)))
    out = np.empty((4, 2048, D), np.float32)
    for c in range(8):
        b, hh = c // 2, c % 2
        out[b, hh * NTOK:(hh + 1) * NTOK, :] = res.results[c]["out"]
    return out


# version 08
# speedup vs baseline: 1.0927x; 1.0927x over previous
import math
from contextlib import ExitStack

import numpy as np

import concourse.bass as bass
import concourse.mybir as mybir
from concourse.expressions import RuntimeValue

F32 = mybir.dt.float32
BF16 = mybir.dt.bfloat16
I32 = mybir.dt.int32
AF = mybir.ActivationFunctionType
ALU = mybir.AluOpType
ds = bass.ds
AXX = mybir.AxisListType.X

P = 128
D = 2048
KD = 16
NTOK = 1024
STRIP = 1152
NT = STRIP // P
NCH = 8
HK, HV = 16, 32
DK = DV = 128
SENT = 1923491837
LNDK = float(np.log(np.float64(128.0)))
EPS_L2 = 1e-6
EPS_LN = 1e-5
EPS_RMS = 1e-6

VH = [2 * (r % 16) + (r // 16) for r in range(32)]
USE_EEU = False
QKV_SLABS = [(0, 512), (512, 512), (1024, 128)]
OWN_SLABS = [(128, 512), (640, 512)]


class Sem:
    def __init__(self, ctx, nc, name):
        self.h = ctx.enter_context(nc.semaphore(name))
        self.n = 0
        self.name = name


class Prog:
    ENGINES = ("sync", "gpsimd", "tensor", "vector", "scalar")

    def __init__(self, serial_sem=None):
        self.ops = {e: [] for e in self.ENGINES}
        self.last_wait = {}
        self.serial = serial_sem
        self.serial_on = serial_sem is not None
        self.pending_tails = {}
        self.rec = []

    def emit(self, eng, fn, waits=(), inc=None, inc_amt=1, dma=False):
        toks = [w for w in list(waits) + self.pending_tails.pop(eng, []) if w is not None]
        ser = self.serial if self.serial_on else None
        ser_wait = ser.n if ser is not None else None

        if ser is not None and dma:
            assert inc is not None, "serial mode requires an inc sem on DMA ops"
        inc_val = (inc.n + inc_amt) if inc is not None else None
        self.rec.append((eng, [(id(s), v) for (s, v) in toks],
                         id(inc.h) if inc is not None else None, inc_amt,
                         (ser.n if ser is not None else None)))

        def op(e):
            if ser is not None:
                e.wait_ge(ser.h, ser_wait)
            for (s, v) in toks:
                key = (eng, id(s))
                if self.last_wait.get(key, -1) < v:
                    e.wait_ge(s, v)
                    self.last_wait[key] = v
            last = fn(e)
            if inc is not None:
                last = last.then_inc(inc.h, inc_amt)
            if ser is not None:
                if dma:
                    e.wait_ge(inc.h, inc_val)
                e.nop().then_inc(ser.h, 1)

        self.ops[eng].append(op)
        if ser is not None:
            ser.n += 1
        if inc is not None:
            inc.n += inc_amt
            return (inc.h, inc.n)
        return None


class Builder:
    def __init__(self, probes=(), serial=False, serial_phases=None):
        self.nc = bass.Bass("TRN2", target_bir_lowering=False)
        self.ctx = ExitStack()
        self.probes = probes
        self.probe_aps = {}
        self.sems = {}
        use_ser = serial or serial_phases
        self._ser = Sem(self.ctx, self.nc, "s_serial") if use_ser else None
        self.prog = Prog(self._ser)
        self.serial_phases = set(serial_phases or ([
            "hs", "p1", "p2", "p3", "p4", "p4l", "p4l2", "p4l3", "p4p",
            "p4p1", "p4p2", "p4p3",
            "p4s", "p4h", "p6", "p7a", "p7b", "p8"] if serial else []))

    def ph(self, name):
        self.prog.serial_on = (self._ser is not None) and (name in self.serial_phases)

    def phase_barrier(self):
        tails = [(s.h, s.n) for s in self.sems.values() if s.n > 0]
        self.prog.pending_tails = {e: list(tails) for e in Prog.ENGINES}

    def dt(self, name, shape, dtype=F32, kind="ExternalInput", **kw):
        return self.nc.dram_tensor(name, shape, dtype, kind=kind, **kw).ap()

    def sb(self, name, shape, dtype=F32):
        return self.ctx.enter_context(self.nc.sbuf_tensor(name, shape, dtype))

    def psum(self, name, shape, dtype=F32, scope=None):
        st = scope if scope is not None else self.ctx
        return st.enter_context(self.nc.psum_tensor(name, shape, dtype))

    def sem(self, name):
        s = Sem(self.ctx, self.nc, name)
        self.sems[name] = s
        return s

    def probe(self, name, shape, dtype=F32):
        if name in self.probes:
            ap = self.dt("probe_" + name, shape, dtype, kind="ExternalOutput")
            self.probe_aps[name] = ap
            return ap
        return None

    def E(self, eng, fn, waits=(), inc=None, inc_amt=1, dma=False):
        return self.prog.emit(eng, fn, waits, inc, inc_amt, dma)

    def ccol(self, value, rows=P):
        if value not in self._const_cols:
            j = len(self._const_cols)
            assert j < 8
            if "s_cc" not in self.sems:
                self.sem("s_cc")
            tok = self.E("vector", lambda e, j=j, value=value: e.memset(
                self.const_sb[:, j:j + 1], float(value)), inc=self.sems["s_cc"])
            self._const_cols[value] = (j, tok)
        j, tok = self._const_cols[value]
        return self.const_sb[0:rows, j:j + 1], tok

    def dump(self, name, src_hbm):
        ap = self.probe_aps.get(name)
        if ap is None:
            return None
        if "s_probe" not in self.sems:
            self.sems["s_probe"] = Sem(self.ctx, self.nc, "s_probe")
        s_pr = self.sems["s_probe"]
        return self.E("sync", lambda e: e.dma_start(out=ap, in_=src_hbm),
                      inc=s_pr, inc_amt=16, dma=True)

    def declare(self):
        dt = self.dt
        self.x_strip = dt("x_strip", [STRIP, D])
        self.wqkvzT = dt("wqkvzT", [D, 12288])
        self.wbaT = dt("wbaT", [D, 64])
        self.conv_wi = dt("conv_wi", [8192, 4])
        self.dt_col_i = dt("dt_col", [32, 1])
        self.alog_col_i = dt("alog_col", [32, 1])
        self.eye_i = dt("eye_i", [P, P])
        self.mtri_attnT_i = dt("mtri_attnT", [P, P])
        self.mtri_A_i = dt("mtri_A", [P, P])
        self.mtri_AT_i = dt("mtri_AT", [P, P])
        self.nw_col_i = dt("nw_col", [P, 1])
        self.woutT_i = dt("woutT", [4096, D])
        self.par_ii = dt("par_i", [1, 1], I32)
        self.sent_ii = dt("sent_i", [1, 1], I32)
        self.gwgT = dt("gwgT", [D, 64])
        self.inv_map = dt("inv_map", [1, 64], I32)
        self.muW_i = dt("muW", [8, D, 8])
        self.gwT_i = dt("gwT", [8, D, 1408])
        self.uwT_i = dt("uwT", [8, D, 1408])
        self.dwT_i = dt("dwT", [8, 1408, D])
        self.gqaT_i = dt("gqaT", [8, D, 64])
        self.uqaT_i = dt("uqaT", [8, D, 64])
        self.dqaT_i = dt("dqaT", [8, 1408, 64])
        self.gqbF_i = dt("gqbF", [8, 64, 1408])
        self.uqbF_i = dt("uqbF", [8, 64, 1408])
        self.dqbF_i = dt("dqbF", [8, 64, D])
        self.gsc_i = dt("gsc", [8, 1, 1408])
        self.usc_i = dt("usc", [8, 1, 1408])
        self.dsc_i = dt("dsc", [8, 1, D])
        self.shgT_i = dt("shgT", [D, 2816])
        self.shuT_i = dt("shuT", [D, 2816])
        self.shdT_i = dt("shdT", [2816, D])
        self.E8_i = dt("E8_i", [8, 64])
        self.out_hbm = dt("out", [NTOK, D], kind="ExternalOutput")
        self.mixed_hbm = dt("mixed_hbm", [64, P, NTOK], BF16, kind="Internal")
        self.z_hbm = dt("z_hbm", [32, P, NTOK], BF16, kind="Internal")
        self.hg_hbm = dt("hg_hbm", [NCH, 32, P, P], BF16, kind="Internal")
        self.x2_hbm = dt("x2_hbm", [NTOK, D], F32, kind="Internal")
        self.sheets_hbm = dt("sheets_hbm", [8, 32, NTOK], F32, kind="Internal")
        self.st_sh = dt("st_sh", [2, 32, P, P], F32, kind="Internal", addr_space="Shared")
        self.fl_sh = dt("fl_sh", [2, 1, 1], I32, kind="Internal", addr_space="Shared")

    def sbf(self, name, shape, dtype=F32):
        return self.ctx_front.enter_context(self.nc.sbuf_tensor(name, shape, dtype, side="right"))

    def sbm(self, name, shape, dtype=F32):
        return self.ctx_mid.enter_context(self.nc.sbuf_tensor(name, shape, dtype, side="right"))

    def sb4(self, name, shape, dtype=F32):
        return self.ctx_p4sb.enter_context(self.nc.sbuf_tensor(name, shape, dtype, side="right"))

    def alloc_front(self):
        self.ctx_mid = ExitStack()
        self.ctx_front = ExitStack()
        sb, psum = self.sb, self.psum
        self.ones_col = sb("ones_col", [P, 1], BF16)
        self.const_sb = sb("const_sb", [P, 8])
        self._const_cols = {}
        self.sh_gg = self.sbm("sh_gg", [32, NTOK])
        self.sh_gc = self.sbm("sh_gc", [32, NTOK])
        self.sh_sps = self.sbm("sh_sps", [32, NTOK])
        self.sh_lnq = self.sbm("sh_lnq", [32, NTOK])
        self.sh_lnk = self.sbm("sh_lnk", [32, NTOK])
        self.sh_u = self.sbm("sh_u", [32, NTOK])
        self.sh_w = self.sbm("sh_w", [32, NTOK])
        self.sh_p = self.sbm("sh_p", [32, NTOK])
        self.sh_be = self.sbm("sh_be", [32, NTOK])
        self.sh_eg = self.sbm("sh_eg", [32, NTOK])
        self.sh_bet = self.sbm("sh_bet", [32, NTOK])
        self.sh_eu = self.sbm("sh_eu", [32, NTOK])
        self.gam_sb = self.sbm("gam_sb", [32, NCH])
        self.gamrow = self.sbm("gamrow", [1, 32 * NCH])
        self.state_f = self.sbm("state_f", [P, 32, P])
        self.state_b = self.sbm("state_b", [P, 32, P], BF16)
        self.par_sb = self.sbm("par_sb", [1, 1], I32)
        self.ptgt_sb = self.sbm("ptgt_sb", [1, 1], I32)
        self.parf_col = self.sbm("parf_col", [P, 1])
        self.flag_sb = self.sbm("flag_sb", [1, 1], I32)
        self.sent_sb = self.sbm("sent_sb", [1, 1], I32)
        self.zero_i32 = self.sbm("zero_i32", [1, 1], I32)
        self.x_sb = self.sbf("x_sb", [P, 2, D])
        self.stat_sb = sb("stat_sb", [P, 10])
        self.ln_tok = self.sbf("ln_tok", [P, 2, D], BF16)
        self.ln1T = self.sbf("ln1T", [P, KD, STRIP], BF16)
        self.wq_sb = self.sbf("wq_sb", [P, 2, KD, P], BF16)
        self.strip_sb = self.sbf("strip_sb", [P, 2, STRIP], BF16)
        self.conv_acc = self.sbf("conv_acc", [P, 2, NTOK], BF16)
        self.mixed_sb = self.sbf("mixed_sb", [P, 2, NTOK], BF16)
        self.sq_sb = self.sbf("sq_sb", [P, 2, NTOK], BF16)
        self.scr32 = self.sbf("scr32", [32, 2, NTOK])
        self.convw_sb = self.sbf("convw_sb", [P, 64, 4])
        self.wba_sb = self.sbf("wba_sb", [P, KD, 64], BF16)
        self.ba_b = self.sbf("ba_b", [32, NTOK])
        self.ba_a = self.sbf("ba_a", [32, NTOK])
        self.rowbuf = self.sbf("rowbuf", [1, 2, NTOK])
        self.eye_sb = sb("eye_sb", [P, P], BF16)
        self.mtA_sb = sb("mtA_sb", [P, P])
        self.mtAT_sb = sb("mtAT_sb", [P, P])
        self.mtQ_sb = sb("mtQ_sb", [P, P])
        self.dt_col = sb("dt_col_sb", [32, 1])
        self.alog_col = sb("alog_col_sb", [32, 1])
        self.negea_col = sb("negea_col", [32, 1])

    def phase01_ln1(self):
        self.ph("p1")
        E = self.E
        self._kkguard = None
        self._qkguard = None
        self._ppg = None
        self._pbg = None
        self._pgg = None
        self._vng = None
        self._kbeg = None
        self._vbg = None
        self._mskg = None
        self._ktguard = [None, None]
        self.eyef = self.sb("eyef", [P, P])
        self.t_eyef = self.E("sync", lambda e: e.dma_start(out=self.eyef[:, :], in_=self.eye_i),
                             inc=self.sem("s_eyf"), inc_amt=16, dma=True)
        self._rsqg = None
        self._rsqfg = None
        self._x6g = {0: None, 1: None}
        self._t6g = [None, None]
        self._pAg = [None, None]
        self._pBg = [None, None]
        self._pCg = [None, None]
        self._pDtg = [None, None]
        self._pDg = None
        self._pD1g = None
        self._wexg = None
        self._awgg = None
        self._awug = None
        self._awdg = None
        self._silg = None
        self._da0g = None
        self._gwg = {}
        self._uwg = {}
        self._dwg = {}
        self._ldg = {}
        self._gpsg = [None, None]
        self._upsg = [None, None]
        self._dpsg = [None, None]
        self._actg = {}
        self.onesrow_f = self.sb("onesrow_f", [1, P])
        self.onesrow_b = self.sb("onesrow_b", [1, P], BF16)
        self.nw_sb = self.sb("nw_sb", [P, 1])
        self.t_onesrow = self.E("vector", lambda e: (
            e.memset(self.onesrow_f[:, :], 1.0), e.memset(self.onesrow_b[:, :], 1.0))[-1],
            inc=self.sem("s_or"))
        self.t_nw = self.E("sync", lambda e: e.dma_start(out=self.nw_sb[:, :], in_=self.nw_col_i),
                           inc=self.sem("s_nw"), inc_amt=16, dma=True)
        self.ctx_p1 = ExitStack()
        self.ps_t = [self.psum("ps_t0", [P, P], BF16, self.ctx_p1),
                     self.psum("ps_t1", [P, P], BF16, self.ctx_p1)]
        s_x = self.sem("s_x")
        s_c = self.sem("s_c")
        s_st = self.sem("s_st")
        s_v = self.sem("s_v")
        s_tp = self.sem("s_tp")
        s_te = self.sem("s_te")
        tc = []
        tc.append(E("sync", lambda e: e.dma_start(
            out=self.convw_sb[:, :, :], in_=self.conv_wi.rearrange("(c p) t -> p c t", p=P)),
            inc=s_c, inc_amt=16, dma=True))
        tc.append(E("gpsimd", lambda e: e.dma_start(
            out=self.wba_sb[:, :, :],
            in_=self.wbaT.rearrange("(kc kp) c -> kp kc c", kp=P)),
            inc=s_c, inc_amt=16, dma=True))
        tc.append(E("gpsimd", lambda e: e.dma_start(out=self.eye_sb[:, :], in_=self.eye_i),
                    inc=s_c, inc_amt=16, dma=True))
        tc.append(E("sync", lambda e: e.dma_start(out=self.mtA_sb[:, :], in_=self.mtri_A_i),
                    inc=s_c, inc_amt=16, dma=True))
        tc.append(E("sync", lambda e: e.dma_start(out=self.mtAT_sb[:, :], in_=self.mtri_AT_i),
                    inc=s_c, inc_amt=16, dma=True))
        tc.append(E("sync", lambda e: e.dma_start(out=self.mtQ_sb[:, :], in_=self.mtri_attnT_i),
                    inc=s_c, inc_amt=16, dma=True))
        tc.append(E("sync", lambda e: e.dma_start(out=self.dt_col[:, :], in_=self.dt_col_i),
                    inc=s_c, inc_amt=16, dma=True))
        tc.append(E("sync", lambda e: e.dma_start(out=self.alog_col[:, :], in_=self.alog_col_i),
                    inc=s_c, inc_amt=16, dma=True))
        self.t_consts = tc
        self.t_ones = E("vector", lambda e: e.memset(self.ones_col[:, :], 1.0), inc=self.sem("s_ones"))
        t_eye = tc[2]

        t_evs = []
        self._lnguard = {}
        for t in range(NT):
            r = t % 2
            lng0 = self._lnguard.get(r)
            lng = lng0
            t_x = E("sync", lambda e, t=t, r=r: e.dma_start(
                out=self.x_sb[:, r, :], in_=self.x_strip[t * P:(t + 1) * P, :]),
                waits=[self._xguard[r] if hasattr(self, "_xguard") and r in self._xguard else None],
                inc=s_x, inc_amt=16, dma=True)
            c0 = r * 5
            st_ = self.stat_sb
            t_s1 = E("scalar", lambda e, r=r, c0=c0: e.activation(
                self.ln_tok[:, r, :], self.x_sb[:, r, :], AF.Identity,
                accum_out=st_[:, c0:c0 + 1]), waits=[t_x, lng0], inc=s_st)
            t_s2 = E("scalar", lambda e, r=r, c0=c0: e.activation(
                self.ln_tok[:, r, :], self.x_sb[:, r, :], AF.Square,
                accum_out=st_[:, c0 + 1:c0 + 2]), waits=[t_s1], inc=s_st)
            t_mu = E("vector", lambda e, c0=c0: e.tensor_scalar(
                st_[:, c0 + 2:c0 + 3], st_[:, c0:c0 + 1], 1.0 / D, None, ALU.mult),
                waits=[t_s1], inc=s_v)
            t_msq = E("vector", lambda e, c0=c0: e.tensor_tensor(
                st_[:, c0 + 4:c0 + 5], st_[:, c0 + 2:c0 + 3], st_[:, c0 + 2:c0 + 3], ALU.mult),
                waits=[t_mu], inc=s_v)
            t_var = E("vector", lambda e, c0=c0: e.scalar_tensor_tensor(
                st_[:, c0 + 3:c0 + 4], st_[:, c0 + 1:c0 + 2], 1.0 / D,
                st_[:, c0 + 4:c0 + 5], ALU.mult, ALU.subtract),
                waits=[t_s2, t_msq], inc=s_v)
            bb_eps, tok_eps = self.ccol(EPS_LN)
            t_lnv = E("scalar", lambda e, c0=c0, bb=bb_eps: e.activation(
                st_[:, c0 + 4:c0 + 5], st_[:, c0 + 3:c0 + 4], AF.Ln, bias=bb),
                waits=[t_var, tok_eps], inc=s_st)
            t_rstd = E("scalar", lambda e, c0=c0: e.activation(
                st_[:, c0 + 3:c0 + 4], st_[:, c0 + 4:c0 + 5], AF.Exp, scale=-0.5),
                waits=[t_lnv], inc=s_st)
            t_nrm = E("vector", lambda e, r=r, c0=c0: e.tensor_scalar(
                self.ln_tok[:, r, :], self.x_sb[:, r, :],
                st_[:, c0 + 2:c0 + 3], st_[:, c0 + 3:c0 + 4],
                ALU.subtract, ALU.mult), waits=[t_rstd], inc=s_v)
            if not hasattr(self, "_xguard"):
                self._xguard = {}
            self._xguard[r] = t_nrm
            for kc in range(KD):
                i = t * KD + kc
                pb = i % 2
                guard = t_evs[i - 2] if i >= 2 else None
                t_tp = E("tensor", lambda e, kc=kc, r=r, pb=pb: e.transpose(
                    self.ps_t[pb][:, :], self.ln_tok[:, r, kc * P:(kc + 1) * P],
                    self.eye_sb[:, :]),
                    waits=[t_nrm, t_eye, guard], inc=s_tp)
                eng = "vector" if (i % 2 == 0) else "scalar"
                if eng == "vector":
                    t_te = E("vector", lambda e, kc=kc, t=t, pb=pb: e.tensor_copy(
                        self.ln1T[:, kc, t * P:(t + 1) * P], self.ps_t[pb][:, :]),
                        waits=[t_tp], inc=s_te)
                else:
                    t_te = E("scalar", lambda e, kc=kc, t=t, pb=pb: e.copy(
                        self.ln1T[:, kc, t * P:(t + 1) * P], self.ps_t[pb][:, :]),
                        waits=[t_tp], inc=s_te)
                t_evs.append(t_te)
                last_tp = t_tp
            self._lnguard[r] = last_tp
        self.t_ln1T_done = (s_te.h, s_te.n)

    def phase2_qkvz(self):
        self.ph("p2")
        E = self.E
        self.ctx_p1.close()
        self.ctx_p2 = ExitStack()
        self.ps_q = [self.psum("ps_qa", [P, 2, 512], F32, self.ctx_p2),
                     self.psum("ps_qb", [P, 2, 512], F32, self.ctx_p2)]
        self.ps_q3 = [self.psum("ps_q3a", [P, P], F32, self.ctx_p2),
                      self.psum("ps_q3b", [P, P], F32, self.ctx_p2)]
        self.ps_nrm = self.psum("ps_nrm", [1, NTOK], F32, self.ctx_p2)
        s_w = self.sem("s_w")
        s_qm = self.sem("s_qm")
        s_qe = self.sem("s_qe")
        s_cv = self.sem("s_cv")
        s_mx = self.sem("s_mx")
        s_sq = self.sem("s_sq")
        s_nm = self.sem("s_nm")
        s_nr = self.sem("s_nr")
        s_nd = self.sem("s_nd")
        s_mh = self.sem("s_mh")
        s_ba = self.sem("s_ba")
        wq = self.wq_sb
        ln1T_done = self.t_ln1T_done
        t_ones_tok = self.t_ones
        prev_evac = [None, None]
        prev_w_consumer = [None, None]
        strip_guard = [None, None]
        conv_guard = [None, None]
        sq_guard = [None, None]
        mix_guard = [None, None]
        row_guard = [None, None]

        def w_ap(ct):
            return self.wqkvzT.rearrange("(kc kp) c -> kp kc c", kp=P)[:, :, ct * P:(ct + 1) * P]

        for ct in range(97):
            r = ct % 2
            is_ba = (ct == 96)
            is_z = 64 <= ct < 96
            is_qk = ct < 32
            slabs = QKV_SLABS if ct < 64 else OWN_SLABS
            if not is_ba:
                t_w = E("gpsimd", lambda e, ct=ct, r=r: e.dma_start(out=wq[:, r, :, :], in_=w_ap(ct)),
                        waits=[prev_w_consumer[r]], inc=s_w, inc_amt=16, dma=True)
            else:
                t_w = self.t_consts[1]

            def mm_tile(e, ct=ct, r=r, slabs=slabs, is_ba=is_ba):
                last = None
                for kc in range(KD):
                    for si, (s0, sw) in enumerate(slabs):
                        if is_ba:
                            lhs = self.wba_sb[:, kc, :]
                        else:
                            lhs = wq[:, r, kc, :]
                        if sw == 512:
                            out_ap = self.ps_q[r][0:64, si, :] if is_ba else self.ps_q[r][:, si, :]
                        else:
                            out_ap = self.ps_q3[r][0:64, :] if is_ba else self.ps_q3[r][:, :]
                        last = e.matmul(out_ap, lhs, self.ln1T[:, kc, s0:s0 + sw],
                                        start=(kc == 0), stop=(kc == KD - 1))
                return last
            t_mm = E("tensor", mm_tile, waits=[t_w, ln1T_done, prev_evac[r]], inc=s_qm)
            prev_w_consumer[r] = t_mm

            if is_ba:
                t_es = []
                for si in range(2):
                    t_es.append(E("vector", lambda e, r=r, si=si: e.tensor_copy(
                        self.ba_b[:, si * 512:si * 512 + 512], self.ps_q[r][0:32, si, :]),
                        waits=[t_mm], inc=s_ba))
                    t_es.append(E("vector", lambda e, r=r, si=si: e.tensor_copy(
                        self.ba_a[:, si * 512:si * 512 + 512], self.ps_q[r][32:64, si, :]),
                        waits=[t_mm], inc=s_ba))
                prev_evac[r] = t_es[-1]
                continue
            if is_z:
                t_e1 = E("scalar", lambda e, r=r: e.activation(self.mixed_sb[:, r, 0:512], self.ps_q[r][:, 0, :], AF.Silu),
                         waits=[t_mm, (s_mh.h, max(0, s_mh.n - 16)), mix_guard[r]], inc=s_qe)
                t_e2 = E("scalar", lambda e, r=r: e.activation(self.mixed_sb[:, r, 512:1024], self.ps_q[r][:, 1, :], AF.Silu),
                         waits=[t_mm], inc=s_qe)
                prev_evac[r] = t_e2
                E("sync", lambda e, ct=ct, r=r: e.dma_start(out=self.z_hbm[ct - 64, :, :], in_=self.mixed_sb[:, r, :]),
                  waits=[t_e1, t_e2], inc=s_mh, inc_amt=16, dma=True)
                continue
            t_e = []
            for si, (s0, sw) in enumerate(slabs):
                if sw == 512:
                    t_e.append(E("vector", lambda e, r=r, si=si, s0=s0: e.tensor_copy(
                        self.strip_sb[:, r, s0:s0 + 512], self.ps_q[r][:, si, :]),
                        waits=[t_mm, strip_guard[r]], inc=s_qe))
                else:
                    t_e.append(E("vector", lambda e, r=r, s0=s0: e.tensor_copy(
                        self.strip_sb[:, r, s0:s0 + P], self.ps_q3[r][:, :]),
                        waits=[t_mm, strip_guard[r]], inc=s_qe))
            prev_evac[r] = t_e[-1]
            t_c = E("vector", lambda e, r=r, ct=ct: e.tensor_scalar(
                self.conv_acc[:, r, :], self.strip_sb[:, r, 128:1152],
                self.convw_sb[:, ct, 3:4], None, ALU.mult),
                waits=t_e + [self.t_consts[0], conv_guard[r]], inc=s_cv)
            for j in range(3):
                t_c = E("vector", lambda e, r=r, ct=ct, j=j: e.scalar_tensor_tensor(
                    self.conv_acc[:, r, :], self.strip_sb[:, r, 125 + j:125 + j + NTOK],
                    self.convw_sb[:, ct, j:j + 1], self.conv_acc[:, r, :],
                    ALU.mult, ALU.add), waits=[t_c], inc=s_cv)
            t_mxd = E("scalar", lambda e, r=r: e.activation(
                self.mixed_sb[:, r, :], self.conv_acc[:, r, :], AF.Silu),
                waits=[t_c, (s_mh.h, max(0, s_mh.n - 16)), mix_guard[r]], inc=s_mx)
            conv_guard[r] = t_mxd
            strip_guard[r] = t_c
            E("sync", lambda e, ct=ct, r=r: e.dma_start(out=self.mixed_hbm[ct, :, :], in_=self.mixed_sb[:, r, :]),
              waits=[t_mxd], inc=s_mh, inc_amt=16, dma=True)
            if is_qk:
                t_sq = E("vector", lambda e, r=r: e.tensor_mul(
                    self.sq_sb[:, r, :], self.mixed_sb[:, r, :], self.mixed_sb[:, r, :]),
                    waits=[t_mxd, sq_guard[r]], inc=s_sq)
                mix_guard[r] = t_sq
                t_n = E("tensor", lambda e, r=r: (
                    e.matmul(self.ps_nrm[0:1, 0:512], self.ones_col[:, :], self.sq_sb[:, r, 0:512], start=True, stop=True),
                    e.matmul(self.ps_nrm[0:1, 512:1024], self.ones_col[:, :], self.sq_sb[:, r, 512:1024], start=True, stop=True))[-1],
                    waits=[t_sq, (s_nr.h, s_nr.n), t_ones_tok],
                    inc=s_nm, inc_amt=2)
                sq_guard[r] = t_n
                rb = ct % 2
                t_r1 = E("scalar", lambda e, rb=rb: e.activation(self.rowbuf[:, rb, :], self.ps_nrm[0:1, :], AF.Sqrt),
                         waits=[t_n, row_guard[rb]], inc=s_nr)
                bb_l2, tok_l2 = self.ccol(EPS_L2, rows=1)
                t_r2 = E("scalar", lambda e, rb=rb, bb=bb_l2: e.activation(self.rowbuf[:, rb, :], self.rowbuf[:, rb, :], AF.Ln, bias=bb),
                         waits=[t_r1, tok_l2], inc=s_nr)
                tgt = self.sh_lnq if ct < 16 else self.sh_lnk
                kh = ct % 16
                E("sync", lambda e, kh=kh, tgt=tgt, rb=rb: e.dma_start(
                    out=tgt[kh:kh + 1, :], in_=self.rowbuf[:, rb, :]),
                    waits=[t_r2], inc=s_nd, inc_amt=16, dma=True)
                row_guard[rb] = E("sync", lambda e, kh=kh, tgt=tgt, rb=rb: e.dma_start(
                    out=tgt[16 + kh:16 + kh + 1, :], in_=self.rowbuf[:, rb, :]),
                    waits=[t_r2], inc=s_nd, inc_amt=16, dma=True)
        self.t_qkvz_done = [(self.sems["s_mh"].h, self.sems["s_mh"].n),
                            (self.sems["s_nd"].h, self.sems["s_nd"].n),
                            (self.sems["s_ba"].h, self.sems["s_ba"].n)]

    def phase3_sheets(self):
        self.ph("p3")
        E = self.E
        s_p3 = self.sem("s_p3")
        s_p3s = self.sem("s_p3s")
        s_gmr = self.sem("s_gmr")
        ba_done = self.t_qkvz_done[2]
        n_done = self.t_qkvz_done[1]
        GG, GC, SPS = self.sh_gg, self.sh_gc, self.sh_sps
        LNQ, LNK = self.sh_lnq, self.sh_lnk
        U, W, PP = self.sh_u, self.sh_w, self.sh_p
        BE, EG = self.sh_be, self.sh_eg
        b_rows = self.ba_b[:, :]
        a_rows = self.ba_a[:, :]
        bb_one, tok_one = self.ccol(1.0, rows=32)

        t1 = E("scalar", lambda e: e.activation(SPS[:, :], b_rows, AF.Abs), waits=[ba_done], inc=s_p3s)
        t2 = E("scalar", lambda e: e.activation(SPS[:, :], SPS[:, :], AF.Exp, scale=-1.0), waits=[t1], inc=s_p3s)
        t3 = E("scalar", lambda e, bb=bb_one: e.activation(SPS[:, :], SPS[:, :], AF.Ln, bias=bb), waits=[t2, tok_one], inc=s_p3s)
        t4 = E("scalar", lambda e: e.activation(self.scr32[:, 0, :], b_rows, AF.Relu, scale=-1.0),
               waits=[ba_done], inc=s_p3s)
        t5 = E("vector", lambda e: e.tensor_add(SPS[:, :], SPS[:, :], self.scr32[:, 0, :]), waits=[t3, t4], inc=s_p3)
        u1 = E("scalar", lambda e: e.activation(GG[:, :], a_rows, AF.Abs, bias=self.dt_col[:, :]),
               waits=[ba_done, self.t_consts[6]], inc=s_p3s)
        u2 = E("scalar", lambda e: e.activation(GG[:, :], GG[:, :], AF.Exp, scale=-1.0), waits=[u1], inc=s_p3s)
        u3 = E("scalar", lambda e, bb=bb_one: e.activation(GG[:, :], GG[:, :], AF.Ln, bias=bb), waits=[u2, tok_one], inc=s_p3s)
        u4 = E("scalar", lambda e: e.activation(self.scr32[:, 1, :], a_rows, AF.Relu, bias=self.dt_col[:, :]),
               waits=[ba_done, self.t_consts[6], t4], inc=s_p3s)
        u5 = E("vector", lambda e: e.tensor_add(GG[:, :], GG[:, :], self.scr32[:, 1, :]), waits=[u3, u4], inc=s_p3)
        e1 = E("scalar", lambda e: e.activation(self.negea_col[:, :], self.alog_col[:, :], AF.Exp),
               waits=[self.t_consts[7]], inc=s_p3s)
        e2 = E("vector", lambda e: e.tensor_scalar(self.negea_col[:, :], self.negea_col[:, :], -1.0, None, ALU.mult),
               waits=[e1], inc=s_p3)
        g1 = E("vector", lambda e: e.tensor_scalar(GG[:, :], GG[:, :], self.negea_col[:, :], None, ALU.mult),
               waits=[u5, e2], inc=s_p3)
        gs = []
        for ci in range(NCH):
            gs.append(E("vector", lambda e, ci=ci: e.tensor_tensor_scan(
                GC[:, ci * P:(ci + 1) * P], GG[:, ci * P:(ci + 1) * P],
                GG[:, ci * P:(ci + 1) * P], 0.0, ALU.add, ALU.bypass),
                waits=[g1], inc=s_p3))
        w1 = E("vector", lambda e: e.tensor_scalar(W[:, :], GC[:, :], -1.0, None, ALU.mult),
               waits=gs, inc=s_p3)
        w2 = E("vector", lambda e: e.tensor_sub(W[:, :], W[:, :], LNK[:, :]),
               waits=[w1, n_done], inc=s_p3)
        p1 = E("vector", lambda e: e.tensor_sub(PP[:, :], GC[:, :], SPS[:, :]),
               waits=gs + [t5], inc=s_p3)
        p2 = E("vector", lambda e: e.tensor_sub(PP[:, :], PP[:, :], LNK[:, :]),
               waits=[p1, n_done], inc=s_p3)
        b1 = E("scalar", lambda e: e.activation(BE[:, :], PP[:, :], AF.Exp), waits=[p2], inc=s_p3s)
        q1 = E("vector", lambda e: e.tensor_sub(U[:, :], GC[:, :], LNQ[:, :]),
               waits=gs + [n_done], inc=s_p3)
        q2 = E("vector", lambda e: e.tensor_scalar(U[:, :], U[:, :], -0.5 * LNDK, None, ALU.add),
               waits=[q1], inc=s_p3)
        egs = []
        for ci in range(NCH):
            egs.append(E("vector", lambda e, ci=ci: e.tensor_scalar(
                EG[:, ci * P:(ci + 1) * P], GC[:, ci * P:(ci + 1) * P],
                GC[:, ci * P + P - 1:ci * P + P], None, ALU.subtract),
                waits=gs, inc=s_p3))
        eg2 = E("vector", lambda e: e.tensor_add(EG[:, :], EG[:, :], LNK[:, :]),
                waits=egs + [n_done], inc=s_p3)
        eg3 = E("scalar", lambda e: e.activation(EG[:, :], EG[:, :], AF.Exp, scale=-1.0),
                waits=[eg2], inc=s_p3s)
        gm1 = []
        for ci in range(NCH):
            gm1.append(E("vector", lambda e, ci=ci: e.tensor_copy(
                self.gam_sb[:, ci:ci + 1], GC[:, ci * P + P - 1:ci * P + P]),
                waits=gs, inc=s_p3))
        gm2 = E("scalar", lambda e: e.activation(self.gam_sb[:, :], self.gam_sb[:, :], AF.Exp),
                waits=gm1, inc=s_p3s)
        gm3 = E("sync", lambda e: e.dma_start(out=self.gamrow[:, :], in_=self.gam_sb[:, :]),
                waits=[gm2], inc=s_gmr, inc_amt=16, dma=True)
        bt1 = E("scalar", lambda e: e.activation(self.sh_bet[:, :], SPS[:, :], AF.Exp, scale=-1.0),
                waits=[t5], inc=s_p3s)
        eu1 = E("scalar", lambda e: e.activation(self.sh_eu[:, :], U[:, :], AF.Exp),
                waits=[q2], inc=s_p3s)
        self.t_sheets = [w2, p2, b1, q2, eg3, gm3, t5, u5, bt1, eu1]
        self.t_gamrow = gm3
        if "sheets" in self.probes:
            s_shd = self.sem("s_shd")
            rows = [GG, GC, SPS, LNQ, LNK, U, W, PP]
            for i, tn in enumerate(rows):
                E("sync", lambda e, tn=tn, i=i: e.dma_start(
                    out=self.sheets_hbm[i, :, :], in_=tn[:, :]),
                    waits=self.t_sheets, inc=s_shd, inc_amt=16, dma=True)
            self.dump("sheets", self.sheets_hbm)

    def phase35_handshake_init(self):
        self.ph("hs")
        """Emitted FIRST: zero states, publish zeroed state slot, zero flag."""
        E = self.E
        s_hs = self.sem("s_hs")
        s_hv = self.sem("s_hv")
        self.parf_i = self.dt("parf_col_i", [P, 1])
        self.ptgt_i = self.dt("ptgt_i", [1, 1], I32)

        tz = E("vector", lambda e: e.memset(self.zero_i32[:, :], 0), inc=s_hv)
        t0 = E("vector", lambda e: e.memset(self.state_f[:, :, :], 0.0), inc=s_hv)
        t1 = E("vector", lambda e: e.memset(self.state_b[:, :, :], 0.0), inc=s_hv)
        tp = E("sync", lambda e: e.dma_start(out=self.par_sb[:, :], in_=self.par_ii),
               inc=s_hs, inc_amt=16, dma=True)
        tt = E("sync", lambda e: e.dma_start(out=self.ptgt_sb[:, :], in_=self.ptgt_i),
               inc=s_hs, inc_amt=16, dma=True)
        tf = E("sync", lambda e: e.dma_start(out=self.parf_col[:, :], in_=self.parf_i),
               inc=s_hs, inc_amt=16, dma=True)
        ts = E("sync", lambda e: e.dma_start(out=self.sent_sb[:, :], in_=self.sent_ii),
               inc=s_hs, inc_amt=16, dma=True)
        def pub(e):
            me_reg = e.alloc_register("me_hs")
            e.reg_load(me_reg, self.par_sb[0:1, 0:1])
            me = RuntimeValue(me_reg, min_val=0, max_val=1)
            self._me_rv_gp = me
            return e.dma_start(out=self.st_sh[ds(me, 1), :, :, :].rearrange("o h a b -> (o a) h b"),
                               in_=self.state_f[:, :, :])
        t2 = E("gpsimd", pub, waits=[t0, tp], inc=s_hs, inc_amt=16, dma=True)
        t3 = E("gpsimd", lambda e: e.dma_start(
            out=self.fl_sh[ds(self._me_rv_gp, 1), :, :], in_=self.zero_i32[0:1, 0:1]),
            waits=[t2, tz], inc=s_hs, inc_amt=16, dma=True)
        self.t_hs_done = t3
        self.s_hs = s_hs

    def phase4_core(self):
        self.ph("p4")
        """Per-chunk: reload mixed/z, transposes, masks, T-solve, scan."""
        E = self.E
        nc = self.nc
        self.phase_barrier()
        self.ctx_front.close()
        self.ctx_p2.close()
        self.ctx_p4 = ExitStack()
        self.ctx_p4sb = ExitStack()
        psum = lambda n, s, d=F32: self.psum(n, s, d, self.ctx_p4)
        self.pb_rep = [psum("pb_rep0", [P, 128], F32)]
        self.pb_kk = psum("pb_kk", [P, 128], F32)
        self.pb_qk = psum("pb_qk", [P, 128], F32)
        self.pb_p = [psum("pb_p0", [P, 128], F32), psum("pb_p1", [P, 128], F32)]
        self.pb_b = psum("pb_b", [P, 128], F32)
        self.pb_g = [psum("pb_g0", [P, 128], F32), psum("pb_g1", [P, 128], F32)]
        sb = self.sb4
        self.q_ch = sb("q_ch", [P, 16, P], BF16)
        self.k_ch = sb("k_ch", [P, 16, P], BF16)
        self.v_ch = sb("v_ch", [P, 32, P], BF16)
        self.z_ch = sb("z_ch", [P, 32, P], BF16)
        self.kr_ch = sb("kr_ch", [P, 16, P], BF16)
        self.vr_ch = sb("vr_ch", [P, 32, P], BF16)
        self.tcol = sb("tcol", [P, 8, 32])
        self.repbuf = sb("repbuf", [P, P])
        self.msk = [sb("msk0", [P, P]), sb("msk1", [P, P]), sb("msk2", [P, P])]
        self.tsp = [[sb("tsp%d_%d" % (rp, i), [P, P], BF16) for i in range(2)] for rp in range(2)]
        self.tsb = [[sb("tsb%d_%d" % (rp, i), [P, P], BF16) for i in range(2)] for rp in range(2)]
        self.tsg = [[sb("tsg%d_%d" % (rp, i), [P, P], BF16) for i in range(2)] for rp in range(2)]
        self.kbe_sb = sb("kbe_sb", [P, P], BF16)
        self.ktil_s = sb("ktil_s", [P, 32, P], BF16)
        self.vb_sb = sb("vb_sb", [P, P], BF16)
        self.val_s = sb("val_s", [P, 32, P], BF16)
        self.kcdT_s = sb("kcdT_s", [P, 32, P], BF16)
        self.qt_s = sb("qt_s", [P, 32, P], BF16)
        self.attnT_s = sb("attnT_s", [P, 32, P], BF16)
        self.vnew_sb = sb("vnew_sb", [P, P], BF16)
        self.hT_sb = sb("hT_sb", [P, P], BF16)
        self.eu_f = sb("eu_f", [P, P])
        self.sq128 = sb("sq128", [P, P], BF16)
        self.rsq_row = sb("rsq_row", [1, P], BF16)
        self.rsq_rowf = sb("rsq_rowf", [1, P])
        self.rsqbc_sb = sb("rsqbc_sb", [P, P])
        self.hg_sb = sb("hg_sb", [P, 32, P], BF16)
        self.gambc = sb("gambc", [P, 32 * NCH])

        s_ld4 = self.sem("s_ld4")
        s_t4 = self.sem("s_t4")
        s_e4 = self.sem("s_e4")
        s_s4 = self.sem("s_s4")
        s_m4 = self.sem("s_m4")
        s_sc4 = self.sem("s_sc4")
        s_hg = self.sem("s_hg")
        s_pol = self.sem("s_pol")
        s_pol2 = self.sem("s_pol2")
        s_sto = self.sem("s_sto")

        t_eyef = self.t_eyef
        self._ppg = [None, None]
        self._pgg = [None, None]
        t_gb = E("tensor", lambda e: e.matmul(self.pb_g[0][:, :], self.onesrow_f[0:1, :],
                                              self.gamrow[0:1, 0:128], start=True, stop=True),
                 waits=[self.t_gamrow, self.t_onesrow], inc=s_m4)
        t_gbe = E("vector", lambda e: e.tensor_copy(self.gambc[:, 0:128], self.pb_g[0][:, :]),
                  waits=[t_gb], inc=s_e4)
        t_gb2 = E("tensor", lambda e: e.matmul(self.pb_g[0][:, :], self.onesrow_f[0:1, :],
                                               self.gamrow[0:1, 128:256], start=True, stop=True),
                  waits=[self.t_gamrow, t_gbe], inc=s_m4)
        t_gbe2 = E("vector", lambda e: e.tensor_copy(self.gambc[:, 128:256], self.pb_g[0][:, :]),
                   waits=[t_gb2], inc=s_e4)
        t_gambc = t_gbe2
        self._pgg[0] = t_gbe2

        mixed_done = self.t_qkvz_done[0]
        sheets_done = list(self.t_sheets)
        SHEETS = [self.sh_u, self.sh_w, self.sh_p, self.sh_be, self.sh_eg,
                  self.sh_bet, self.sh_eu]
        ch_guard = None
        scan_started = False

        def rep_T(col_ap, wait_toks):
            tok_v = E("vector", lambda e, col_ap=col_ap: e.tensor_copy(
                self.repbuf[:, :], col_ap.broadcast_to((P, P))),
                waits=wait_toks, inc=s_e4)
            tok_t = E("tensor", lambda e: e.transpose(
                self.pb_rep[0][:, :], self.repbuf[:, :], self.eyef[:, :]),
                waits=[tok_v, t_eyef] + self._repguard, inc=s_t4)
            return self.pb_rep[0], tok_t, 0

        self._repguard = []
        self._eug = None
        self._hTg = None
        self._tcol_Sg = None

        for ci in range(NCH):
            c0 = ci * P
            self.ph("p4l")
            t_q = E("gpsimd", lambda e, ci=ci: e.dma_start(
                out=self.q_ch[:, :, :], in_=self.mixed_hbm[0:16, :, ci * P:(ci + 1) * P].rearrange("t p j -> p t j")),
                waits=[mixed_done, ch_guard], inc=s_ld4, inc_amt=16, dma=True)
            t_k = E("gpsimd", lambda e, ci=ci: e.dma_start(
                out=self.k_ch[:, :, :], in_=self.mixed_hbm[16:32, :, ci * P:(ci + 1) * P].rearrange("t p j -> p t j")),
                waits=[mixed_done, ch_guard], inc=s_ld4, inc_amt=16, dma=True)
            t_v = E("gpsimd", lambda e, ci=ci: e.dma_start(
                out=self.v_ch[:, :, :], in_=self.mixed_hbm[32:64, :, ci * P:(ci + 1) * P].rearrange("t p j -> p t j")),
                waits=[mixed_done, ch_guard], inc=s_ld4, inc_amt=16, dma=True)
            t_z = E("gpsimd", lambda e, ci=ci: e.dma_start(
                out=self.z_ch[:, :, :], in_=self.z_hbm[:, :, ci * P:(ci + 1) * P].rearrange("t p j -> p t j")),
                waits=[mixed_done, ch_guard], inc=s_ld4, inc_amt=16, dma=True)
            self.ph("p4l2")
            t_tc = []
            tc_guard = ch_guard
            for j, sh in enumerate(SHEETS):
                tok_t = E("tensor", lambda e, sh=sh, c0=c0: e.transpose(
                    self.pb_rep[0][0:P, 0:32], sh[:, c0:c0 + P], self.eyef[0:32, 0:32]),
                    waits=sheets_done + [t_eyef, tc_guard, self._tcol_Sg] + self._repguard, inc=s_t4)
                tok_e = E("vector", lambda e, j=j: e.tensor_copy(
                    self.tcol[:, j, :], self.pb_rep[0][0:P, 0:32]),
                    waits=[tok_t], inc=s_e4)
                self._repguard = [tok_e]
                t_tc.append(tok_e)
            self.ph("p4l3")
            slots = [self.pb_kk[:, :].bitcast(BF16)[:, 0:P],
                     self.pb_qk[:, :].bitcast(BF16)[:, 0:P]]
            t_kr, t_vr = [], []
            self._ktguard = [self._kkguard, self._qkguard]
            tr_i = 0
            for grp, (src_t, dst_t, n, t_src) in enumerate(
                    ((self.k_ch, self.kr_ch, 16, t_k),
                     (self.v_ch, self.vr_ch, 32, t_v))):
                for i in range(n):
                    sl = tr_i % 2
                    tr_i += 1
                    tok_t = E("tensor", lambda e, src_t=src_t, i=i, sl=sl: e.transpose(
                        slots[sl], src_t[:, i, :], self.eye_sb[:, :]),
                        waits=[t_src, self._ktguard[sl]], inc=s_t4)
                    if sl == 0:
                        f = lambda e, dst_t=dst_t, i=i, sl=sl: e.tensor_copy(dst_t[:, i, :], slots[sl])
                        tok_e = E("vector", f, waits=[tok_t], inc=s_e4)
                    else:
                        f = lambda e, dst_t=dst_t, i=i, sl=sl: e.copy(dst_t[:, i, :], slots[sl])
                        tok_e = E("scalar", f, waits=[tok_t], inc=s_sc4)
                    self._ktguard[sl] = tok_e
                    (t_kr if grp == 0 else t_vr).append(tok_e)
            self._kkguard = self._ktguard[0]
            self._qkguard = self._ktguard[1]
            self.ph("p4p")
            prep_done = {}

            def emit_masks_kkqk(r):
                kh = r % 16
                rp = r % 2
                self.ph("p4p1")
                ps_u, tk_u, _ = rep_T(self.tcol[:, 0, r:r + 1], t_tc)
                m0 = E("vector", lambda e, ps_u=ps_u: e.tensor_add(self.msk[0][:, :], ps_u[:, :], self.mtQ_sb[:, :]),
                       waits=[tk_u, self.t_consts[5]], inc=s_e4)
                self._repguard = [m0]
                e0 = E("scalar", lambda e, r=r: e.activation(self.msk[0][:, :], self.msk[0][:, :], AF.Exp,
                                                             bias=self.tcol[:, 1, r:r + 1]),
                       waits=[m0], inc=s_sc4)
                ps_w, tk_w, _ = rep_T(self.tcol[:, 1, r:r + 1], t_tc)
                m1 = E("vector", lambda e, ps_w=ps_w: e.tensor_add(self.msk[1][:, :], ps_w[:, :], self.mtA_sb[:, :]),
                       waits=[tk_w, self.t_consts[3]], inc=s_e4)
                self._repguard = [m1]
                e1 = E("scalar", lambda e, r=r: e.activation(self.msk[1][:, :], self.msk[1][:, :], AF.Exp,
                                                             bias=self.tcol[:, 2, r:r + 1]),
                       waits=[m1], inc=s_sc4)
                ps_p, tk_p, _ = rep_T(self.tcol[:, 2, r:r + 1], t_tc)
                m2 = E("vector", lambda e, ps_p=ps_p: e.tensor_add(self.msk[2][:, :], ps_p[:, :], self.mtAT_sb[:, :]),
                       waits=[tk_p, self.t_consts[4]], inc=s_e4)
                self._repguard = [m2]
                e2 = E("scalar", lambda e, r=r: e.activation(self.msk[2][:, :], self.msk[2][:, :], AF.Exp,
                                                             bias=self.tcol[:, 1, r:r + 1]),
                       waits=[m2], inc=s_sc4)
                tsp, tsb, tsg = self.tsp[rp], self.tsb[rp], self.tsg[rp]
                t_kk = E("tensor", lambda e, kh=kh: e.matmul(self.pb_kk[:, :], self.k_ch[:, kh, :], self.k_ch[:, kh, :],
                                                             start=True, stop=True),
                         waits=[t_k, self._kkguard], inc=s_m4)
                t_qk = E("tensor", lambda e, kh=kh: e.matmul(self.pb_qk[:, :], self.k_ch[:, kh, :], self.q_ch[:, kh, :],
                                                             start=True, stop=True),
                         waits=[t_q, t_k, self._qkguard], inc=s_m4)
                aA = E("vector", lambda e, tsp=tsp: e.scalar_tensor_tensor(tsp[0][:, :], self.pb_kk[:, :], -1.0,
                                                                           self.msk[1][:, :], ALU.mult, ALU.mult),
                       waits=[t_kk, e1], inc=s_e4)
                aAT = E("vector", lambda e, tsb=tsb: e.scalar_tensor_tensor(tsb[0][:, :], self.pb_kk[:, :], -1.0,
                                                                            self.msk[2][:, :], ALU.mult, ALU.mult),
                        waits=[t_kk, e2], inc=s_e4)
                self._kkguard = aAT
                aQ = E("vector", lambda e, r=r: e.tensor_mul(self.attnT_s[:, r, :], self.pb_qk[:, :], self.msk[0][:, :]),
                       waits=[t_qk, e0], inc=s_e4)
                self._qkguard = aQ
                g0 = E("vector", lambda e, tsg=tsg, tsb=tsb: e.tensor_add(tsg[0][:, :], tsb[0][:, :], self.eye_sb[:, :]),
                       waits=[aAT], inc=s_e4)
                return dict(r=r, kh=kh, rp=rp, cp=0, cb=0, cg=0,
                            tok_p=aA, tok_b=aAT, tok_g=g0, aA=aA, aAT=aAT, aQ=aQ)

            def emit_level(st, j):
                rp = st["rp"]
                tsp, tsb, tsg = self.tsp[rp], self.tsb[rp], self.tsg[rp]
                cp, cb, cg = st["cp"], st["cb"], st["cg"]
                np_, nb, ng = 1 - cp, 1 - cb, 1 - cg
                t_ps = E("tensor", lambda e, cb=cb, cp=cp, rp=rp, tsp=tsp, tsb=tsb: e.matmul(
                    self.pb_p[rp][:, :], tsb[cb][:, :], tsp[cp][:, :], start=True, stop=True),
                    waits=[st["tok_p"], st["tok_b"], self._ppg[rp]], inc=s_m4)
                if j < 6:
                    t_bs = E("tensor", lambda e, cb=cb, cp=cp, tsp=tsp, tsb=tsb: e.matmul(
                        self.pb_b[:, :], tsp[cp][:, :], tsb[cb][:, :], start=True, stop=True),
                        waits=[st["tok_p"], st["tok_b"], self._pbg], inc=s_m4)
                ev_p = E("vector", lambda e, np_=np_, rp=rp, tsp=tsp: e.tensor_copy(tsp[np_][:, :], self.pb_p[rp][:, :]),
                         waits=[t_ps], inc=s_e4)
                self._ppg[rp] = ev_p
                if j < 6:
                    ev_b = E("scalar", lambda e, nb=nb, tsb=tsb: e.copy(tsb[nb][:, :], self.pb_b[:, :]),
                             waits=[t_bs], inc=s_sc4)
                    self._pbg = ev_b
                    st["tok_b"] = ev_b
                t_gm = E("tensor", lambda e, np_=np_, cg=cg, rp=rp, tsp=tsp, tsg=tsg: (
                    e.matmul(self.pb_g[rp][:, :], tsp[np_][:, :], tsg[cg][:, :], start=True, stop=False),
                    e.matmul(self.pb_g[rp][:, :], self.eye_sb[:, :], tsg[cg][:, :], start=False, stop=True))[-1],
                    waits=[ev_p, st["tok_g"], self._pgg[rp], t_gambc], inc=s_m4)
                ev_g = E("vector", lambda e, ng=ng, rp=rp, tsg=tsg: e.tensor_copy(tsg[ng][:, :], self.pb_g[rp][:, :]),
                         waits=[t_gm], inc=s_e4)
                self._pgg[rp] = ev_g
                st["tok_g"] = ev_g
                st["tok_p"] = ev_p
                st["cp"], st["cb"], st["cg"] = np_, nb, ng

            def emit_apps(st):
                r, kh, rp = st["r"], st["kh"], st["rp"]
                tsg = self.tsg[rp]
                G_slot = st["cg"]
                tok_g = st["tok_g"]
                self.ph("p4p3")
                kbe1 = E("scalar", lambda e, kh=kh, r=r: e.activation(
                    self.kbe_sb[:, :], self.kr_ch[:, kh, :], AF.Identity,
                    scale=self.tcol[:, 3, r:r + 1]),
                    waits=t_kr + t_tc + [self._kbeg], inc=s_sc4)
                kt1 = E("scalar", lambda e, kh=kh, r=r: e.activation(
                    self.ktil_s[:, r, :], self.kr_ch[:, kh, :], AF.Identity,
                    scale=self.tcol[:, 4, r:r + 1]),
                    waits=t_kr + t_tc, inc=s_sc4)
                vb1 = E("scalar", lambda e, r=r: e.activation(
                    self.vb_sb[:, :], self.vr_ch[:, VH[r], :], AF.Identity,
                    scale=self.tcol[:, 5, r:r + 1]),
                    waits=t_vr + t_tc + [self._vbg], inc=s_sc4)
                t_val = E("tensor", lambda e, cg=G_slot, tsg=tsg: e.matmul(self.pb_kk[:, :], tsg[cg][:, :],
                                                                           self.vb_sb[:, :], start=True, stop=True),
                          waits=[tok_g, vb1, st["aA"], st["aAT"], self._kkguard], inc=s_m4)
                ev_val = E("vector", lambda e, r=r: e.tensor_copy(self.val_s[:, r, :], self.pb_kk[:, :]),
                           waits=[t_val], inc=s_e4)
                self._kkguard = ev_val
                self._vbg = t_val
                t_kcd = E("tensor", lambda e, cg=G_slot, tsg=tsg: e.matmul(self.pb_qk[:, :], self.kbe_sb[:, :],
                                                                           tsg[cg][:, :], start=True, stop=True),
                          waits=[tok_g, kbe1, st["aQ"], self._qkguard], inc=s_m4)
                ev_kcd = E("scalar", lambda e, r=r: e.copy(self.kcdT_s[:, r, :], self.pb_qk[:, :]),
                           waits=[t_kcd], inc=s_sc4)
                self._qkguard = ev_kcd
                self._kbeg = t_kcd
                ps_e, tk_e, _ = rep_T(self.tcol[:, 6, r:r + 1], t_tc)
                qt1 = E("vector", lambda e, kh=kh, r=r, ps_e=ps_e: e.scalar_tensor_tensor(
                    self.qt_s[:, r, :], ps_e[:, :], 1.0, self.q_ch[:, kh, :], ALU.mult, ALU.mult),
                    waits=[tk_e, t_q], inc=s_e4)
                self._repguard = [qt1]
                prep_done[r] = [ev_val, ev_kcd, qt1, st["aQ"], kt1]

            for pr in range(16):
                sts = [emit_masks_kkqk(2 * pr), emit_masks_kkqk(2 * pr + 1)]
                self.ph("p4p2")
                for j in range(1, 7):
                    for st in sts:
                        emit_level(st, j)
                for st in sts:
                    emit_apps(st)
            for r in ():
                if "prep" in self.probes and ci == 0 and r == 0:
                    s_pp = self.sem("s_pp")
                    pa = self.probe_aps["prep"]
                    for i, (tn, ap_) in enumerate([
                            (self.attnT_s, self.attnT_s[:, 0, :]),
                            (None, self.tsg[G_slot][:, :]),
                            (None, self.val_s[:, 0, :]),
                            (None, self.kcdT_s[:, 0, :]),
                            (None, self.qt_s[:, 0, :]),
                            (None, self.tsp[0][:, :]),
                            (None, self.tsb[0][:, :])]):
                        E("sync", lambda e, i=i, ap_=ap_: e.dma_start(out=pa[i, :, :], in_=ap_),
                          waits=prep_done[r] + [aA, aAT], inc=s_pp, inc_amt=16, dma=True)
                    self._t_prep_probe = (s_pp.h, s_pp.n)
            self.ph("p4s")
            if ci == 0:
                def poll(e):
                    flag_reg = e.alloc_register("flag_pl")
                    cnt_reg = e.alloc_register("cnt_pl")
                    tgt_reg = e.alloc_register("tgt_pl")
                    e.reg_mov(cnt_reg, 0)
                    e.reg_load(tgt_reg, self.ptgt_sb[0:1, 0:1])
                    e.reg_mov(flag_reg, -12345)
                    me = self._me_rv_gp
                    with e.While(lambda: RuntimeValue(flag_reg) != RuntimeValue(tgt_reg)):
                        e.dma_start(out=self.flag_sb[:, :], in_=self.fl_sh[ds(1 - me, 1), :, :]).then_inc(s_pol.h, 16)
                        e.reg_add(cnt_reg, cnt_reg, 16)
                        e.wait_ge(s_pol.h, RuntimeValue(cnt_reg))
                        e.reg_load(flag_reg, self.flag_sb[0:1, 0:1])
                    return e.dma_start(out=self.state_f[:, :, :],
                                       in_=self.st_sh[ds(1 - me, 1), :, :, :].rearrange("o h a b -> (o a) h b"))
                t_poll = E("gpsimd", poll, waits=[self.t_hs_done], inc=s_pol2, inc_amt=16, dma=True)
                t_init = E("vector", lambda e: e.tensor_scalar(
                    self.state_f[:, :, :], self.state_f[:, :, :], self.parf_col[:, 0:1], None, ALU.mult),
                    waits=[t_poll, (self.s_hs.h, self.s_hs.n)], inc=s_e4)
                t_initb = E("scalar", lambda e: e.copy(self.state_b[:, :, :], self.state_f[:, :, :]),
                            waits=[t_init], inc=s_sc4)
                self._stok = [[t_initb, t_init]] * 32
            hg_toks = []
            for r in range(32):
                pd = prep_done[r]
                sready = self._stok[r]
                rp = r % 2
                t_vn = E("tensor", lambda e, r=r, rp=rp: e.matmul(self.pb_p[rp][:, :], self.kcdT_s[:, r, :],
                                                                  self.state_b[:, r, :], start=True, stop=True),
                         waits=pd + sready + [self._ppg[rp]], inc=s_m4)
                vn = E("vector", lambda e, r=r, rp=rp: e.tensor_sub(self.vnew_sb[:, :], self.val_s[:, r, :],
                                                                    self.pb_p[rp][:, :]),
                       waits=[t_vn, self._vng], inc=s_e4)
                self._ppg[rp] = vn
                t_h = E("tensor", lambda e, r=r: (
                    e.matmul(self.pb_b[:, :], self.state_b[:, r, :], self.qt_s[:, r, :], start=True, stop=False),
                    e.matmul(self.pb_b[:, :], self.vnew_sb[:, :], self.attnT_s[:, r, :], start=False, stop=True))[-1],
                    waits=[vn, self._pbg], inc=s_m4)
                ev_h = E("vector", lambda e: e.tensor_copy(self.hT_sb[:, :], self.pb_b[:, :]),
                         waits=[t_h, self._hTg], inc=s_e4)
                self._pbg = ev_h
                t_sq = E("scalar", lambda e: e.activation(self.sq128[:, :], self.hT_sb[:, :], AF.Square),
                         waits=[ev_h], inc=s_sc4)
                self._hTg = t_sq
                t_cs = E("tensor", lambda e: e.matmul(self.pb_kk[0:1, :], self.ones_col[:, :], self.sq128[:, :],
                                                      start=True, stop=True),
                         waits=[t_sq, self._kkguard], inc=s_m4)
                bb_eps, tok_eps2 = self.ccol(EPS_RMS, rows=1)
                t_l1 = E("scalar", lambda e, bb=bb_eps: e.activation(self.rsq_rowf[:, :], self.pb_kk[0:1, :], AF.Ln,
                                                                     bias=bb, scale=1.0 / 128.0),
                         waits=[t_cs, tok_eps2, self._rsqfg], inc=s_sc4)
                t_l2 = E("scalar", lambda e: e.activation(self.rsq_row[:, :], self.rsq_rowf[:, :], AF.Exp, scale=-0.5),
                         waits=[t_l1], inc=s_sc4)
                self._rsqfg = t_l2
                t_rb = E("tensor", lambda e: e.matmul(self.pb_qk[:, :], self.onesrow_b[0:1, :], self.rsq_row[0:1, :],
                                                      start=True, stop=True),
                         waits=[t_l2, self._qkguard, self.t_onesrow], inc=s_m4)
                ev_rb = E("vector", lambda e: e.tensor_copy(self.rsqbc_sb[:, :], self.pb_qk[:, :]),
                          waits=[t_rb, self._rsqg], inc=s_e4)
                self._qkguard = ev_rb
                g1 = E("vector", lambda e: e.tensor_mul(self.msk[0][:, :], self.hT_sb[:, :], self.rsqbc_sb[:, :]),
                       waits=[ev_rb, ev_h, self._mskg], inc=s_e4)
                self._rsqg = g1
                g2 = E("vector", lambda e, r=r: e.scalar_tensor_tensor(
                    self.hg_sb[:, VH[r], :], self.z_ch[:, VH[r], :], self.nw_sb[:, 0:1],
                    self.msk[0][:, :], ALU.mult, ALU.mult),
                    waits=[g1, t_z, self.t_nw], inc=s_e4)
                self._mskg = g2
                hg_toks.append(g2)
                self._kkguard = t_l1
                t_su = E("tensor", lambda e, r=r, rp=rp: e.matmul(self.pb_g[rp][:, :], self.ktil_s[:, r, :],
                                                                  self.vnew_sb[:, :], start=True, stop=True),
                         waits=[vn, self._pgg[rp]], inc=s_m4)
                self._vng = t_su
                if "prep" in self.probes and ci == 0 and r == 0:
                    s_pp = self.sems["s_pp"]
                    pa = self.probe_aps["prep"]
                    E("sync", lambda e: e.dma_start(out=pa[7, :, :], in_=self.vnew_sb[:, :]),
                      waits=[vn, g2], inc=s_pp, inc_amt=16, dma=True)
                    E("gpsimd", lambda e: e.dma_start(out=pa[8, :, :], in_=self.msk[0][:, :]),
                      waits=[g1, g2], inc=s_pp, inc_amt=16, dma=True)
                    E("sync", lambda e: e.dma_start(out=pa[9, :, :], in_=self.sq128[:, :]),
                      waits=[t_sq, g2, t_cs], inc=s_pp, inc_amt=16, dma=True)
                su = E("vector", lambda e, r=r, ci=ci, rp=rp: e.scalar_tensor_tensor(
                    self.state_f[:, r, :], self.state_f[:, r, :], self.gambc[:, r * NCH + ci:r * NCH + ci + 1],
                    self.pb_g[rp][:, :], ALU.mult, ALU.add),
                    waits=[t_su, t_gambc], inc=s_e4)
                self._pgg[rp] = su
                sb1 = E("scalar", lambda e, r=r: e.copy(self.state_b[:, r, :], self.state_f[:, r, :]),
                        waits=[su], inc=s_sc4)
                self._stok[r] = [sb1]
            self._tcol_Sg = self._stok[31][0]
            t_hgst = E("sync", lambda e, ci=ci: e.dma_start(
                out=self.hg_hbm[ci, :, :, :].rearrange("h a b -> a h b"),
                in_=self.hg_sb[:, :, :]),
                waits=hg_toks, inc=s_hg, inc_amt=16, dma=True)
            ch_guard = t_hgst
        self.ph("p4h")
        t_pub = E("gpsimd", lambda e: e.dma_start(
            out=self.st_sh[ds(self._me_rv_gp, 1), :, :, :].rearrange("o h a b -> (o a) h b"),
            in_=self.state_f[:, :, :]),
            waits=[self._stok[31][0], (self.sems["s_e4"].h, self.sems["s_e4"].n)],
            inc=s_sto, inc_amt=16, dma=True)
        E("gpsimd", lambda e: e.dma_start(out=self.fl_sh[ds(self._me_rv_gp, 1), :, :], in_=self.sent_sb[0:1, 0:1]),
          waits=[t_pub], inc=s_sto, inc_amt=16, dma=True)
        self.t_hg_all = (s_hg.h, s_hg.n)

    def phase6_wout(self):
        self.ph("p6")
        """W_out (two c-halves, ci-outer, vh accumulated in psum) + LN + residual.
        Half-0 raw rows bounce through x2_hbm; half-1 completes rows, does both
        layernorms, writes x2 to HBM and x2lnT to SBUF."""
        E = self.E
        self.ctx_p4.close()
        self.ctx_p4sb.close()
        self.ctx_mid.close()
        self.ctx_p6 = ExitStack()
        self.ctx_p6keep = ExitStack()
        self.ctx_p6sb = ExitStack()
        sb6k = lambda n, s, d=F32: self.ctx_p6keep.enter_context(self.nc.sbuf_tensor(n, s, d, side="right"))
        sb6 = lambda n, s, d=F32: self.ctx_p6sb.enter_context(self.nc.sbuf_tensor(n, s, d, side="right"))
        ps6 = lambda n, s, d=F32: self.psum(n, s, d, self.ctx_p6)
        self.x2lnT = sb6k("x2lnT", [P, KD, NTOK], BF16)
        self.wo_sb = sb6("wo_sb", [P, 32, NTOK], BF16)
        self.hg6 = sb6("hg6", [P, 2, 32, P], BF16)
        self.x2_sb = sb6("x2_sb", [P, 2, D])
        self.x6_sb = sb6("x6_sb", [P, 2, D])
        self.st6 = sb6("st6", [P, 12])
        self.x2ln_sb = sb6("x2ln_sb", [P, 2, D], BF16)
        self.ps_w6 = [ps6("ps_w6a", [P, 2, 512]), ps6("ps_w6b", [P, 2, 512])]
        self.ps_t6 = [ps6("ps_t6a", [P, P], BF16), ps6("ps_t6b", [P, P], BF16)]
        s_w6 = self.sem("s_w6")
        s_h6 = self.sem("s_h6")
        s_m6 = self.sem("s_m6")
        s_e6 = self.sem("s_e6")
        s_s6 = self.sem("s_s6")
        s_x6 = self.sem("s_x6")
        s_st6 = self.sem("s_st6")
        s_t6 = self.sem("s_t6")
        hg_all = self.t_hg_all
        self.phase_barrier()
        ev_guard = [None, None]
        hg_guard = [None, None]
        x2g = [None, None]
        half_toks = {}
        wo_guard = None
        self._lnTg = [None, None]
        bb_eps, tok_eps = self.ccol(EPS_LN)
        for half in range(2):
            cs = half * NTOK
            t_w = E("gpsimd", lambda e, cs=cs: e.dma_start(
                out=self.wo_sb[:, :, :],
                in_=self.woutT_i.rearrange("(vh p) c -> p vh c", p=P)[:, :, cs:cs + NTOK]),
                waits=[wo_guard], inc=s_w6, inc_amt=16, dma=True)
            for ci in range(NCH):
                rr = ci % 2
                t_hg = E("sync", lambda e, ci=ci, rr=rr: e.dma_start(
                    out=self.hg6[:, rr, :, :], in_=self.hg_hbm[ci, :, :, :].rearrange("h a b -> a h b")),
                    waits=[hg_all, hg_guard[rr]], inc=s_h6, inc_amt=16, dma=True)
                def mm6(e, rr=rr):
                    last = None
                    for vh in range(32):
                        for cc in range(2):
                            last = e.matmul(self.ps_w6[rr][:, cc, :],
                                            self.hg6[:, rr, vh, :],
                                            self.wo_sb[:, vh, cc * 512:cc * 512 + 512],
                                            start=(vh == 0), stop=(vh == 31))
                    return last
                t_mm = E("tensor", mm6, waits=[t_w, t_hg, ev_guard[rr]], inc=s_m6)
                hg_guard[rr] = t_mm
                wo_guard = t_mm
                if half == 0:
                    ev1 = E("vector", lambda e, rr=rr: e.tensor_copy(
                        self.x2_sb[:, rr, 0:512], self.ps_w6[rr][:, 0, :]),
                        waits=[t_mm, x2g[rr]], inc=s_e6)
                    ev2 = E("vector", lambda e, rr=rr: e.tensor_copy(
                        self.x2_sb[:, rr, 512:1024], self.ps_w6[rr][:, 1, :]),
                        waits=[t_mm], inc=s_e6)
                    ev_guard[rr] = ev2
                    t_st = E("sync", lambda e, ci=ci, rr=rr: e.dma_start(
                        out=self.x2_hbm[ci * P:(ci + 1) * P, 0:1024], in_=self.x2_sb[:, rr, 0:1024]),
                        waits=[ev1, ev2], inc=s_st6, inc_amt=16, dma=True)
                    x2g[rr] = t_st
                    half_toks[ci] = t_st
                    continue
                t_r0 = E("sync", lambda e, ci=ci, rr=rr: e.dma_start(
                    out=self.x2_sb[:, rr, 0:1024], in_=self.x2_hbm[ci * P:(ci + 1) * P, 0:1024]),
                    waits=[half_toks[ci], x2g[rr]], inc=s_x6, inc_amt=16, dma=True)
                ev1 = E("vector", lambda e, rr=rr: e.tensor_copy(
                    self.x2_sb[:, rr, 1024:1536], self.ps_w6[rr][:, 0, :]),
                    waits=[t_mm, t_r0], inc=s_e6)
                ev2 = E("vector", lambda e, rr=rr: e.tensor_copy(
                    self.x2_sb[:, rr, 1536:2048], self.ps_w6[rr][:, 1, :]),
                    waits=[t_mm], inc=s_e6)
                ev_guard[rr] = ev2
                t_x = E("sync", lambda e, ci=ci, rr=rr: e.dma_start(
                    out=self.x6_sb[:, rr, :], in_=self.x_strip[(1 + ci) * P:(2 + ci) * P, :]),
                    waits=[self._x6g[rr]], inc=s_x6, inc_amt=16, dma=True)
                c0 = rr * 6
                st_ = self.st6
                t_s1 = E("scalar", lambda e, rr=rr, c0=c0: e.activation(
                    self.x2ln_sb[:, rr, :], self.x2_sb[:, rr, :], AF.Identity,
                    accum_out=st_[:, c0:c0 + 1]),
                    waits=[t_r0, ev1, ev2, self._lnTg[rr]], inc=s_s6)
                t_s2 = E("scalar", lambda e, rr=rr, c0=c0: e.activation(
                    self.x2ln_sb[:, rr, :], self.x2_sb[:, rr, :], AF.Square,
                    accum_out=st_[:, c0 + 1:c0 + 2]), waits=[t_s1], inc=s_s6)
                t_mu = E("vector", lambda e, c0=c0: e.tensor_scalar(
                    st_[:, c0 + 2:c0 + 3], st_[:, c0:c0 + 1], 1.0 / D, None, ALU.mult),
                    waits=[t_s1], inc=s_e6)
                t_msq = E("vector", lambda e, c0=c0: e.tensor_tensor(
                    st_[:, c0 + 4:c0 + 5], st_[:, c0 + 2:c0 + 3], st_[:, c0 + 2:c0 + 3], ALU.mult),
                    waits=[t_mu], inc=s_e6)
                t_var = E("vector", lambda e, c0=c0: e.scalar_tensor_tensor(
                    st_[:, c0 + 3:c0 + 4], st_[:, c0 + 1:c0 + 2], 1.0 / D,
                    st_[:, c0 + 4:c0 + 5], ALU.mult, ALU.subtract),
                    waits=[t_s2, t_msq], inc=s_e6)
                t_lnv = E("scalar", lambda e, c0=c0, bb=bb_eps: e.activation(
                    st_[:, c0 + 4:c0 + 5], st_[:, c0 + 3:c0 + 4], AF.Ln, bias=bb),
                    waits=[t_var, tok_eps], inc=s_s6)
                t_rstd = E("scalar", lambda e, c0=c0: e.activation(
                    st_[:, c0 + 3:c0 + 4], st_[:, c0 + 4:c0 + 5], AF.Exp, scale=-0.5),
                    waits=[t_lnv], inc=s_s6)
                t_n1 = E("vector", lambda e, rr=rr, c0=c0: e.tensor_scalar(
                    self.x2_sb[:, rr, :], self.x2_sb[:, rr, :],
                    st_[:, c0 + 2:c0 + 3], st_[:, c0 + 3:c0 + 4],
                    ALU.subtract, ALU.mult), waits=[t_rstd], inc=s_e6)
                t_n2 = E("vector", lambda e, rr=rr: e.tensor_add(
                    self.x2_sb[:, rr, :], self.x2_sb[:, rr, :], self.x6_sb[:, rr, :]),
                    waits=[t_n1, t_x], inc=s_e6)
                self._x6g[rr] = t_n2
                t_st = E("sync", lambda e, ci=ci, rr=rr: e.dma_start(
                    out=self.x2_hbm[ci * P:(ci + 1) * P, :], in_=self.x2_sb[:, rr, :]),
                    waits=[t_n2], inc=s_st6, inc_amt=16, dma=True)
                x2g[rr] = t_st
                u_s1 = E("scalar", lambda e, rr=rr, c0=c0: e.activation(
                    self.x2ln_sb[:, rr, :], self.x2_sb[:, rr, :], AF.Identity,
                    accum_out=st_[:, c0:c0 + 1]), waits=[t_n2], inc=s_s6)
                u_s2 = E("scalar", lambda e, rr=rr, c0=c0: e.activation(
                    self.x2ln_sb[:, rr, :], self.x2_sb[:, rr, :], AF.Square,
                    accum_out=st_[:, c0 + 1:c0 + 2]), waits=[u_s1], inc=s_s6)
                u_mu = E("vector", lambda e, c0=c0: e.tensor_scalar(
                    st_[:, c0 + 2:c0 + 3], st_[:, c0:c0 + 1], 1.0 / D, None, ALU.mult),
                    waits=[u_s1], inc=s_e6)
                u_msq = E("vector", lambda e, c0=c0: e.tensor_tensor(
                    st_[:, c0 + 4:c0 + 5], st_[:, c0 + 2:c0 + 3], st_[:, c0 + 2:c0 + 3], ALU.mult),
                    waits=[u_mu], inc=s_e6)
                u_var = E("vector", lambda e, c0=c0: e.scalar_tensor_tensor(
                    st_[:, c0 + 3:c0 + 4], st_[:, c0 + 1:c0 + 2], 1.0 / D,
                    st_[:, c0 + 4:c0 + 5], ALU.mult, ALU.subtract),
                    waits=[u_s2, u_msq], inc=s_e6)
                u_lnv = E("scalar", lambda e, c0=c0, bb=bb_eps: e.activation(
                    st_[:, c0 + 4:c0 + 5], st_[:, c0 + 3:c0 + 4], AF.Ln, bias=bb),
                    waits=[u_var], inc=s_s6)
                u_rstd = E("scalar", lambda e, c0=c0: e.activation(
                    st_[:, c0 + 3:c0 + 4], st_[:, c0 + 4:c0 + 5], AF.Exp, scale=-0.5),
                    waits=[u_lnv], inc=s_s6)
                u_nrm = E("vector", lambda e, rr=rr, c0=c0: e.tensor_scalar(
                    self.x2ln_sb[:, rr, :], self.x2_sb[:, rr, :],
                    st_[:, c0 + 2:c0 + 3], st_[:, c0 + 3:c0 + 4],
                    ALU.subtract, ALU.mult), waits=[u_rstd], inc=s_e6)
                for kc in range(KD):
                    i = ci * KD + kc
                    pb = i % 2
                    t_tp = E("tensor", lambda e, kc=kc, rr=rr, pb=pb: e.transpose(
                        self.ps_t6[pb][:, :], self.x2ln_sb[:, rr, kc * P:(kc + 1) * P],
                        self.eye_sb[:, :]),
                        waits=[u_nrm, self._t6g[pb]], inc=s_t6)
                    eng = "vector" if i % 2 == 0 else "scalar"
                    f = (lambda e, kc=kc, ci=ci, pb=pb: e.tensor_copy(
                        self.x2lnT[:, kc, ci * P:(ci + 1) * P], self.ps_t6[pb][:, :])) \
                        if eng == "vector" else \
                        (lambda e, kc=kc, ci=ci, pb=pb: e.copy(
                            self.x2lnT[:, kc, ci * P:(ci + 1) * P], self.ps_t6[pb][:, :]))
                    tok = E(eng, f, waits=[t_tp], inc=s_e6 if eng == "vector" else s_s6)
                    self._t6g[pb] = tok
                    self._lnTg[rr] = t_tp
        self.t_x2lnT_done = [(self.sems["s_e6"].h, self.sems["s_e6"].n),
                             (self.sems["s_s6"].h, self.sems["s_s6"].n)]
        self.t_x2_hbm_done = (self.sems["s_st6"].h, self.sems["s_st6"].n)

    def phase7_ffn(self):
        self.ph("p7a")
        """FFN gating. Gating transients live in ctx_p7g (freed before experts);
        softT/scal/y survive in ctx_p7sb."""
        E = self.E
        self.phase_barrier()
        self.ctx_p6.close()
        self.ctx_p6sb.close()
        ps7s = ExitStack()
        self.ctx_p7 = ps7s
        self.ctx_p7sb = ExitStack()
        self.ctx_p7g = ExitStack()
        sb7 = lambda n, s, d=F32: self.ctx_p7sb.enter_context(self.nc.sbuf_tensor(n, s, d, side="right"))
        sb7g = lambda n, s, d=F32: self.ctx_p7g.enter_context(self.nc.sbuf_tensor(n, s, d, side="right"))
        ps7 = lambda n, s, d=F32: self.psum(n, s, d, ps7s)
        self.pA = ps7("pA", [P, 2, 512])
        self.pB = ps7("pB", [P, 2, 512])
        self.pC = ps7("pC", [P, 2, 512])
        self.pD = ps7("pD", [P, 2, 512])
        self.pDt = [self.pC[:, 0, 0:P], self.pC[:, 1, 0:P]]
        self.softT_sb = sb7("softT_sb", [8, 8, NTOK], BF16)
        self.scal_sb = sb7("scal_sb", [P, 8, 8])
        self.y_sb = sb7("y_sb", [P, 8, D], BF16)
        self.gwg_sb = sb7g("gwg_sb", [P, KD, 64], BF16)
        self.scores = sb7g("scores", [P, 8, 64])
        self.scr7 = sb7g("scr7", [P, 64])
        self.col7 = sb7g("col7", [P, 8])
        self.mr8 = sb7g("mr8", [P, 8])
        self.wd_sb = sb7g("wd_sb", [P, 64])
        self.wdT_sb = sb7g("wdT_sb", [64, NTOK], BF16)
        self.iot = sb7g("iot", [64, 64])
        self.invf = sb7g("invf", [1, 64])
        self.G_sb = sb7g("G_sb", [64, 64], BF16)
        self.flat_sb = sb7g("flat_sb", [P, 8, 64])
        self.ef_sb = sb7g("ef_sb", [P, 64])
        self.den7 = sb7g("den7", [P, 8, 1])
        self.soft_sb = sb7g("soft_sb", [P, 8, 64])
        s_l7 = self.sem("s_l7")
        s_m7 = self.sem("s_m7")
        s_e7 = self.sem("s_e7")
        s_s7 = self.sem("s_s7")
        s_g7 = self.sem("s_g7")
        xt = self.t_x2lnT_done

        t_gw = E("gpsimd", lambda e: e.dma_start(
            out=self.gwg_sb[:, :, :], in_=self.gwgT.rearrange("(kc kp) c -> kp kc c", kp=P)),
            inc=s_l7, inc_amt=16, dma=True)
        t_iv = E("gpsimd", lambda e: e.dma_start(out=self.invf[:, :], in_=self.inv_map),
                 inc=s_l7, inc_amt=16, dma=True)
        t_io = E("gpsimd", lambda e: e.iota(self.iot[:, :], [[0, 64]], base=0, channel_multiplier=1, allow_small_or_imprecise_dtypes=True),
                 inc=s_g7)
        t_ib = E("tensor", lambda e: e.matmul(self.pD[0:64, 0, 0:64], self.onesrow_f[0:1, 0:64],
                                              self.invf[0:1, :], start=True, stop=True),
                 waits=[t_iv, self.t_onesrow], inc=s_m7)
        t_G = E("vector", lambda e: e.tensor_tensor(self.G_sb[:, :], self.iot[:, :], self.pD[0:64, 0, 0:64], ALU.is_equal),
                waits=[t_ib, t_io], inc=s_e7)
        t_mr = E("vector", lambda e: e.memset(self.mr8[:, :], 3e38), inc=s_e7)
        sc_toks = []
        for tt in range(8):
            rr = tt % 2
            def mmsc(e, tt=tt, rr=rr):
                last = None
                for kc in range(KD):
                    last = e.matmul(self.pA[0:P, rr, 0:64], self.x2lnT[:, kc, tt * P:(tt + 1) * P],
                                    self.gwg_sb[:, kc, :], start=(kc == 0), stop=(kc == KD - 1))
                return last
            t_sc = E("tensor", mmsc, waits=[t_gw, self._pAg[rr]] + xt, inc=s_m7)
            t_mx = E("vector", lambda e, rr=rr: e.reduce_max(self.col7[:, 0:1], self.pA[0:P, rr, 0:64], axis=AXX),
                     waits=[t_sc], inc=s_e7)
            t_ng = E("vector", lambda e: e.tensor_scalar(self.col7[:, 1:2], self.col7[:, 0:1], -1.0, None, ALU.mult),
                     waits=[t_mx], inc=s_e7)
            t_ex = E("scalar", lambda e, tt=tt, rr=rr: e.activation(
                self.scores[:, tt, :], self.pA[0:P, rr, 0:64], AF.Exp, bias=self.col7[:, 1:2],
                accum_out=self.col7[:, 2:3]), waits=[t_ng], inc=s_s7)
            self._pAg[rr] = t_ex
            t_rc = E("vector", lambda e: e.reciprocal(self.col7[:, 3:4], self.col7[:, 2:3]),
                     waits=[t_ex], inc=s_e7)
            t_sm = E("vector", lambda e, tt=tt: e.tensor_scalar(
                self.scores[:, tt, :], self.scores[:, tt, :], self.col7[:, 3:4], None, ALU.mult),
                waits=[t_rc], inc=s_e7)
            t_cp = E("vector", lambda e, tt=tt: e.tensor_copy(self.scr7[:, :], self.scores[:, tt, :]),
                     waits=[t_sm], inc=s_e7)
            tk = t_cp
            for kk in range(6):
                t_m = E("vector", lambda e: e.reduce_max(self.mr8[:, 0:1], self.scr7[:, :], axis=AXX),
                        waits=[tk, t_mr], inc=s_e7)
                tk = E("vector", lambda e: e.match_replace(self.scr7[:, :], self.mr8[:, :], self.scr7[:, :], -1e30),
                       waits=[t_m], inc=s_e7)
            t_msk = E("vector", lambda e: e.tensor_scalar(self.scr7[:, :], self.scr7[:, :], -1e29, None, ALU.is_lt),
                      waits=[tk], inc=s_e7)
            t_wd = E("vector", lambda e, tt=tt: e.tensor_mul(self.wd_sb[:, :], self.scores[:, tt, :], self.scr7[:, :]),
                     waits=[t_msk], inc=s_e7)
            t_tp = E("tensor", lambda e, rr=rr: e.transpose(self.pDt[rr][0:64, :], self.wd_sb[:, 0:64], self.eyef[:, :]),
                     waits=[t_wd, self._pDtg[rr], self.t_eyef], inc=s_m7)
            t_we = E("vector", lambda e, tt=tt, rr=rr: e.tensor_copy(self.wdT_sb[:, tt * P:(tt + 1) * P], self.pDt[rr][0:64, :]),
                     waits=[t_tp], inc=s_e7)
            self._pDtg[rr] = t_we
            t_fl = E("tensor", lambda e, tt=tt, rr=rr: e.matmul(
                self.pB[0:P, rr, 0:64], self.wdT_sb[:, tt * P:(tt + 1) * P], self.G_sb[:, :],
                start=True, stop=True), waits=[t_we, t_G, self._pBg[rr]], inc=s_m7)
            t_fe = E("vector", lambda e, tt=tt, rr=rr: e.tensor_copy(self.flat_sb[:, tt, :], self.pB[0:P, rr, 0:64]),
                     waits=[t_fl], inc=s_e7)
            self._pBg[rr] = t_fe
            t_sl = E("vector", lambda e, tt=tt: e.reduce_sum(
                self.scal_sb[:, tt, :],
                self.flat_sb[:, tt, :].rearrange("p (f k) -> p f k", f=8), axis=AXX),
                waits=[t_fe], inc=s_e7)
            t_ef = E("scalar", lambda e, tt=tt: e.activation(self.ef_sb[:, :], self.flat_sb[:, tt, :], AF.Exp),
                     waits=[t_fe], inc=s_s7)
            t_nz = E("vector", lambda e, tt=tt: e.tensor_scalar(self.scr7[:, :], self.flat_sb[:, tt, :], 0.0, None, ALU.not_equal),
                     waits=[t_fe, t_wd], inc=s_e7)
            t_em = E("vector", lambda e: e.tensor_mul(self.ef_sb[:, :], self.ef_sb[:, :], self.scr7[:, :]),
                     waits=[t_ef, t_nz], inc=s_e7)
            t_dn = E("vector", lambda e: e.reduce_sum(self.den7[:, :, :], self.ef_sb[:, :].rearrange("p (f k) -> p f k", f=8), axis=AXX),
                     waits=[t_em], inc=s_e7)
            t_dm = E("vector", lambda e: e.tensor_scalar(self.den7[:, :, :], self.den7[:, :, :], 1e-30, None, ALU.max),
                     waits=[t_dn], inc=s_e7)
            t_dr = E("vector", lambda e: e.reciprocal(self.den7[:, :, :], self.den7[:, :, :]),
                     waits=[t_dm], inc=s_e7)
            t_sf = E("vector", lambda e, tt=tt: e.tensor_tensor(
                self.soft_sb[:, tt, :].rearrange("p (f k) -> p f k", f=8),
                self.ef_sb[:, :].rearrange("p (f k) -> p f k", f=8),
                self.den7[:, :, :].broadcast_to((P, 8, 8)),
                ALU.mult), waits=[t_dr], inc=s_e7)
            sc_toks.append(t_sf)
            for f in range(8):
                rr2 = f % 2
                t_t2 = E("tensor", lambda e, tt=tt, f=f, rr2=rr2: e.transpose(
                    self.pDt[rr2][0:8, :], self.soft_sb[:, tt, 8 * f:8 * f + 8], self.eyef[:, :]),
                    waits=[t_sf, self._pDtg[rr2]], inc=s_m7)
                t_e2 = E("vector", lambda e, tt=tt, f=f, rr2=rr2: e.tensor_copy(
                    self.softT_sb[:, f, tt * P:(tt + 1) * P], self.pDt[rr2][0:8, :]),
                    waits=[t_t2], inc=s_e7)
                self._pDtg[rr2] = t_e2
                sc_toks.append(t_e2)
        self.t_gating = sc_toks

    def phase7_experts(self):
        self.ph("p7b")
        E = self.E
        self.phase_barrier()
        self.ctx_p7g.close()
        sb7 = lambda n, s, d=F32: self.ctx_p7sb.enter_context(self.nc.sbuf_tensor(n, s, d, side="right"))
        self.E8_sb = sb7("E8_sb", [8, 64], BF16)
        self.qaTg = sb7("qaTg", [P, KD, 64], BF16)
        self.qaTu = sb7("qaTu", [P, KD, 64], BF16)
        self.muW_sb = sb7("muW_sb", [P, KD, 8], BF16)
        self.dqaT_sb = sb7("dqaT_sb", [P, 11, 64], BF16)
        self.qbg_sb = sb7("qbg_sb", [64, 1408], BF16)
        self.qbu_sb = sb7("qbu_sb", [64, 1408], BF16)
        self.qbd_sb = sb7("qbd_sb", [64, D], BF16)
        self.scrow = sb7("scrow", [1, 4864], BF16)
        self.qmt_sb = sb7("qmt_sb", [8, 64], BF16)
        self.umt_sb = sb7("umt_sb", [8, 64], BF16)
        self.gmt_sb = sb7("gmt_sb", [8, P], BF16)
        self.gmt2_sb = sb7("gmt2_sb", [8, P], BF16)
        self.wexp_sb = sb7("wexp_sb", [64, NTOK])
        self.awg_sb = sb7("awg_sb", [64, NTOK], BF16)
        self.awu_sb = sb7("awu_sb", [64, NTOK], BF16)
        self.awd_sb = sb7("awd_sb", [64, NTOK], BF16)
        self.w_ring = sb7("w_ring", [P, 3, KD, P], BF16)
        self.silu_sb = sb7("silu_sb", [P, NTOK], BF16)
        self.act_sb = sb7("act_sb", [P, 6, NTOK], BF16)
        self.dw_sb = sb7("dw_sb", [P, 6, D], BF16)
        self.da_f32 = sb7("da_f32", [64, NTOK])
        s_l7 = self.sems["s_l7"]
        s_m7 = self.sems["s_m7"]
        s_e7 = self.sems["s_e7"]
        s_s7 = self.sems["s_s7"]
        gat = self.t_gating
        I11 = 11
        self._yg = {}

        t_e8 = E("gpsimd", lambda e: e.dma_start(out=self.E8_sb[:, :], in_=self.E8_i),
                 inc=s_l7, inc_amt=16, dma=True)

        def casted_load(out_ap, in_ap, waits=()):
            return E("gpsimd", lambda e: e.dma_start(out=out_ap, in_=in_ap),
                     waits=list(waits), inc=s_l7, inc_amt=16, dma=True)

        for f in range(9):
            shared = (f == 8)
            softT_f = self.softT_sb[:, min(f, 7), :]
            if not shared:
                lds = {}
                lds["qag"] = casted_load(self.qaTg[:, :, :],
                                         self.gqaT_i[f].rearrange("(kc kp) c -> kp kc c", kp=P),
                                         [self._ldg.get("qag")])
                lds["qau"] = casted_load(self.qaTu[:, :, :],
                                         self.uqaT_i[f].rearrange("(kc kp) c -> kp kc c", kp=P),
                                         [self._ldg.get("qau")])
                lds["muw"] = casted_load(self.muW_sb[:, :, :],
                                         self.muW_i[f].rearrange("(kc kp) c -> kp kc c", kp=P),
                                         [self._ldg.get("muw")])
                lds["dqa"] = casted_load(self.dqaT_sb[:, :, :],
                                         self.dqaT_i[f].rearrange("(ic ip) c -> ip ic c", ip=P),
                                         [self._ldg.get("dqa")])
                lds["qbg"] = casted_load(self.qbg_sb[:, :], self.gqbF_i[f], [self._ldg.get("qbg")])
                lds["qbu"] = casted_load(self.qbu_sb[:, :], self.uqbF_i[f], [self._ldg.get("qbu")])
                lds["qbd"] = casted_load(self.qbd_sb[:, :], self.dqbF_i[f], [self._ldg.get("qbd")])
                lds["gsc"] = casted_load(self.scrow[:, 0:1408], self.gsc_i[f], [self._ldg.get("gsc")])
                lds["usc"] = casted_load(self.scrow[:, 1408:2816], self.usc_i[f], [self._ldg.get("gsc")])
                lds["dsc"] = casted_load(self.scrow[:, 2816:4864], self.dsc_i[f], [self._ldg.get("gsc")])
                pres = []
                scw = [lds["gsc"], lds["usc"], lds["dsc"], lds["qbg"], lds["qbu"], lds["qbd"]]
                for (qb, s0, width) in ((self.qbg_sb, 0, 1408), (self.qbu_sb, 1408, 1408), (self.qbd_sb, 2816, 2048)):
                    for c0 in range(0, width, 512):
                        cw = min(512, width - c0)
                        t_b = E("tensor", lambda e, s0=s0, c0=c0, cw=cw: e.matmul(
                            self.pD[0:64, 0, 0:cw], self.onesrow_b[0:1, 0:64],
                            self.scrow[0:1, s0 + c0:s0 + c0 + cw], start=True, stop=True),
                            waits=scw + [self._pDg, self._upsg[1], self.t_onesrow], inc=s_m7)
                        t_v = E("vector", lambda e, qb=qb, c0=c0, cw=cw: e.tensor_mul(
                            qb[:, c0:c0 + cw], qb[:, c0:c0 + cw], self.pD[0:64, 0, 0:cw]),
                            waits=[t_b], inc=s_e7)
                        self._pDg = t_v
                        pres.append(t_v)
                def mm_qmt(e, qa):
                    last = None
                    for kc in range(KD):
                        last = e.matmul(self.pD[0:8, 1, 0:64], self.muW_sb[:, kc, :], qa[:, kc, :],
                                        start=(kc == 0), stop=(kc == KD - 1))
                    return last
                t_qm = E("tensor", lambda e: mm_qmt(e, self.qaTg),
                         waits=[lds["qag"], lds["muw"], self._pD1g], inc=s_m7)
                e_qm = E("vector", lambda e: e.tensor_copy(self.qmt_sb[:, :], self.pD[0:8, 1, 0:64]),
                         waits=[t_qm], inc=s_e7)
                t_um = E("tensor", lambda e: mm_qmt(e, self.qaTu), waits=[lds["qau"], e_qm], inc=s_m7)
                e_um = E("vector", lambda e: e.tensor_copy(self.umt_sb[:, :], self.pD[0:8, 1, 0:64]),
                         waits=[t_um], inc=s_e7)
                self._pD1g = e_um
                t_wx = E("tensor", lambda e, softT_f=softT_f: (
                    e.matmul(self.pB[0:64, 0, :], self.E8_sb[:, :], softT_f[0:8, 0:512], start=True, stop=True),
                    e.matmul(self.pB[0:64, 1, :], self.E8_sb[:, :], softT_f[0:8, 512:1024], start=True, stop=True))[-1],
                    waits=gat + [t_e8, self._pBg[0], self._pBg[1],
                                 self._dpsg[0], self._dpsg[1]], inc=s_m7)
                e_wx = E("vector", lambda e: (
                    e.tensor_copy(self.wexp_sb[:, 0:512], self.pB[0:64, 0, :]),
                    e.tensor_copy(self.wexp_sb[:, 512:1024], self.pB[0:64, 1, :]))[-1],
                    waits=[t_wx, self._wexg], inc=s_e7)
                self._pBg = [e_wx, e_wx]
                def mm_a(e, qa, qmt, softT_f=softT_f):
                    last = None
                    for kc in range(KD):
                        for th in range(2):
                            last = e.matmul(self.pA[0:64, th, :], qa[:, kc, :],
                                            self.x2lnT[:, kc, th * 512:th * 512 + 512],
                                            start=(kc == 0), stop=False)
                    for th in range(2):
                        last = e.matmul(self.pA[0:64, th, :], qmt, softT_f[:, th * 512:th * 512 + 512],
                                        start=False, stop=True)
                    return last
                t_a = E("tensor", lambda e: mm_a(e, self.qaTg, self.qmt_sb[:, :]),
                        waits=[lds["qag"], e_qm, self._pAg[0], self._pAg[1]] + gat + self.t_x2lnT_done, inc=s_m7)
                e_aw = E("vector", lambda e: (
                    e.tensor_mul(self.awg_sb[:, 0:512], self.pA[0:64, 0, :], self.wexp_sb[:, 0:512]),
                    e.tensor_mul(self.awg_sb[:, 512:1024], self.pA[0:64, 1, :], self.wexp_sb[:, 512:1024]))[-1],
                    waits=[t_a, e_wx, self._awgg], inc=s_e7)
                t_a2 = E("tensor", lambda e: mm_a(e, self.qaTu, self.umt_sb[:, :]),
                         waits=[lds["qau"], e_um, e_aw], inc=s_m7)
                e_aw2 = E("vector", lambda e: (
                    e.tensor_mul(self.awu_sb[:, 0:512], self.pA[0:64, 0, :], self.wexp_sb[:, 0:512]),
                    e.tensor_mul(self.awu_sb[:, 512:1024], self.pA[0:64, 1, :], self.wexp_sb[:, 512:1024]))[-1],
                    waits=[t_a2, self._awug], inc=s_e7)
                self._pAg = [e_aw2, e_aw2]
                self._awgg = self._awug = None
                a_done = [e_aw, e_aw2]
            else:
                a_done = [self._pAg[0], self._pD1g, self._dpsg[0], self._dpsg[1]]
            nI = 22 if shared else 11
            blocks = [list(range(b, min(b + 6, nI))) for b in range(0, nI, 6)]
            nblk = len(blocks)
            for half, ics in enumerate(blocks):
                act_toks = []
                for j, ic in enumerate(ics):
                    rw = ic % 2
                    if shared:
                        gsl = self.shgT_i.rearrange("(kc kp) c -> kp kc c", kp=P)[:, :, ic * P:(ic + 1) * P]
                        usl = self.shuT_i.rearrange("(kc kp) c -> kp kc c", kp=P)[:, :, ic * P:(ic + 1) * P]
                    else:
                        gsl = self.gwT_i[f].rearrange("(kc kp) c -> kp kc c", kp=P)[:, :, ic * P:(ic + 1) * P]
                        usl = self.uwT_i[f].rearrange("(kc kp) c -> kp kc c", kp=P)[:, :, ic * P:(ic + 1) * P]
                    rg = ic % 2
                    gps = self.pA if rg == 0 else self.pC
                    ups = self.pB if rg == 0 else self.pD
                    wg = 2 * (ic % 2)
                    t_lg = casted_load(self.w_ring[:, wg, :, :], gsl, [self._gwg.get(wg)])
                    def mm_g(e, gps=gps, shared=shared, wg=wg):
                        last = None
                        for kc in range(KD):
                            for th in range(2):
                                last = e.matmul(gps[:, th, :], self.w_ring[:, wg, kc, :],
                                                self.x2lnT[:, kc, th * 512:th * 512 + 512],
                                                start=(kc == 0), stop=(shared and kc == KD - 1))
                        return last
                    t_g1 = E("tensor", mm_g,
                             waits=[t_lg, self._gpsg[rg]] + a_done + self.t_x2lnT_done, inc=s_m7)
                    toks_g = [t_g1]
                    if not shared:
                        gm_t = self.pD if rg == 0 else self.pB
                        def mm_gm(e, gm_t=gm_t, wg=wg):
                            last = None
                            for kc in range(KD):
                                last = e.matmul(gm_t[0:8, 1, 0:P], self.muW_sb[:, kc, :],
                                                self.w_ring[:, wg, kc, :],
                                                start=(kc == 0), stop=(kc == KD - 1))
                            return last
                        t_gm = E("tensor", mm_gm,
                                 waits=[t_lg, lds["muw"], self._pD1g, self._upsg[1 - rg]], inc=s_m7)
                        e_gm = E("vector", lambda e, gm_t=gm_t: e.tensor_copy(self.gmt_sb[:, :], gm_t[0:8, 1, 0:P]),
                                 waits=[t_gm], inc=s_e7)
                        self._pD1g = e_gm
                        self._upsg[1 - rg] = e_gm
                        def mm_g2(e, gps=gps, ic=ic, softT_f=softT_f):
                            last = None
                            for th in range(2):
                                last = e.matmul(gps[:, th, :], self.gmt_sb[:, :],
                                                softT_f[:, th * 512:th * 512 + 512],
                                                start=False, stop=False)
                            for th in range(2):
                                last = e.matmul(gps[:, th, :], self.qbg_sb[:, ic * P:(ic + 1) * P],
                                                self.awg_sb[:, th * 512:th * 512 + 512],
                                                start=False, stop=True)
                            return last
                        t_g2 = E("tensor", mm_g2, waits=[e_gm, a_done[0]] + pres, inc=s_m7)
                        toks_g.append(t_g2)
                        self._qbgg = t_g2
                    t_sil = E("scalar", lambda e, gps=gps: e.activation(
                        self.silu_sb[:, :], gps[:, :, :].rearrange("p a b -> p (a b)"),
                        AF.Silu), waits=toks_g + [self._silg], inc=s_s7)
                    self._gpsg[rg] = t_sil
                    self._gwg[wg] = toks_g[-1]
                    wu = 1
                    t_lu = casted_load(self.w_ring[:, wu, :, :], usl, [self._uwg.get(wu)])
                    def mm_u(e, ups=ups, shared=shared, wu=wu):
                        last = None
                        for kc in range(KD):
                            for th in range(2):
                                last = e.matmul(ups[:, th, :], self.w_ring[:, wu, kc, :],
                                                self.x2lnT[:, kc, th * 512:th * 512 + 512],
                                                start=(kc == 0), stop=(shared and kc == KD - 1))
                        return last
                    t_u1 = E("tensor", mm_u,
                             waits=[t_lu, self._upsg[rg], self._dpsg[0], self._dpsg[1],
                                    self._pD1g] + self.t_x2lnT_done, inc=s_m7)
                    toks_u = [t_u1]
                    if not shared:
                        def mm_um(e, gm_t=gm_t, wu=wu):
                            last = None
                            for kc in range(KD):
                                last = e.matmul(gm_t[0:8, 1, 0:P], self.muW_sb[:, kc, :],
                                                self.w_ring[:, wu, kc, :],
                                                start=(kc == 0), stop=(kc == KD - 1))
                            return last
                        t_um2 = E("tensor", mm_um, waits=[t_lu, lds["muw"], self._pD1g], inc=s_m7)
                        e_um2 = E("vector", lambda e, gm_t=gm_t: e.tensor_copy(self.gmt2_sb[:, :], gm_t[0:8, 1, 0:P]),
                                  waits=[t_um2], inc=s_e7)
                        self._pD1g = e_um2
                        def mm_u2(e, ups=ups, ic=ic, softT_f=softT_f):
                            last = None
                            for th in range(2):
                                last = e.matmul(ups[:, th, :], self.gmt2_sb[:, :],
                                                softT_f[:, th * 512:th * 512 + 512],
                                                start=False, stop=False)
                            for th in range(2):
                                last = e.matmul(ups[:, th, :], self.qbu_sb[:, ic * P:(ic + 1) * P],
                                                self.awu_sb[:, th * 512:th * 512 + 512],
                                                start=False, stop=True)
                            return last
                        t_u2 = E("tensor", mm_u2, waits=[a_done[1], t_u1, e_um2] + pres, inc=s_m7)
                        toks_u.append(t_u2)
                        self._qbuu = t_u2
                    t_act = E("vector", lambda e, ups=ups, j=j: e.tensor_mul(
                        self.act_sb[:, j, :], self.silu_sb[:, :],
                        ups[:, :, :].rearrange("p a b -> p (a b)")),
                        waits=toks_u + [t_sil, self._actg.get(j)], inc=s_e7)
                    self._upsg[rg] = t_act
                    self._silg = t_act
                    self._uwg[wu] = toks_u[-1]
                    act_toks.append(t_act)
                    self._actg[j] = None
                nic = len(ics)
                if not shared:
                    def mm_da(e, nic=nic, ics=ics):
                        last = None
                        for j in range(nic):
                            for th in range(2):
                                last = e.matmul(self.pA[0:64, th, :], self.dqaT_sb[:, ics[j], :],
                                                self.act_sb[:, j, th * 512:th * 512 + 512],
                                                start=(j == 0), stop=(j == nic - 1))
                        return last
                    t_da = E("tensor", mm_da, waits=act_toks + [lds["dqa"], self._pAg[0], self._pAg[1]], inc=s_m7)
                    if half < nblk - 1:
                        e_da0 = E("vector", lambda e: e.tensor_copy(
                            self.da_f32[:, :], self.pA[0:64, :, :].rearrange("p a b -> p (a b)")),
                            waits=[t_da, self._da0g], inc=s_e7)
                        self._da0g = e_da0
                        self._gpsg[0] = e_da0
                        self._pAg = [e_da0, e_da0]
                        da_tok = e_da0
                    else:
                        e_daA = E("vector", lambda e: e.scalar_tensor_tensor(
                            self.da_f32[:, :], self.pA[0:64, :, :].rearrange("p a b -> p (a b)"),
                            1.0, self.da_f32[:, :], ALU.mult, ALU.add),
                            waits=[t_da, self._da0g], inc=s_e7)
                        e_daB = E("vector", lambda e: e.tensor_mul(
                            self.awd_sb[:, :], self.da_f32[:, :], self.wexp_sb[:, :]),
                            waits=[e_daA, self._awdg], inc=s_e7)
                        self._da0g = e_daB
                        self._gpsg[0] = e_daA
                        self._pAg = [e_daA, e_daA]
                        self._awdg = None
                        self._wexg = e_daB
                        da_tok = e_daB
                        self._daB = e_daB
                else:
                    da_tok = None
                t_dws = []
                for j, ic in enumerate(ics):
                    if shared:
                        dsl = self.shdT_i.rearrange("(ic ip) c -> ip ic c", ip=P)[:, ic, :]
                    else:
                        dsl = self.dwT_i[f].rearrange("(ic ip) c -> ip ic c", ip=P)[:, ic, :]
                    t_dws.append(casted_load(self.dw_sb[:, j, :], dsl,
                                             [self._dwg.get(j)] + act_toks))
                lora_now = (not shared) and (half == nblk - 1)
                first_write = (f == 0 and half == 0)
                for tt in range(8):
                    for hc in range(4):
                        rd = (tt * 4 + hc) % 2
                        def mm_d(e, tt=tt, hc=hc, rd=rd, nic=nic, lora_now=lora_now):
                            last = None
                            for j in range(nic):
                                last = e.matmul(self.pB[:, rd, :], self.act_sb[:, j, tt * P:(tt + 1) * P],
                                                self.dw_sb[:, j, hc * 512:hc * 512 + 512],
                                                start=(j == 0), stop=(not lora_now) and j == nic - 1)
                            if lora_now:
                                last = e.matmul(self.pB[:, rd, :], self.awd_sb[:, tt * P:(tt + 1) * P],
                                                self.qbd_sb[:, hc * 512:hc * 512 + 512],
                                                start=False, stop=True)
                            return last
                        wl = t_dws + act_toks + [self._dpsg[rd]]
                        if lora_now:
                            wl += [da_tok, lds["qbd"]] + pres
                        t_d = E("tensor", mm_d, waits=wl, inc=s_m7)
                        if first_write:
                            e_y = E("vector", lambda e, tt=tt, hc=hc, rd=rd, f=f: e.tensor_scalar(
                                self.y_sb[:, tt, hc * 512:hc * 512 + 512], self.pB[:, rd, :],
                                self.scal_sb[:, tt, f:f + 1], None, ALU.mult),
                                waits=[t_d, self._yg.get((tt, hc))], inc=s_e7)
                        elif not shared:
                            e_y = E("vector", lambda e, tt=tt, hc=hc, rd=rd, f=f: e.scalar_tensor_tensor(
                                self.y_sb[:, tt, hc * 512:hc * 512 + 512], self.pB[:, rd, :],
                                self.scal_sb[:, tt, f:f + 1],
                                self.y_sb[:, tt, hc * 512:hc * 512 + 512], ALU.mult, ALU.add),
                                waits=[t_d, self._yg.get((tt, hc))], inc=s_e7)
                        else:
                            e_y = E("vector", lambda e, tt=tt, hc=hc, rd=rd: e.tensor_add(
                                self.y_sb[:, tt, hc * 512:hc * 512 + 512],
                                self.y_sb[:, tt, hc * 512:hc * 512 + 512], self.pB[:, rd, :]),
                                waits=[t_d, self._yg.get((tt, hc))], inc=s_e7)
                        self._dpsg[rd] = e_y
                        self._yg[(tt, hc)] = e_y
                for j in range(len(ics)):
                    self._dwg[j] = self._dpsg[0]
            if not shared:
                self._ldg = {
                    "qag": t_a, "qau": t_a2, "muw": self._pD1g,
                    "dqa": self._daB, "qbg": self._qbgg, "qbu": self._qbuu,
                    "qbd": e_y, "gsc": pres[-1],
                }
        self.t_y_done = [self._yg[(tt, hc)] for tt in range(8) for hc in range(4)]

    def phase8_out(self):
        self.ph("p8")
        E = self.E
        self.phase_barrier()
        s_e8 = self.sem("s_e8")
        s_s8 = self.sem("s_s8")
        s_x8 = self.sem("s_x8")
        s_o8 = self.sem("s_o8")
        self.ctx_p8sb = ExitStack()
        sb8 = lambda n, s, d=F32: self.ctx_p8sb.enter_context(self.nc.sbuf_tensor(n, s, d, side="right"))
        self.x8_sb = sb8("x8_sb", [P, 2, D])
        self.o8_sb = sb8("o8_sb", [P, 2, D])
        self.st8 = sb8("st8", [P, 12])
        bb_eps, tok_eps = self.ccol(EPS_LN)
        xg = {0: None, 1: None}
        og = {0: None, 1: None}
        st_ = self.st8
        for tt in range(8):
            rr = tt % 2
            c0 = rr * 6
            yt = [self._yg[(tt, hc)] for hc in range(4)]
            t_x = E("sync", lambda e, tt=tt, rr=rr: e.dma_start(
                out=self.x8_sb[:, rr, :], in_=self.x2_hbm[tt * P:(tt + 1) * P, :]),
                waits=[self.t_x2_hbm_done, xg[rr]], inc=s_x8, inc_amt=16, dma=True)
            t_s1 = E("scalar", lambda e, tt=tt, c0=c0: e.activation(
                self.o8_sb[:, 0, :], self.y_sb[:, tt, :], AF.Identity,
                accum_out=st_[:, c0:c0 + 1]), waits=yt + [og[0]], inc=s_s8)
            t_s2 = E("scalar", lambda e, tt=tt, c0=c0: e.activation(
                self.o8_sb[:, 0, :], self.y_sb[:, tt, :], AF.Square,
                accum_out=st_[:, c0 + 1:c0 + 2]), waits=[t_s1], inc=s_s8)
            t_mu = E("vector", lambda e, c0=c0: e.tensor_scalar(
                st_[:, c0 + 2:c0 + 3], st_[:, c0:c0 + 1], 1.0 / D, None, ALU.mult),
                waits=[t_s1], inc=s_e8)
            t_msq = E("vector", lambda e, c0=c0: e.tensor_tensor(
                st_[:, c0 + 4:c0 + 5], st_[:, c0 + 2:c0 + 3], st_[:, c0 + 2:c0 + 3], ALU.mult),
                waits=[t_mu], inc=s_e8)
            t_var = E("vector", lambda e, c0=c0: e.scalar_tensor_tensor(
                st_[:, c0 + 3:c0 + 4], st_[:, c0 + 1:c0 + 2], 1.0 / D,
                st_[:, c0 + 4:c0 + 5], ALU.mult, ALU.subtract),
                waits=[t_s2, t_msq], inc=s_e8)
            t_lnv = E("scalar", lambda e, c0=c0, bb=bb_eps: e.activation(
                st_[:, c0 + 4:c0 + 5], st_[:, c0 + 3:c0 + 4], AF.Ln, bias=bb),
                waits=[t_var, tok_eps], inc=s_s8)
            t_rstd = E("scalar", lambda e, c0=c0: e.activation(
                st_[:, c0 + 3:c0 + 4], st_[:, c0 + 4:c0 + 5], AF.Exp, scale=-0.5),
                waits=[t_lnv], inc=s_s8)
            t_n1 = E("vector", lambda e, tt=tt, rr=rr, c0=c0: e.tensor_scalar(
                self.o8_sb[:, rr, :], self.y_sb[:, tt, :],
                st_[:, c0 + 2:c0 + 3], st_[:, c0 + 3:c0 + 4],
                ALU.subtract, ALU.mult), waits=[t_rstd, og[rr]], inc=s_e8)
            t_n2 = E("vector", lambda e, rr=rr: e.tensor_add(
                self.o8_sb[:, rr, :], self.o8_sb[:, rr, :], self.x8_sb[:, rr, :]),
                waits=[t_n1, t_x], inc=s_e8)
            xg[rr] = t_n2
            t_o = E("sync", lambda e, tt=tt, rr=rr: e.dma_start(
                out=self.out_hbm[tt * P:(tt + 1) * P, :], in_=self.o8_sb[:, rr, :]),
                waits=[t_n2], inc=s_o8, inc_amt=16, dma=True)
            og[rr] = t_o
        self.t_out_done = (s_o8.h, s_o8.n)

    def finalize(self, wait_all=True):
        if "mixed" in self.probes:
            self.dump("mixed", self.mixed_hbm)
        if "zout" in self.probes:
            self.dump("zout", self.z_hbm)
        sems = [s for s in self.sems.values() if s.n > 0 and s.name != "s_serial"]

        def fin(e):
            for s in sems:
                e.wait_ge(s.h, s.n)
            return e.nop()
        self.prog.emit("sync", fin)

    def compile(self):
        prog = self.prog
        with self.nc.Block() as block:
            @block.sync
            def _(e):
                for op in prog.ops["sync"]:
                    op(e)

            @block.gpsimd
            def _(e):
                for op in prog.ops["gpsimd"]:
                    op(e)

            @block.tensor
            def _(e):
                for op in prog.ops["tensor"]:
                    op(e)

            @block.vector
            def _(e):
                for op in prog.ops["vector"]:
                    op(e)

            @block.scalar
            def _(e):
                for op in prog.ops["scalar"]:
                    op(e)
        self.ctx.close()
        return self.nc


def build_all(probes=(), serial=False, serial_phases=None):
    b = Builder(probes, serial, serial_phases)
    b.declare()
    b.probe("mixed", [64, P, NTOK], BF16)
    b.probe("zout", [32, P, NTOK], BF16)
    b.probe("sheets", [8, 32, NTOK], F32)
    b.probe("hg", [NCH, 32, P, P], BF16)
    b.probe("prep", [10, P, P], BF16)
    b.probe("x2", [NTOK, D], F32)
    b.alloc_front()
    b.phase35_handshake_init()
    b.phase01_ln1()
    b.phase2_qkvz()
    b.phase3_sheets()
    b.phase4_core()
    b.phase6_wout()
    if "x2" in b.probes:
        b.E("sync", lambda e: e.dma_start(out=b.probe_aps["x2"], in_=b.x2_hbm),
            waits=[b.t_x2_hbm_done], inc=b.sem("s_prx"), inc_amt=16, dma=True)
    b.phase7_ffn()
    b.phase7_experts()
    b.phase8_out()
    b.finalize()
    nc = b.compile()
    return nc, b.probe_aps


from concourse.bass_utils import run_bass_kernel_spmd

VHP = VH


def _tri_masks():
    NEG = np.float32(-1e30)
    c = np.arange(P)
    m_attnT = np.where(c[None, :] >= c[:, None], np.float32(0), NEG).astype(np.float32)
    m_A = np.where(c[:, None] > c[None, :], np.float32(0), NEG).astype(np.float32)
    return m_attnT, m_A, np.ascontiguousarray(m_A.T)


def _prep_core_inputs(c, full, shared_cache):
    b, hh = c // 2, c % 2
    xb = np.asarray(full["L0_attn_hidden_states"][b], np.float32)
    if hh == 0:
        halo = np.zeros((P, D), np.float32)
        own = xb[0:NTOK]
    else:
        halo = xb[NTOK - P:NTOK]
        own = xb[NTOK:2 * NTOK]
    x_strip = np.ascontiguousarray(np.concatenate([halo, own], axis=0))
    in_map = dict(shared_cache)
    in_map["x_strip"] = x_strip
    in_map["par_i"] = np.array([[hh]], np.int32)
    in_map["parf_col_i"] = np.full((P, 1), float(hh), np.float32)
    in_map["ptgt_i"] = np.array([[0 if hh == 0 else SENT]], np.int32)
    return in_map


def _prep_shared(full):
    f32 = lambda k: np.asarray(full[k], np.float32)
    Wq = f32("L0_attn_W_qkvz")
    Wba = f32("L0_attn_W_ba")
    conv_w = f32("L0_attn_conv_w")
    perm = np.array(VHP, np.int64)
    wba_perm = np.concatenate([Wba[perm], Wba[32 + perm]], axis=0)
    m_attnT, m_A, m_AT = _tri_masks()
    gW, uW, dW = f32("L0_ffn_gate_W"), f32("L0_ffn_up_W"), f32("L0_ffn_down_W")
    gqa, uqa, dqa = f32("L0_ffn_gate_qa"), f32("L0_ffn_up_qa"), f32("L0_ffn_down_qa")
    gqb, uqb, dqb = f32("L0_ffn_gate_qb"), f32("L0_ffn_up_qb"), f32("L0_ffn_down_qb")
    ac = np.ascontiguousarray
    return {
        "wqkvzT": ac(Wq.T),
        "wbaT": ac(wba_perm.T),
        "conv_wi": ac(conv_w[:, 0, :]),
        "dt_col": ac(f32("L0_attn_dt_bias")[perm][:, None]),
        "alog_col": ac(f32("L0_attn_A_log")[perm][:, None]),
        "eye_i": np.eye(P, dtype=np.float32),
        "mtri_attnT": m_attnT, "mtri_A": m_A, "mtri_AT": m_AT,
        "nw_col": ac(f32("L0_attn_norm_weight")[:, None]),
        "woutT": ac(f32("L0_attn_W_out").T),
        "sent_i": np.array([[SENT]], np.int32),
        "gwgT": ac(f32("L0_ffn_gate_weight").T),
        "inv_map": ac(np.asarray(full["L0_ffn_inv_mapping"], np.int32).reshape(1, 64)),
        "muW": ac(f32("L0_ffn_mask_up_W")),
        "gwT": ac(np.transpose(gW, (0, 2, 1))),
        "uwT": ac(np.transpose(uW, (0, 2, 1))),
        "dwT": ac(np.transpose(dW, (0, 2, 1))),
        "gqaT": ac(np.transpose(gqa.reshape(8, 64, D), (0, 2, 1))),
        "uqaT": ac(np.transpose(uqa.reshape(8, 64, D), (0, 2, 1))),
        "dqaT": ac(np.transpose(dqa.reshape(8, 64, 1408), (0, 2, 1))),
        "gqbF": ac(np.transpose(gqb, (0, 1, 3, 2)).reshape(8, 64, 1408)),
        "uqbF": ac(np.transpose(uqb, (0, 1, 3, 2)).reshape(8, 64, 1408)),
        "dqbF": ac(np.transpose(dqb, (0, 1, 3, 2)).reshape(8, 64, D)),
        "gsc": ac(f32("L0_ffn_gate_scale")[:, None, :]),
        "usc": ac(f32("L0_ffn_up_scale")[:, None, :]),
        "dsc": ac(f32("L0_ffn_down_scale")[:, None, :]),
        "shgT": ac(f32("L0_ffn_sh_gate_W").T),
        "shuT": ac(f32("L0_ffn_sh_up_W").T),
        "shdT": ac(f32("L0_ffn_sh_down_W").T),
        "E8_i": np.repeat(np.eye(8, dtype=np.float32), 8, axis=1),
    }


SERIAL = False


def kernel(**inputs):
    nc, _ = build_all(probes=(), serial=SERIAL)
    shared = _prep_shared(inputs)
    in_maps = [_prep_core_inputs(c, inputs, shared) for c in range(8)]
    res = run_bass_kernel_spmd(nc, in_maps, core_ids=list(range(8)))
    out = np.empty((4, 2048, D), np.float32)
    for c in range(8):
        b, hh = c // 2, c % 2
        out[b, hh * NTOK:(hh + 1) * NTOK, :] = res.results[c]["out"]
    return out
